# version 05
# baseline (speedup 1.0000x reference)
import math
import contextlib
import numpy as np
import ml_dtypes
import concourse.bass as bass
import concourse.mybir as mybir
from concourse.bass_utils import run_bass_kernel_spmd

F32 = mybir.dt.float32
BF16 = mybir.dt.bfloat16
I32 = mybir.dt.int32
AF = mybir.ActivationFunctionType
ALU = mybir.AluOpType
AX = mybir.AxisListType.X

NCORES = 8
D = 1024
DT = 8
B = 2
L = 2048
T = B * L
NF = 2049
NFP = 2176
N2 = 4096
SN = 64
HSL = 128
E = 8
I_MOE = 4096
SH_I = 2048
NH = 16
DH = 64
FFI = 2816
EPS = 1e-5

GRID_C = [0.5, 0.5, 0.5, 0.5, 1.0, 1.0, 2.0]
MID = [0.25, 0.75, 1.25, 1.75, 2.5, 3.5, 5.0]



ENGINES = ("tensor", "vector", "scalar", "sync", "gpsimd")


class MockIns:
    def then_inc(self, sem, k):
        return self


class MockEng:

    def __init__(self):
        self.n = 0
        self.ndma = 0

    def wait_ge(self, *a, **k):
        pass

    def dma_start(self, *a, **k):
        self.ndma += 1
        return MockIns()

    def indirect_dma_start(self, *a, **k):
        self.ndma += 1
        return MockIns()

    def __getattr__(self, name):
        def f(*a, **k):
            self.n += 1
            return MockIns()
        return f


CHAIN_SEM = {"vector": "ve_p", "scalar": "sc_p", "gpsimd": "gp_p"}
CHAIN_OPS_EXCLUDE = {"wait_ge", "dma_start", "indirect_dma_start", "then_inc"}


class ChainProxy:

    def __init__(self, eng, sem, start, dma_sem=None):
        self._eng = eng
        self._sem = sem
        self._cnt = start
        self._dma = dma_sem

    def wait_ge(self, sem, v):
        return self._eng.wait_ge(sem, v)

    def dma_start(self, *a, **k):
        ins = self._eng.dma_start(*a, **k)
        if self._dma is not None:
            ins.then_inc(self._dma, 16)
        return ins

    def indirect_dma_start(self, *a, **k):
        ins = self._eng.indirect_dma_start(*a, **k)
        if self._dma is not None:
            ins.then_inc(self._dma, 16)
        return ins

    def __getattr__(self, name):
        f = getattr(self._eng, name)
        if name.startswith("_") or name in CHAIN_OPS_EXCLUDE or not callable(f):
            return f

        def wrapped(*a, **k):
            if self._sem is None:
                return f(*a, **k)
            if self._cnt > 0:
                self._eng.wait_ge(self._sem, self._cnt)
            ins = f(*a, **k)
            ins.then_inc(self._sem, 1)
            self._cnt += 1
            return ins
        return wrapped


class Plan:

    def __init__(self, nc, es):
        self.nc = nc
        self.es = es
        self.ops = {e: [] for e in ENGINES}
        self.sems = {}
        self.cnt = {}
        self.dma_sems = set()
        for s in CHAIN_SEM.values():
            self.sem(s)

    def sem(self, name):
        if name not in self.sems:
            self.sems[name] = self.es.enter_context(self.nc.semaphore(name))
            self.cnt[name] = 0
        return self.sems[name]

    def emit(self, eng, fn):
        self.ops[eng].append(fn)

    def inc(self, name, k=1):
        self.sem(name)
        self.cnt[name] += k
        return self.cnt[name]

    def val(self, name):
        self.sem(name)
        return self.cnt[name]

    def op(self, eng, fn, inc=None, wait=None, dma=None):
        sems = self.sems
        mock = MockEng()
        fn(mock)
        chain = CHAIN_SEM.get(eng)
        base = None
        if chain is not None:
            base = self.cnt[chain]
            self.cnt[chain] += mock.n
            if inc is not None and mock.n > 0:
                raise RuntimeError("no explicit inc on chained compute closures")
        if mock.ndma > 0:
            assert dma is not None and inc is None, "DMA closures need dma= sem"
            self.inc(dma, 16 * mock.ndma)
            self.dma_sems.add(dma)
        if inc is not None:
            self.inc(inc[0], inc[1])
        wl = list(wait or [])
        dsem = self.sem(dma) if dma is not None else None

        nexp = mock.n

        def run(ex, fn=fn, wl=wl, inc=inc, chain=chain, base=base, dsem=dsem, nexp=nexp):
            px = ChainProxy(ex, sems.get(chain), base or 0, dma_sem=dsem)
            for s, v in wl:
                px.wait_ge(sems[s], v)
            ins = fn(px)
            if chain is not None:
                assert px._cnt == (base or 0) + nexp, \
                    f"op count mismatch: expected {nexp}, got {px._cnt - (base or 0)} at {fn}"
            if inc is not None:
                ins.then_inc(sems[inc[0]], inc[1])
        self.emit(eng, run)

    def finalize(self, block):
        for eng in ENGINES:
            ops = self.ops[eng]
            if not ops:
                continue
            dec = getattr(block, eng)

            def section(ex, ops=ops):
                for fn in ops:
                    fn(ex)
            dec(section)


def cdiv(a, b):
    return (a + b - 1) // b



def _bf(x):
    return np.ascontiguousarray(x.astype(ml_dtypes.bfloat16))


def _f32(x):
    return np.ascontiguousarray(x, dtype=np.float32)


def host_consts():
    c = {}
    ll = np.arange(L, dtype=np.float64)
    ff = np.arange(NF, dtype=np.float64)
    th = 2.0 * np.pi * np.outer(ll, ff) / N2
    twc = np.zeros((L, NFP)); twc[:, :NF] = np.cos(th)
    tws = np.zeros((L, NFP)); tws[:, :NF] = -np.sin(th)
    c["twc_t"] = _bf(twc.reshape(L, 17, 128).transpose(1, 0, 2))
    c["tws_t"] = _bf(tws.reshape(L, 17, 128).transpose(1, 0, 2))
    a = np.full(NF, 2.0 / N2); a[0] = 1.0 / N2; a[-1] = 1.0 / N2
    thi = th.T
    twic = np.zeros((NFP, L)); twic[:NF] = np.cos(thi) * a[:, None]
    twis = np.zeros((NFP, L)); twis[:NF] = -np.sin(thi) * a[:, None]
    c["twic_t"] = _bf(twic.reshape(NFP, 16, 128).transpose(1, 0, 2))
    c["twis_t"] = _bf(twis.reshape(NFP, 16, 128).transpose(1, 0, 2))
    c["ident"] = _bf(np.eye(128))
    c["ident_f"] = _f32(np.eye(128))
    c["ones_col_f"] = _f32(np.ones((128, 1)))
    c["ones_col_b"] = _bf(np.ones((128, 1)))
    c["ones_row_f"] = _f32(np.ones((1, 128)))
    c["ones_row_b"] = _bf(np.ones((1, 128)))
    c["iota_col"] = _f32(np.arange(128, dtype=np.float32).reshape(128, 1))
    s_i = np.arange(128)[:, None]
    q_i = np.arange(512)[None, :]
    for j in range(4):
        c[f"mask{j}"] = _bf((s_i <= q_i - 128 * j).astype(np.float32))
    return c


def host_prepare(inputs):
    consts = host_consts()
    x = np.asarray(inputs["L0_attn_x"])
    shared = {}
    shared.update(consts)
    shared["ssm_D_row"] = _f32(np.asarray(inputs["L0_attn_D"]).reshape(1, D))
    shared["ssm_gamma"] = _f32(np.asarray(inputs["L0_attn_gamma"]).reshape(D, 1))
    shared["ssm_beta"] = _f32(np.asarray(inputs["L0_attn_beta"]).reshape(D, 1))
    shared["w_out"] = _f32(np.asarray(inputs["L0_attn_W_out"]))
    shared["b_out"] = _f32(np.asarray(inputs["L0_attn_b_out"]).reshape(2 * D, 1))
    shared["gate_wT"] = _f32(np.asarray(inputs["L0_ffn_gate_w"]).T)
    shared["moe_wg"] = _f32(np.asarray(inputs["L0_ffn_wg"]))
    shared["moe_wu"] = _f32(np.asarray(inputs["L0_ffn_wu"]))
    shared["moe_wd"] = _f32(np.asarray(inputs["L0_ffn_wd"]))
    shared["ws_g"] = _f32(np.asarray(inputs["L0_ffn_ws_g"]))
    shared["ws_u"] = _f32(np.asarray(inputs["L0_ffn_ws_u"]))
    shared["ws_d"] = _f32(np.asarray(inputs["L0_ffn_ws_d"]))
    shared["wqT"] = _f32(np.asarray(inputs["L1_attn_w_q"]).reshape(D, D).T)
    shared["wkT"] = _f32(np.asarray(inputs["L1_attn_w_k"]).reshape(D, D).T)
    shared["wvT"] = _f32(np.asarray(inputs["L1_attn_w_v"]).reshape(D, D).T)
    shared["w_o"] = _f32(np.asarray(inputs["L1_attn_w_o"]))
    shared["w_gate"] = _f32(np.asarray(inputs["L1_ffn_w_gate"]))
    shared["w_up"] = _f32(np.asarray(inputs["L1_ffn_w_up"]))
    shared["w_down"] = _f32(np.asarray(inputs["L1_ffn_w_down"]))

    A_re = np.asarray(inputs["L0_attn_A_re"]); A_im = np.asarray(inputs["L0_attn_A_im"])
    C_re = np.asarray(inputs["L0_attn_C_re"]); C_im = np.asarray(inputs["L0_attn_C_im"])
    log_dt = np.asarray(inputs["L0_attn_log_dt"])

    in_maps = []
    for c in range(NCORES):
        m = dict(shared)
        xc = x[2 * c:2 * c + 2].reshape(T, D).T
        m["x0T"] = _f32(xc)
        hs = slice(HSL * c, HSL * (c + 1))
        m["ssm_A_re"] = _f32(A_re[hs]); m["ssm_A_im"] = _f32(A_im[hs])
        m["ssm_C_re"] = _f32(C_re[hs]); m["ssm_C_im"] = _f32(C_im[hs])
        m["ssm_log_dt"] = _f32(log_dt[hs].reshape(HSL, 1))
        in_maps.append(m)
    return in_maps


def input_specs():
    sp = {
        "x0T": ([D, T], F32),
        "twc_t": ([17, L, 128], BF16), "tws_t": ([17, L, 128], BF16),
        "twic_t": ([16, NFP, 128], BF16), "twis_t": ([16, NFP, 128], BF16),
        "ident": ([128, 128], BF16), "ident_f": ([128, 128], F32),
        "ones_col_f": ([128, 1], F32), "ones_col_b": ([128, 1], BF16),
        "ones_row_f": ([1, 128], F32), "ones_row_b": ([1, 128], BF16),
        "iota_col": ([128, 1], F32),
        "mask0": ([128, 512], BF16), "mask1": ([128, 512], BF16),
        "mask2": ([128, 512], BF16), "mask3": ([128, 512], BF16),
        "ssm_A_re": ([HSL, SN], F32), "ssm_A_im": ([HSL, SN], F32),
        "ssm_C_re": ([HSL, SN], F32), "ssm_C_im": ([HSL, SN], F32),
        "ssm_log_dt": ([HSL, 1], F32),
        "ssm_D_row": ([1, D], F32),
        "ssm_gamma": ([D, 1], F32), "ssm_beta": ([D, 1], F32),
        "w_out": ([D, 2 * D], F32), "b_out": ([2 * D, 1], F32),
        "gate_wT": ([D, E], F32),
        "moe_wg": ([E, D, I_MOE], F32), "moe_wu": ([E, D, I_MOE], F32),
        "moe_wd": ([E, I_MOE, D], F32),
        "ws_g": ([D, SH_I], F32), "ws_u": ([D, SH_I], F32), "ws_d": ([SH_I, D], F32),
        "wqT": ([D, D], F32), "wkT": ([D, D], F32), "wvT": ([D, D], F32),
        "w_o": ([D, D], F32),
        "w_gate": ([FFI, D], F32), "w_up": ([FFI, D], F32), "w_down": ([D, FFI], F32),
    }
    return sp



class Bufs:
    def __init__(self, nc, es):
        self.nc = nc
        self.AR_A = es.enter_context(nc.sbuf_tensor("AR_A", [128, 16384], F32))
        self.AR_B = es.enter_context(nc.sbuf_tensor("AR_B", [128, 16384], F32))
        self.AR_C = es.enter_context(nc.sbuf_tensor("AR_C", [128, 8192], F32))
        self.AR_W = es.enter_context(nc.sbuf_tensor("AR_W", [128, 7168], F32))
        self.SM = es.enter_context(nc.sbuf_tensor("SM", [128, 1536], F32))
        self.PS = [es.enter_context(nc.psum_tensor(f"PS{i}", [128, 512], F32))
                   for i in range(6)]
        self.PSW = es.enter_context(nc.psum_tensor("PSW", [128, 1024], F32))

    def a_bf(self):
        return self.AR_A[:, :].bitcast(BF16)

    def b_bf(self):
        return self.AR_B[:, :].bitcast(BF16)

    def c_bf(self):
        return self.AR_C[:, :].bitcast(BF16)

    def w_bf(self):
        return self.AR_W[:, :].bitcast(BF16)


def fm_dram(ap, tc0, tn, p=128):
    d = ap.shape[0]
    return ap.rearrange("(dt p) t -> p dt t", p=p)[:, :, tc0:tc0 + tn]



def emit_ln(plan, bufs, ph, src, dst, dstT=None, gamma=None, beta=None,
            src_f32=True, wait_st=None):
    nc = plan.nc
    cst = plan.consts
    CH = 512
    nch = T // CH
    A, Bb = bufs.AR_A, bufs.AR_B
    a_bf = bufs.a_bf(); b_bf = bufs.b_bf()
    PS_SUM, PS_SQ, PS_R, PS_MR = bufs.PS[0], bufs.PS[1], bufs.PS[2], bufs.PS[3]
    SM = bufs.SM
    ld, vv, ss, mmx, stx = ph + "ld", "ve_p", "sc_p", ph + "mm", ph + "st"
    for s in (ld, mmx, stx):
        plan.sem(s)

    deps = list(wait_st or [])
    if gamma is not None:
        plan.op("sync", lambda sy, g=gamma: sy.dma_start(
            out=SM[:, 8:16], in_=g.rearrange("(dt p) o -> p (dt o)", p=128),
            allow_slow_non_contiguous=True), dma=ld, wait=deps)
        plan.op("sync", lambda sy, b=beta: sy.dma_start(
            out=SM[:, 16:24], in_=b.rearrange("(dt p) o -> p (dt o)", p=128),
            allow_slow_non_contiguous=True), dma=ld)

    prev_norm = 0
    prev_statfin = 0
    prev_store = 0
    prev_evac = 0
    C = bufs.AR_C
    mean = C[0:1, 1024:1536]
    ex2 = C[0:1, 1536:2048]
    rstd = C[0:1, 2048:2560]
    xa = A[:, 0:4096]
    xav = xa.rearrange("p (dt t) -> p dt t", dt=8)
    xnreg = b_bf[:, 8192:8192 + 4096]

    for c in range(nch):
        t0 = c * CH
        w = deps + ([] if c == 0 else [(vv, prev_norm)])
        if src_f32:
            plan.op("sync", lambda sy, t0=t0: sy.dma_start(
                out=xav, in_=fm_dram(src, t0, CH)), dma=ld, wait=w)
            ld_x = plan.val(ld)
            v_pre = plan.val(vv)
        else:
            xab = a_bf[:, 8192:8192 + 4096].rearrange("p (dt t) -> p dt t", dt=8)
            plan.op("sync", lambda sy, t0=t0, xab=xab: sy.dma_start(
                out=xab, in_=fm_dram(src, t0, CH)), dma=ld, wait=w)
            ld_x = plan.val(ld)
            plan.op("vector", lambda v, xab=xab: v.tensor_copy(xav, xab),
                    wait=[(ld, ld_x)])
            v_pre = plan.val(vv)
        plan.op("scalar", lambda s: s.activation(Bb[:, 0:4096], xa, AF.Square),
                wait=deps + [(ld, ld_x), (vv, v_pre)])
        sq_done = plan.val(ss)
        def stats(t, ld_x=ld_x, v_pre=v_pre, sq_done=sq_done, pf=prev_statfin, dp=list(deps)):
            for s, v in dp:
                t.wait_ge(plan.sems[s], v)
            t.wait_ge(plan.sems[ld], ld_x)
            t.wait_ge(plan.sems[vv], max(v_pre, pf))
            for i in range(8):
                t.matmul(PS_SUM[0:1, :], cst["ones_col_f"], A[:, 512 * i:512 * (i + 1)],
                         start=(i == 0), stop=(i == 7))
            t.wait_ge(plan.sems[ss], sq_done)
            for i in range(8):
                ins = t.matmul(PS_SQ[0:1, :], cst["ones_col_f"], Bb[:, 512 * i:512 * (i + 1)],
                               start=(i == 0), stop=(i == 7))
            return ins
        plan.op("tensor", stats, inc=(mmx, 1))
        mm_stats = plan.val(mmx)
        def statfin_v(v, mm_stats=mm_stats):
            v.wait_ge(plan.sems[mmx], mm_stats)
            v.tensor_scalar(mean, PS_SUM[0:1, :], 1.0 / D, None, ALU.mult)
            v.tensor_scalar(ex2, PS_SQ[0:1, :], 1.0 / D, None, ALU.mult)
            v.tensor_mul(rstd, mean, mean)
            return v.tensor_sub(ex2, ex2, rstd)
        plan.op("vector", statfin_v)
        def statfin_s(s, vw=plan.val(vv)):
            s.wait_ge(plan.sems[vv], vw)
            s.activation(rstd, ex2, AF.Ln, bias=EPS)
            return s.activation(rstd, rstd, AF.Exp, scale=-0.5)
        plan.op("scalar", statfin_s)
        def statfin2_v(v, sw=plan.val(ss)):
            v.wait_ge(plan.sems[ss], sw)
            v.tensor_mul(mean, mean, rstd)
            return v.tensor_scalar(mean, mean, -1.0, None, ALU.mult)
        plan.op("vector", statfin2_v)
        prev_statfin = plan.val(vv)
        def bcast(t, vw=prev_statfin, pn=prev_norm):
            t.wait_ge(plan.sems[vv], max(vw, pn))
            t.matmul(PS_R[:, :], cst["ones_row_f"], rstd, start=True, stop=True)
            return t.matmul(PS_MR[:, :], cst["ones_row_f"], mean, start=True, stop=True)
        plan.op("tensor", bcast, inc=(mmx, 1))
        mm_b = plan.val(mmx)
        def norm(v, mm_b=mm_b, ps=prev_store):
            v.wait_ge(plan.sems[mmx], mm_b)
            if ps:
                v.wait_ge(plan.sems[stx], ps)
            for i in range(8):
                xi = A[:, 512 * i:512 * (i + 1)]
                ti = Bb[:, 512 * i:512 * (i + 1)]
                xo = xnreg[:, 512 * i:512 * (i + 1)]
                v.tensor_mul(ti, xi, PS_R[:, :])
                if gamma is None:
                    ins = v.tensor_add(xo, ti, PS_MR[:, :])
                else:
                    ti2 = bufs.AR_C[:, 0:512]
                    v.tensor_add(ti2, ti, PS_MR[:, :])
                    v.tensor_scalar(ti, ti2, SM[:, 8 + i:9 + i], None, ALU.mult)
                    ins = v.tensor_scalar(xo, ti, SM[:, 16 + i:17 + i], None, ALU.add)
            return ins
        plan.op("vector", norm)
        prev_norm = plan.val(vv)
        plan.op("sync", lambda sy, t0=t0: sy.dma_start(
            out=fm_dram(dst, t0, CH),
            in_=xnreg.rearrange("p (dt t) -> p dt t", dt=8)),
            wait=[(vv, prev_norm)], dma=stx)
        if dstT is not None:
            psb = bufs.PSW[:, :].bitcast(BF16)
            for j in range(4):
                def trans(t, j=j, nw=prev_norm, pe=prev_evac):
                    t.wait_ge(plan.sems[vv], max(nw, pe))
                    for i in range(8):
                        ins = t.transpose(psb[:, 128 * i:128 * (i + 1)],
                                          xnreg[:, 512 * i + 128 * j:512 * i + 128 * (j + 1)],
                                          cst["ident"])
                    return ins
                plan.op("tensor", trans, inc=(mmx, 1))
                treg = b_bf[:, 12288 + 1024 * j:12288 + 1024 * (j + 1)]
                def evac(v, mw=plan.val(mmx), j=j, pst=prev_store, treg=treg):
                    v.wait_ge(plan.sems[mmx], mw)
                    if pst:
                        v.wait_ge(plan.sems[stx], pst)
                    return v.tensor_copy(treg, psb[:, 0:1024])
                plan.op("vector", evac)
                prev_evac = plan.val(vv)
                plan.op("sync", lambda sy, t0=t0, j=j, treg=treg: sy.dma_start(
                    out=dstT[t0 + 128 * j:t0 + 128 * (j + 1), :], in_=treg),
                    wait=[(vv, prev_evac)], dma=stx)
        prev_store = plan.val(stx)
    return stx, plan.val(stx)



CONST_NAMES = ["ident", "ident_f", "ones_col_f", "ones_col_b", "ones_row_f", "ones_row_b",
               "iota_col", "mask0", "mask1", "mask2", "mask3"]

FLOAT_CONSTS = [EPS]


def load_consts(plan, es, dram):
    nc = plan.nc
    shapes = input_specs()
    plan.consts = {}
    for n in CONST_NAMES:
        sh, dt = shapes[n]
        t = es.enter_context(nc.sbuf_tensor("cst_" + n, sh, dt))
        plan.consts[n] = t[:, :]
        plan.op("gpsimd", lambda g, n=n, t=t, sh=sh: g.dma_start(
            out=t[0:sh[0], 0:sh[1]], in_=dram[n]), dma="cstld")
    for i, v in enumerate(FLOAT_CONSTS):
        t = es.enter_context(nc.sbuf_tensor(f"fc_{i}", [128, 1], F32))
        plan.op("gpsimd", lambda g, t=t, v=v: g.memset(t[:, :], v))
        nc.const_aps.register(F32, v, t[:, :])
    return [("cstld", plan.val("cstld")), ("gp_p", plan.val("gp_p"))]



def build(phase_fns, io_kinds, extra_scratch=None):
    nc = bass.Bass("TRN2", target_bir_lowering=False)
    specs = input_specs()
    io_kinds = dict(io_kinds)
    for n in CONST_NAMES:
        io_kinds.setdefault(n, "ExternalInput")
    es = contextlib.ExitStack()
    with es:
        dram = {}
        for n, kind in io_kinds.items():
            sh, dt = specs[n]
            dram[n] = nc.dram_tensor(n, sh, dt, kind=kind).ap()
        for n, (sh, dt, kind) in (extra_scratch or {}).items():
            dram[n] = nc.dram_tensor(n, sh, dt, kind=kind).ap()
        bufs = Bufs(nc, es)
        plan = Plan(nc, es)
        cw = load_consts(plan, es, dram)
        plan.cst_wait = cw
        for fn in phase_fns:
            fn(plan, bufs, dram)
        final = [(s, plan.cnt[s]) for s in sorted(plan.dma_sems) if plan.cnt[s] > 0]
        def fin(sy):
            for s, v in final:
                sy.wait_ge(plan.sems[s], v)
        plan.emit("sync", fin)
        block = es.enter_context(nc.Block())
        plan.finalize(block)
    return nc



TWO_PI = float(2.0 * np.pi)
INV_2PI = float(1.0 / (2.0 * np.pi))
HALF_PI = float(np.pi / 2.0)



def emit_rr(v, theta, out, iv, s1, s2):
    v.tensor_scalar(out, theta, INV_2PI, None, ALU.mult)
    v.tensor_copy(iv, out)
    v.tensor_copy(s1, iv)
    v.tensor_sub(s2, out, s1)
    v.tensor_scalar(out, s2, 0.5, None, ALU.is_gt)
    v.tensor_scalar(s2, s2, -0.5, None, ALU.is_lt)
    v.tensor_sub(out, out, s2)
    v.tensor_add(s1, s1, out)
    v.scalar_tensor_tensor(out, s1, -TWO_PI, theta, ALU.mult, ALU.add)
    return v.tensor_scalar(out, out, float(np.pi), -float(np.pi), ALU.min, ALU.max)


def emit_ssm_k(plan, bufs, ph, dram):
    cst = plan.consts
    SM, A = bufs.SM, bufs.AR_A
    b_bf = bufs.b_bf()
    a_bf = bufs.a_bf()
    ld, vv, ss, mmx, stx = ph + "ld", "ve_p", "sc_p", ph + "mm", ph + "st"
    for s in (ld, mmx, stx):
        plan.sem(s)
    cols = {"A_re": 0, "A_im": 64, "C_re": 128, "C_im": 192}
    for nm, c0 in cols.items():
        plan.op("sync", lambda sy, nm=nm, c0=c0: sy.dma_start(
            out=SM[:, c0:c0 + 64], in_=dram["ssm_" + nm]), dma=ld,
            wait=plan.cst_wait)
    plan.op("sync", lambda sy: sy.dma_start(
        out=SM[:, 256:257], in_=dram["ssm_log_dt"]), dma=ld)
    ldv = plan.val(ld)
    Are, Aim = SM[:, 0:64], SM[:, 64:128]
    Cre, Cim = SM[:, 128:192], SM[:, 192:256]
    logdt, dt = SM[:, 256:257], SM[:, 257:258]
    rho, phi = SM[:, 320:384], SM[:, 384:448]
    ctre, ctim = SM[:, 448:512], SM[:, 512:576]
    t1, t2, t3, t4 = SM[:, 576:640], SM[:, 640:704], SM[:, 704:768], SM[:, 768:832]
    er, s_, c_ = SM[:, 832:896], SM[:, 896:960], SM[:, 960:1024]
    iv64 = SM[:, 1024:1088].bitcast(I32)

    plan.op("scalar", lambda s: s.activation(dt, logdt, AF.Exp),
            wait=[(ld, ldv)])
    def vstep1(v, sw=plan.val(ss)):
        v.wait_ge(plan.sems[ss], sw)
        v.tensor_scalar(rho, Are, dt, None, ALU.mult)
        return v.tensor_scalar(phi, Aim, dt, None, ALU.mult)
    plan.op("vector", vstep1)
    def vstep2(v, w=plan.val(vv)):
        v.wait_ge(plan.sems[vv], w)
        return emit_rr(v, phi, t2, iv64, t3, t4)
    plan.op("vector", vstep2)
    plan.op("scalar", lambda s: s.activation(s_, t2, AF.Sin),
            wait=[(vv, plan.val(vv))])
    def vstep3(v, w=plan.val(ss)):
        v.wait_ge(plan.sems[ss], w)
        v.tensor_scalar(t1, phi, HALF_PI, None, ALU.add)
        return emit_rr(v, t1, t3, iv64, t2, t4)
    plan.op("vector", vstep3)
    def sstep2(s, w=plan.val(vv)):
        s.wait_ge(plan.sems[vv], w)
        s.activation(c_, t3, AF.Sin)
        return s.activation(er, rho, AF.Exp)
    plan.op("scalar", sstep2)
    zre, zim = SM[:, 1088:1152], SM[:, 1152:1216]
    def vstep4(v, w=plan.val(ss)):
        v.wait_ge(plan.sems[ss], w)
        v.tensor_mul(zre, er, c_)
        v.tensor_mul(zim, er, s_)
        v.tensor_scalar(t1, zre, -1.0, None, ALU.add)
        v.tensor_copy(t2, zim)
        v.tensor_mul(t3, Are, Are)
        v.tensor_mul(t4, Aim, Aim)
        v.tensor_add(t3, t3, t4)
        v.reciprocal(t3, t3)
        v.tensor_mul(t4, t1, Are)
        v.tensor_mul(er, t2, Aim)
        v.tensor_add(t4, t4, er)
        v.tensor_mul(t4, t4, t3)
        v.tensor_mul(er, t2, Are)
        v.tensor_mul(s_, t1, Aim)
        v.tensor_sub(er, er, s_)
        v.tensor_mul(er, er, t3)
        v.tensor_mul(ctre, Cre, t4)
        v.tensor_mul(s_, Cim, er)
        v.tensor_sub(ctre, ctre, s_)
        v.tensor_scalar(ctre, ctre, 2.0, None, ALU.mult)
        v.tensor_mul(ctim, Cre, er)
        v.tensor_mul(s_, Cim, t4)
        v.tensor_add(ctim, ctim, s_)
        return v.tensor_scalar(ctim, ctim, 2.0, None, ALU.mult)
    plan.op("vector", vstep4)

    NCH = 32
    LC = 64
    Pre = A[:, 0:4096]
    Pim = A[:, 4096:8192]
    Qb = A[:, 8192:12288]
    Tb = A[:, 12288:16384]
    kT = b_bf[:, 18432:20480]
    zc = [(SM[:, 1216:1280], SM[:, 1280:1344]),
          (SM[:, 1344:1408], SM[:, 1408:1472])]
    z64re, z64im = zc[0]

    def v3(apf, l0, ln):
        return apf.rearrange("p (n l) -> p n l", n=SN)[:, :, l0:l0 + ln]

    def bc3(z, ln):
        return z.rearrange("p (n o) -> p n o", o=1).broadcast_to([128, SN, ln])

    def vinit(v):
        v.memset(v3(Pre, 0, 1), 1.0)
        v.memset(v3(Pim, 0, 1), 0.0)
        v.tensor_copy(zc[0][0], zre)
        v.tensor_copy(zc[0][1], zim)
        for k in range(6):
            m = 1 << k
            zkre, zkim = zc[k % 2]
            znre, znim = zc[(k + 1) % 2]
            v.tensor_tensor(v3(Tb, 0, m), v3(Pre, 0, m), bc3(zkre, m), ALU.mult)
            v.tensor_tensor(v3(Qb, 0, m), v3(Pim, 0, m), bc3(zkim, m), ALU.mult)
            v.tensor_sub(v3(Pre, m, m), v3(Tb, 0, m), v3(Qb, 0, m))
            v.tensor_tensor(v3(Tb, 0, m), v3(Pim, 0, m), bc3(zkre, m), ALU.mult)
            v.tensor_tensor(v3(Qb, 0, m), v3(Pre, 0, m), bc3(zkim, m), ALU.mult)
            v.tensor_add(v3(Pim, m, m), v3(Tb, 0, m), v3(Qb, 0, m))
            v.tensor_mul(t1, zkre, zkre)
            v.tensor_mul(t2, zkim, zkim)
            v.tensor_sub(znre, t1, t2)
            v.tensor_mul(t1, zkre, zkim)
            ins = v.tensor_scalar(znim, t1, 2.0, None, ALU.mult)
        return ins
    plan.op("vector", vinit)

    for c in range(NCH):
        def kchunk(v, c=c):
            v.tensor_tensor(Qb.rearrange("p (n l) -> p n l", n=SN),
                            Pre.rearrange("p (n l) -> p n l", n=SN),
                            bc3(ctre, LC), ALU.mult)
            v.tensor_tensor(Tb.rearrange("p (n l) -> p n l", n=SN),
                            Pim.rearrange("p (n l) -> p n l", n=SN),
                            bc3(ctim, LC), ALU.mult)
            v.tensor_sub(Qb, Qb, Tb)
            Qv = Qb.rearrange("p (n l) -> p n l", n=SN)
            h = SN
            while h > 1:
                h //= 2
                v.tensor_add(Qv[:, 0:h, :], Qv[:, 0:h, :], Qv[:, h:2 * h, :])
            ins = v.tensor_copy(kT[:, LC * c:LC * (c + 1)], Qb[:, 0:LC])
            if c == NCH - 1:
                return ins
            v.tensor_tensor(Tb.rearrange("p (n l) -> p n l", n=SN),
                            Pre.rearrange("p (n l) -> p n l", n=SN),
                            bc3(z64re, LC), ALU.mult)
            v.tensor_tensor(Qb.rearrange("p (n l) -> p n l", n=SN),
                            Pim.rearrange("p (n l) -> p n l", n=SN),
                            bc3(z64im, LC), ALU.mult)
            v.tensor_sub(Tb, Tb, Qb)
            v.tensor_tensor(Qb.rearrange("p (n l) -> p n l", n=SN),
                            Pre.rearrange("p (n l) -> p n l", n=SN),
                            bc3(z64im, LC), ALU.mult)
            v.tensor_tensor(Pim.rearrange("p (n l) -> p n l", n=SN),
                            Pim.rearrange("p (n l) -> p n l", n=SN),
                            bc3(z64re, LC), ALU.mult)
            v.tensor_add(Pim, Pim, Qb)
            return v.tensor_copy(Pre, Tb)
        plan.op("vector", kchunk)
    kT_done = plan.val(vv)

    ktile = b_bf[:, 16384:18432]
    psb = bufs.PSW[:, :].bitcast(BF16)
    for g in range(2):
        def ktr(t, g=g, w=kT_done, pv=plan.val(vv)):
            t.wait_ge(plan.sems[vv], max(w, pv))
            for j in range(8):
                ins = t.transpose(psb[:, 128 * j:128 * (j + 1)],
                                  kT[:, 1024 * g + 128 * j:1024 * g + 128 * (j + 1)],
                                  cst["ident"])
            return ins
        plan.op("tensor", ktr, inc=(mmx, 1))
        def kev(v, g=g, mw=plan.val(mmx)):
            v.wait_ge(plan.sems[mmx], mw)
            return v.tensor_copy(ktile[:, 1024 * g:1024 * (g + 1)], psb[:, 0:1024])
        plan.op("vector", kev)
    prev_red = plan.val(vv)
    for lt in range(16):
        plan.op("sync", lambda sy, lt=lt: sy.dma_start(
            out=dram["s_kslice"][128 * lt:128 * (lt + 1), :],
            in_=ktile[:, 128 * lt:128 * (lt + 1)]),
            wait=[(vv, prev_red)], dma=stx)
    return stx, plan.val(stx)


def emit_ssm_k_old(plan, bufs, ph, dram):
    nc = plan.nc
    cst = plan.consts
    SM, A, Bb, C = bufs.SM, bufs.AR_A, bufs.AR_B, bufs.AR_C
    b_bf = bufs.b_bf()
    ld, vv, ss, mmx, stx = ph + "ld", "ve_p", "sc_p", ph + "mm", ph + "st"
    for s in (ld, mmx, stx):
        plan.sem(s)
    cols = {"A_re": 0, "A_im": 64, "C_re": 128, "C_im": 192}
    for nm, c0 in cols.items():
        plan.op("sync", lambda sy, nm=nm, c0=c0: sy.dma_start(
            out=SM[:, c0:c0 + 64], in_=dram["ssm_" + nm]), dma=ld,
            wait=plan.cst_wait)
    plan.op("sync", lambda sy: sy.dma_start(
        out=SM[:, 256:257], in_=dram["ssm_log_dt"]), dma=ld)
    ldv = plan.val(ld)
    Are, Aim = SM[:, 0:64], SM[:, 64:128]
    Cre, Cim = SM[:, 128:192], SM[:, 192:256]
    logdt, dt = SM[:, 256:257], SM[:, 257:258]
    rho, phi = SM[:, 320:384], SM[:, 384:448]
    ctre, ctim = SM[:, 448:512], SM[:, 512:576]
    t1, t2, t3, t4 = SM[:, 576:640], SM[:, 640:704], SM[:, 704:768], SM[:, 768:832]
    er, s_, c_ = SM[:, 832:896], SM[:, 896:960], SM[:, 960:1024]
    iv64 = SM[:, 1024:1088].bitcast(I32)

    plan.op("scalar", lambda s: s.activation(dt, logdt, AF.Exp),
            wait=[(ld, ldv)])
    def vstep1(v, sw=plan.val(ss)):
        v.wait_ge(plan.sems[ss], sw)
        v.tensor_scalar(rho, Are, dt, None, ALU.mult)
        return v.tensor_scalar(phi, Aim, dt, None, ALU.mult)
    plan.op("vector", vstep1)
    def vstep2(v, w=plan.val(vv)):
        v.wait_ge(plan.sems[vv], w)
        return emit_rr(v, phi, t2, iv64, t3, t4)
    plan.op("vector", vstep2)
    plan.op("scalar", lambda s: s.activation(s_, t2, AF.Sin),
            wait=[(vv, plan.val(vv))])
    def vstep3(v, w=plan.val(ss)):
        v.wait_ge(plan.sems[ss], w)
        v.tensor_scalar(t1, phi, HALF_PI, None, ALU.add)
        return emit_rr(v, t1, t3, iv64, t2, t4)
    plan.op("vector", vstep3)
    def sstep2(s, w=plan.val(vv)):
        s.wait_ge(plan.sems[vv], w)
        s.activation(c_, t3, AF.Sin)
        return s.activation(er, rho, AF.Exp)
    plan.op("scalar", sstep2)
    def vstep4(v, w=plan.val(ss)):
        v.wait_ge(plan.sems[ss], w)
        v.tensor_mul(t1, er, c_)
        v.tensor_mul(t2, er, s_)
        v.tensor_scalar(t1, t1, -1.0, None, ALU.add)
        v.tensor_mul(t3, Are, Are)
        v.tensor_mul(t4, Aim, Aim)
        v.tensor_add(t3, t3, t4)
        v.reciprocal(t3, t3)
        v.tensor_mul(t4, t1, Are)
        v.tensor_mul(er, t2, Aim)
        v.tensor_add(t4, t4, er)
        v.tensor_mul(t4, t4, t3)
        v.tensor_mul(er, t2, Are)
        v.tensor_mul(s_, t1, Aim)
        v.tensor_sub(er, er, s_)
        v.tensor_mul(er, er, t3)
        v.tensor_mul(ctre, Cre, t4)
        v.tensor_mul(s_, Cim, er)
        v.tensor_sub(ctre, ctre, s_)
        v.tensor_scalar(ctre, ctre, 2.0, None, ALU.mult)
        v.tensor_mul(ctim, Cre, er)
        v.tensor_mul(s_, Cim, t4)
        v.tensor_add(ctim, ctim, s_)
        return v.tensor_scalar(ctim, ctim, 2.0, None, ALU.mult)
    plan.op("vector", vstep4)
    prm_done = plan.val(vv)
    for k, src in enumerate([phi, rho, ctre, ctim]):
        plan.op("sync", lambda sy, k=k, src=src: sy.dma_start(
            out=dram["s_prm"][k:k + 1, :].rearrange("o (p n) -> p (o n)", p=128),
            in_=src), wait=[(vv, prm_done)], dma=stx)
    prm_st = plan.val(stx)

    ktile = [b_bf[:, 16384 + 128 * lt:16384 + 128 * (lt + 1)] for lt in range(16)]
    W1, W2, W3 = A[:, 0:2048], A[:, 2048:4096], A[:, 4096:6144]
    W4, W5, W6 = A[:, 6144:8192], A[:, 8192:10240], A[:, 10240:12288]
    IV = A[:, 12288:14336].bitcast(I32)
    liota = SM[:, 1088:1089]
    prev_red = 0
    prev_bc_mm = 0
    for hc in range(4):
        plan.op("sync", lambda sy, hc=hc: sy.dma_start(
            out=C[0:1, 0:8192].rearrange("p (k n) -> p k n", k=4),
            in_=dram["s_prm"].rearrange("k (c n) -> c k n", n=2048)[hc:hc + 1]),
            wait=[(stx, prm_st), (mmx, prev_bc_mm)], dma=ld)
        ldh = plan.val(ld)
        bcs = [Bb[:, 0:2048], Bb[:, 2048:4096], Bb[:, 4096:6144], Bb[:, 6144:8192]]
        for k in range(4):
            def bc1(t, k=k, ldh=ldh, pr=prev_red, pe=plan.val(vv)):
                t.wait_ge(plan.sems[ld], ldh)
                if pr:
                    t.wait_ge(plan.sems[vv], pr)
                t.wait_ge(plan.sems[vv], pe)
                for ch in range(4):
                    ins = t.matmul(bufs.PS[ch][:, :], cst["ones_row_f"],
                                   C[0:1, 2048 * k + 512 * ch:2048 * k + 512 * (ch + 1)],
                                   start=True, stop=True)
                return ins
            plan.op("tensor", bc1, inc=(mmx, 1))
            def ev1(v, k=k, mw=plan.val(mmx)):
                v.wait_ge(plan.sems[mmx], mw)
                for ch in range(4):
                    ins = v.tensor_copy(bcs[k][:, 512 * ch:512 * (ch + 1)], bufs.PS[ch][:, :])
                return ins
            plan.op("vector", ev1)
        bc_done = plan.val(vv)
        prev_bc_mm = plan.val(mmx)
        phib, rhob, ctreb, ctimb = bcs

        for lt in range(16):
            l0 = lt * 128
            def kcomp_v1(v, l0=l0, bc_done=bc_done):
                v.wait_ge(plan.sems[vv], bc_done)
                v.tensor_scalar(liota, cst["iota_col"], 1.0, float(l0), ALU.mult, ALU.add)
                v.tensor_scalar(W1, phib, liota, None, ALU.mult)
                return emit_rr(v, W1, W2, IV, W3, W6)
            plan.op("vector", kcomp_v1)
            def kcomp_s1(s, w=plan.val(vv)):
                s.wait_ge(plan.sems[vv], w)
                return s.activation(W3, W2, AF.Sin)
            plan.op("scalar", kcomp_s1)
            def kcomp_v2(v, w=plan.val(ss)):
                v.wait_ge(plan.sems[ss], w)
                v.tensor_scalar(W2, W1, HALF_PI, None, ALU.add)
                emit_rr(v, W2, W4, IV, W5, W6)
                return v.tensor_scalar(W5, rhob, liota, None, ALU.mult)
            plan.op("vector", kcomp_v2)
            def kcomp_s2(s, w=plan.val(vv)):
                s.wait_ge(plan.sems[vv], w)
                s.activation(W4, W4, AF.Sin)
                return s.activation(W5, W5, AF.Exp)
            plan.op("scalar", kcomp_s2)
            def kcomp_v3(v, w=plan.val(ss), lt=lt, hc=hc):
                v.wait_ge(plan.sems[ss], w)
                v.tensor_mul(W6, ctreb, W4)
                v.tensor_mul(W1, ctimb, W3)
                v.tensor_sub(W6, W6, W1)
                v.tensor_mul(W6, W6, W5)
                v.reduce_sum(W2[:, 0:32], W6.rearrange("p (h n) -> p h n", n=64), axis=AX)
                return v.tensor_copy(ktile[lt][:, 32 * hc:32 * (hc + 1)], W2[:, 0:32])
            plan.op("vector", kcomp_v3)
        prev_red = plan.val(vv)
    for lt in range(16):
        plan.op("sync", lambda sy, lt=lt: sy.dma_start(
            out=dram["s_kslice"][128 * lt:128 * (lt + 1), :], in_=ktile[lt]),
            wait=[(vv, prev_red)], dma=stx)
    return stx, plan.val(stx)



def emit_dft_fwd(plan, bufs, ph, dram, with_k=True, deps=()):
    cst = plan.consts
    a_bf, b_bf, c_bf, w_bf = bufs.a_bf(), bufs.b_bf(), bufs.c_bf(), bufs.w_bf()
    ld, mmx, stx = ph + "ld", ph + "mm", ph + "st"
    for s in (ld, mmx, stx):
        plan.sem(s)
    vv = "ve_p"
    deps = list(deps) + list(plan.cst_wait)
    nsrc = 3 if with_k else 2
    xt = [[a_bf[:, 16384 * b + 1024 * lt:16384 * b + 1024 * (lt + 1)] for lt in range(16)]
          for b in range(2)]
    kt = [c_bf[:, 1024 * lt:1024 * (lt + 1)] for lt in range(16)]
    ldin = ph + "in"
    for b in range(2):
        plan.op("sync", lambda sy, b=b: sy.dma_start(
            out=a_bf[:, 16384 * b:16384 * (b + 1)].rearrange("p (lt n) -> p lt n", lt=16),
            in_=dram["s_xnT"].rearrange("(b lt p) n -> b lt p n", b=2, p=128)[b]),
            dma=ldin, wait=deps)
    if with_k:
        plan.op("sync", lambda sy: sy.dma_start(
            out=c_bf[:, 0:16384].rearrange("p (lt r h) -> p lt r h", lt=16, r=8),
            in_=dram["s_kag"].rearrange("(r lt p) h -> lt p r h", r=8, p=128)),
            dma=ldin, wait=deps)
    ld_in = plan.val(ldin)
    twbuf = [[w_bf[:, 4096 * bb + 2048 * cs:4096 * bb + 2048 * (cs + 1)]
              for cs in range(2)] for bb in range(2)]
    stage = [b_bf[:, 1024 * i:1024 * (i + 1)] for i in range(4)]
    prev_grp_ev = [0, 0]
    prev_tw_mm = [0, 0]
    prev_store_all = 0
    first_mm = True
    gi = 0
    for ft in range(17):
        bb = ft % 2
        def twload(sy, ft=ft, bb=bb):
            sy.dma_start(
                out=twbuf[bb][0].rearrange("p (lt n) -> p lt n", lt=16),
                in_=dram["twc_t"][ft].rearrange("(lt p) n -> lt p n", p=128))
            return sy.dma_start(
                out=twbuf[bb][1].rearrange("p (lt n) -> p lt n", lt=16),
                in_=dram["tws_t"][ft].rearrange("(lt p) n -> lt p n", p=128))
        plan.op("sync", twload, dma=ld + str(bb), wait=[(mmx, prev_tw_mm[bb])] + deps)
        tw_ld = plan.val(ld + str(bb))
        for si in range(nsrc):
            src = xt[si] if si < 2 else kt
            g = gi % 2
            gi += 1
            if g == 0:
                ps_re = [bufs.PSW[:, 0:512], bufs.PSW[:, 512:1024]]
                ps_im = [bufs.PS[0][:, :], bufs.PS[1][:, :]]
            else:
                ps_re = [bufs.PS[2][:, :], bufs.PS[3][:, :]]
                ps_im = [bufs.PS[4][:, :], bufs.PS[5][:, :]]
            wl = [(ld + str(bb), tw_ld), (ldin, ld_in), (vv, prev_grp_ev[g])]
            if first_mm:
                wl += deps
                first_mm = False
            def mmg(t, src=src, bb=bb, ps_re=ps_re, ps_im=ps_im):
                for cs in range(2):
                    ps = ps_re if cs == 0 else ps_im
                    for nch in range(2):
                        for lt in range(16):
                            ins = t.matmul(ps[nch],
                                           twbuf[bb][cs][:, 128 * lt:128 * (lt + 1)],
                                           src[lt][:, 512 * nch:512 * (nch + 1)],
                                           start=(lt == 0), stop=(lt == 15))
                return ins
            plan.op("tensor", mmg, inc=(mmx, 1), wait=wl)
            if si == nsrc - 1:
                prev_tw_mm[bb] = plan.val(mmx)
            def ev(v, g=g, ps_re=ps_re, ps_im=ps_im):
                v.tensor_copy(stage[2 * g][:, 0:512], ps_re[0])
                v.tensor_copy(stage[2 * g][:, 512:1024], ps_re[1])
                v.tensor_copy(stage[2 * g + 1][:, 0:512], ps_im[0])
                return v.tensor_copy(stage[2 * g + 1][:, 512:1024], ps_im[1])
            plan.op("vector", ev, wait=[(mmx, plan.val(mmx)), (stx, prev_store_all)])
            prev_grp_ev[g] = plan.val(vv)
            if si < 2:
                dsl = dram["s_uf"][si]
            else:
                dsl = dram["s_kf"]
            def store(sy, ft=ft, g=g, dsl=dsl):
                sy.dma_start(out=dsl[0, 128 * ft:128 * (ft + 1), :], in_=stage[2 * g])
                return sy.dma_start(out=dsl[1, 128 * ft:128 * (ft + 1), :], in_=stage[2 * g + 1])
            plan.op("sync", store, dma=stx, wait=[(vv, prev_grp_ev[g])])
            if gi % 2 == 0:
                prev_store_all = plan.val(stx)
    return stx, plan.val(stx)



def emit_conv(plan, bufs, ph, dram, deps=()):
    cst = plan.consts
    a_bf, b_bf, c_bf, w_bf = bufs.a_bf(), bufs.b_bf(), bufs.c_bf(), bufs.w_bf()
    Bb = bufs.AR_B
    ld, mmx, stx = ph + "ld", ph + "mm", ph + "st"
    for s in (ld, mmx, stx):
        plan.sem(s)
    vv, ss = "ve_p", "sc_p"
    deps = list(deps) + list(plan.cst_wait)
    ustg = b_bf[:, 4096:8192]
    s1f, s2f = Bb[:, 4096:5120], Bb[:, 5120:6144]
    dbc = Bb[:, 6144:7168]
    y2r, t1r = Bb[:, 8192:9216], Bb[:, 9216:10240]
    gout = b_bf[:, 20480:21504]

    def vf_t(ft, cs):
        if ft < 16:
            return a_bf[:, 2048 * ft + 1024 * cs:2048 * ft + 1024 * (cs + 1)]
        return b_bf[:, 1024 * cs:1024 * (cs + 1)]

    plan.op("sync", lambda sy: sy.dma_start(out=Bb[0:1, 7168:8192], in_=dram["ssm_D_row"]),
            dma=ph + "d", wait=deps)
    def dbc_mm(t):
        t.matmul(bufs.PS[0][:, :], cst["ones_row_f"], Bb[0:1, 7168:7680], start=True, stop=True)
        return t.matmul(bufs.PS[1][:, :], cst["ones_row_f"], Bb[0:1, 7680:8192], start=True, stop=True)
    plan.op("tensor", dbc_mm, inc=(mmx, 1), wait=[(ph + "d", plan.val(ph + "d"))] + deps)
    def dbc_ev(v):
        v.tensor_copy(dbc[:, 0:512], bufs.PS[0][:, :])
        return v.tensor_copy(dbc[:, 512:1024], bufs.PS[1][:, :])
    plan.op("vector", dbc_ev, wait=[(mmx, plan.val(mmx))])

    prev_itw_mm = [0, 0]
    prev_imm_all = 0
    prev_epi = 0
    prev_gstore = 0
    for b in range(2):
        for ft in range(17):
            def ldt(sy, b=b, ft=ft):
                sy.dma_start(out=ustg[:, 0:2048].rearrange("p (c n) -> p c n", c=2),
                             in_=dram["s_uf"][b].rearrange("c (ft p) n -> ft p c n", p=128)[ft])
                return sy.dma_start(out=ustg[:, 2048:4096].rearrange("p (c n) -> p c n", c=2),
                                    in_=dram["s_kf"].rearrange("c (ft p) n -> ft p c n", p=128)[ft])
            plan.op("sync", ldt, dma=ld,
                    wait=[(vv, plan.val(vv))] + deps)
            ur, ui = ustg[:, 0:1024], ustg[:, 1024:2048]
            kr, ki = ustg[:, 2048:3072], ustg[:, 3072:4096]
            def vfc(v, w=plan.val(ld), ft=ft, mprev=prev_imm_all):
                v.wait_ge(plan.sems[ld], w)
                v.wait_ge(plan.sems[mmx], mprev)
                v.tensor_mul(s1f, ur, kr)
                v.tensor_mul(s2f, ui, ki)
                v.tensor_sub(vf_t(ft, 0), s1f, s2f)
                v.tensor_mul(s1f, ur, ki)
                v.tensor_mul(s2f, ui, kr)
                return v.tensor_add(vf_t(ft, 1), s1f, s2f)
            plan.op("vector", vfc)
        vf_done = plan.val(vv)
        xn_t = [c_bf[:, 1024 * lt:1024 * (lt + 1)] for lt in range(16)]
        plan.op("sync", lambda sy, b=b: sy.dma_start(
            out=c_bf[:, 0:16384].rearrange("p (lt n) -> p lt n", lt=16),
            in_=dram["s_xnT"].rearrange("(b lt p) n -> b p lt n", b=2, p=128)[b]),
            dma=ph + "x", wait=[(vv, prev_epi)] + deps)
        xn_ld = plan.val(ph + "x")
        for lt in range(16):
            bb = 0
            def itwload(sy, lt=lt):
                sy.dma_start(
                    out=w_bf[:, 0:2176].rearrange("p (ft n) -> p ft n", ft=17),
                    in_=dram["twic_t"][lt].rearrange("(ft p) n -> p ft n", p=128))
                return sy.dma_start(
                    out=w_bf[:, 2176:4352].rearrange("p (ft n) -> p ft n", ft=17),
                    in_=dram["twis_t"][lt].rearrange("(ft p) n -> p ft n", p=128))
            plan.op("sync", itwload, dma=ph + "t", wait=[(mmx, prev_imm_all)] + deps)
            tw_ld = plan.val(ph + "t")
            def imm(t):
                for nch in range(2):
                    for cs in range(2):
                        for ft in range(17):
                            ins = t.matmul(
                                bufs.PSW[:, 512 * nch:512 * (nch + 1)],
                                w_bf[:, 2176 * cs + 128 * ft:2176 * cs + 128 * (ft + 1)],
                                vf_t(ft, cs)[:, 512 * nch:512 * (nch + 1)],
                                start=(cs == 0 and ft == 0), stop=(cs == 1 and ft == 16))
                return ins
            plan.op("tensor", imm, inc=(mmx, 1),
                    wait=[(ph + "t", tw_ld), (vv, max(vf_done, prev_epi))])
            prev_itw_mm[bb] = plan.val(mmx)
            prev_imm_all = plan.val(mmx)
            def epi_v(v, lt=lt, mw=plan.val(mmx), xn_ld=xn_ld, sw=plan.val("sc_p")):
                v.wait_ge(plan.sems[mmx], mw)
                v.wait_ge(plan.sems[ph + "x"], xn_ld)
                v.wait_ge(plan.sems["sc_p"], sw)
                v.tensor_mul(t1r, xn_t[lt], dbc)
                return v.tensor_add(y2r, bufs.PSW[:, :], t1r)
            plan.op("vector", epi_v)
            prev_epi = plan.val(vv)
            def epi_s(s, w=prev_epi, pg=prev_gstore):
                s.wait_ge(plan.sems[vv], w)
                s.wait_ge(plan.sems[stx], pg)
                return s.activation(gout, y2r, AF.Gelu_apprx_tanh)
            plan.op("scalar", epi_s)
            plan.op("sync", lambda sy, b=b, lt=lt: sy.dma_start(
                out=dram["s_g"][b, 128 * lt:128 * (lt + 1), :], in_=gout),
                dma=stx, wait=[(ss, plan.val(ss))])
            prev_gstore = plan.val(stx)
    return stx, plan.val(stx)



def emit_wout(plan, bufs, ph, dram, deps=()):
    cst = plan.consts
    a_bf, b_bf, c_bf, w_bf = bufs.a_bf(), bufs.b_bf(), bufs.c_bf(), bufs.w_bf()
    SM = bufs.SM
    ld, mmx, stx = ph + "ld", ph + "mm", ph + "st"
    for s in (ld, mmx, stx):
        plan.sem(s)
    vv, ss = "ve_p", "sc_p"
    deps = list(deps) + list(plan.cst_wait)
    plan.op("sync", lambda sy: sy.dma_start(
        out=SM[:, 24:40], in_=dram["b_out"].rearrange("(dt p) o -> p (dt o)", p=128),
        allow_slow_non_contiguous=True),
        dma=ph + "b", wait=deps)
    bout_ld = plan.val(ph + "b")
    prev_gT_mm = 0
    prev_z = 0
    prev_glu_store = 0
    for b in range(2):
        g_t = [a_bf[:, 1024 * lt:1024 * (lt + 1)] for lt in range(16)]
        plan.op("sync", lambda sy, b=b: sy.dma_start(
            out=a_bf[:, 0:16384].rearrange("p (lt n) -> p lt n", lt=16),
            in_=dram["s_g"].rearrange("b (lt p) n -> b p lt n", p=128)[b]),
            dma=ph + "g", wait=[(mmx, prev_gT_mm)] + deps)
        g_ld = plan.val(ph + "g")
        gT = [c_bf[:, 2048 * ht:2048 * (ht + 1)] for ht in range(8)]
        psb = bufs.PSW[:, :].bitcast(BF16)
        for ht in range(8):
            def trans(t, ht=ht, g_ld=g_ld, pe=plan.val(vv)):
                t.wait_ge(plan.sems[ph + "g"], g_ld)
                t.wait_ge(plan.sems[vv], pe)
                for lt in range(16):
                    ins = t.transpose(psb[:, 128 * lt:128 * (lt + 1)],
                                      g_t[lt][:, 128 * ht:128 * (ht + 1)], cst["ident"])
                return ins
            plan.op("tensor", trans, inc=(mmx, 1))
            def tev(v, ht=ht, mw=plan.val(mmx)):
                v.wait_ge(plan.sems[mmx], mw)
                return v.tensor_copy(gT[ht], psb[:, 0:2048])
            plan.op("vector", tev)
        gT_done = plan.val(vv)
        prev_gT_mm = plan.val(mmx)
        for dpt in range(16):
            bb = dpt % 2
            wslab = w_bf[:, 1024 * bb:1024 * (bb + 1)]
            def wload(g, dpt=dpt, wslab=wslab):
                return g.dma_start(
                    out=wslab.rearrange("p (ht n) -> p ht n", ht=8),
                    in_=dram["w_out"].rearrange("(ht p) dp -> p ht dp", p=128)[:, :, 128 * dpt:128 * (dpt + 1)])
            wsem = ph + "wld" + str(bb)
            plan.op("gpsimd", wload, dma=wsem,
                    wait=[(mmx, prev_gT_mm if dpt < 2 else plan.val(mmx))])
            w_ld = plan.val(wsem)
            def zmm(t, dpt=dpt, wslab=wslab, w_ld=w_ld, gd=gT_done, pz=prev_z, wsem=wsem):
                t.wait_ge(plan.sems[wsem], w_ld)
                t.wait_ge(plan.sems[vv], max(gd, pz))
                for nch in range(4):
                    for ht in range(8):
                        ins = t.matmul(bufs.PS[nch][:, :],
                                       wslab[:, 128 * ht:128 * (ht + 1)],
                                       gT[ht][:, 512 * nch:512 * (nch + 1)],
                                       start=(ht == 0), stop=(ht == 7))
                return ins
            plan.op("tensor", zmm, inc=(mmx, 1))
            zt = b_bf[:, 2048 * dpt:2048 * (dpt + 1)]
            def zev(v, dpt=dpt, zt=zt, mw=plan.val(mmx), bl=bout_ld, sw=plan.val("sc_p")):
                v.wait_ge(plan.sems[mmx], mw)
                v.wait_ge(plan.sems[ph + "b"], bl)
                v.wait_ge(plan.sems["sc_p"], sw)
                for nch in range(4):
                    ins = v.tensor_scalar(zt[:, 512 * nch:512 * (nch + 1)],
                                          bufs.PS[nch][:, :], SM[:, 24 + dpt:25 + dpt],
                                          None, ALU.add)
                return ins
            plan.op("vector", zev)
            prev_z = plan.val(vv)
        for ht in range(8):
            sg = a_bf[:, 16384 + 2048 * (ht % 2):16384 + 2048 * (ht % 2 + 1)]
            def glu_s(s, ht=ht, sg=sg, w=prev_z, pg=prev_glu_store):
                s.wait_ge(plan.sems[vv], w)
                s.wait_ge(plan.sems[stx], pg)
                return s.activation(sg, b_bf[:, 2048 * (8 + ht):2048 * (9 + ht)], AF.Sigmoid)
            plan.op("scalar", glu_s)
            def glu_v(v, ht=ht, sg=sg, w=plan.val(ss), pg=prev_glu_store):
                v.wait_ge(plan.sems[ss], w)
                v.wait_ge(plan.sems[stx], pg)
                return v.tensor_mul(sg, b_bf[:, 2048 * ht:2048 * (ht + 1)], sg)
            plan.op("vector", glu_v)
            plan.op("sync", lambda sy, b=b, ht=ht, sg=sg: sy.dma_start(
                out=dram["s_glu"].rearrange("(ht p) t -> ht p t", p=128)[ht, :, 2048 * b:2048 * (b + 1)],
                in_=sg), dma=stx, wait=[(vv, plan.val(vv))])
            prev_glu_store = plan.val(stx)
    return stx, plan.val(stx)


def emit_l0attn(plan, bufs, dram, deps=()):
    s1, v1 = emit_dft_fwd(plan, bufs, "df", dram, with_k=True, deps=deps)
    s2, v2 = emit_conv(plan, bufs, "cv", dram, deps=[(s1, v1)])
    s3, v3 = emit_wout(plan, bufs, "wo", dram, deps=[(s2, v2)])
    s4, v4 = emit_ln(plan, bufs, "lns", dram["s_glu"], dram["s_attn0"],
                     gamma=dram["ssm_gamma"], beta=dram["ssm_beta"],
                     src_f32=False, wait_st=[(s3, v3)])
    return s4, v4



def emit_moe(plan, bufs, ph, dram, deps=()):
    cst = plan.consts
    a_bf, b_bf, c_bf, w_bf = bufs.a_bf(), bufs.b_bf(), bufs.c_bf(), bufs.w_bf()
    A, Bb, C, SM = bufs.AR_A, bufs.AR_B, bufs.AR_C, bufs.SM
    ld, mmx = ph + "ld", ph + "mm"
    for s in (ld, mmx):
        plan.sem(s)
    vv, ss = "ve_p", "sc_p"
    deps = list(deps) + list(plan.cst_wait)
    IC = 512
    TH = 2048
    xn_t = [a_bf[:, 2048 * i:2048 * (i + 1)] for i in range(8)]
    acc = [Bb[:, 2048 * i:2048 * (i + 1)] for i in range(8)]
    g_t = [a_bf[:, 16384 + 2048 * i:16384 + 2048 * (i + 1)] for i in range(4)]
    u_t = [a_bf[:, 24576 + 2048 * i:24576 + 2048 * (i + 1)] for i in range(4)]
    cwb = C[:, 0:2048]
    crow = C[0:1, 0:2048]
    wd_buf = [c_bf[:, 4096 + 4096 * p:4096 + 4096 * (p + 1)] for p in range(2)]
    cwT = C[0:8, 6144:8192]
    wg_buf = w_bf[:, 0:4096]
    wu_buf = w_bf[:, 4096:8192]
    gate_b = w_bf[:, 8192:8256]

    plan.op("gpsimd", lambda g: g.dma_start(
        out=gate_b.rearrange("p (i e) -> p i e", i=8),
        in_=dram["gate_wT"].rearrange("(i p) e -> p i e", p=128)), dma=ph + "ga", wait=deps)
    gate_ld = plan.val(ph + "ga")

    nbank = 6
    bank_free = [("ve_p", 0)] * nbank
    bk = [0]

    def take_bank():
        b = bk[0] % nbank
        bk[0] += 1
        return bufs.PS[b], bank_free[b], b

    prev_half_v = 0
    for th in range(2):
        t0 = th * TH
        plan.op("sync", lambda sy, t0=t0: sy.dma_start(
            out=a_bf[:, 0:16384].rearrange("p (dt t) -> p dt t", dt=8),
            in_=fm_dram(dram["s_xn"], t0, TH)), dma=ld,
            wait=deps + [(vv, prev_half_v), (ph + "st", plan.val(ph + "st"))])
        xn_ld = plan.val(ld)
        sc0 = SM[:, 40:48]; m1 = SM[:, 48:49]; m2 = SM[:, 49:50]; tmp8 = SM[:, 56:64]
        prev_cwt_mm = 0
        for tt in range(16):
            psg, bfree, bid = take_bank()
            def gmm(t, tt=tt, psg=psg, bfree=bfree, xn_ld=xn_ld):
                t.wait_ge(plan.sems[ld], xn_ld)
                t.wait_ge(plan.sems[ph + "ga"], gate_ld)
                t.wait_ge(plan.sems[bfree[0]], bfree[1])
                for ci in range(8):
                    ins = t.matmul(psg[:, 0:8], xn_t[ci][:, 128 * tt:128 * (tt + 1)],
                                   gate_b[:, 8 * ci:8 * (ci + 1)],
                                   start=(ci == 0), stop=(ci == 7))
                return ins
            plan.op("tensor", gmm, inc=(mmx, 1))
            def cwv(v, psg=psg, mw=plan.val(mmx)):
                v.wait_ge(plan.sems[mmx], mw)
                v.reduce_max(m1, psg[:, 0:8], axis=AX)
                v.tensor_scalar(m1, m1, -1.0, None, ALU.mult)
                return v.tensor_scalar(sc0, psg[:, 0:8], m1, None, ALU.add)
            plan.op("vector", cwv)
            bank_free[bid] = ("ve_p", plan.val(vv))
            plan.op("scalar", lambda s, w=plan.val(vv): (
                s.wait_ge(plan.sems[vv], w), s.activation(sc0, sc0, AF.Exp))[1])
            def cwv2(v, w=plan.val(ss)):
                v.wait_ge(plan.sems[ss], w)
                v.reduce_sum(m1, sc0, axis=AX)
                v.reciprocal(m1, m1)
                v.tensor_scalar(sc0, sc0, m1, None, ALU.mult)
                v.reduce_max(m1, sc0, axis=AX)
                v.tensor_scalar(tmp8, sc0, m1, None, ALU.is_ge)
                v.scalar_tensor_tensor(tmp8, tmp8, -1e30, sc0, ALU.mult, ALU.add)
                v.reduce_max(m2, tmp8, axis=AX)
                v.tensor_scalar(tmp8, sc0, m2, None, ALU.is_ge)
                return v.tensor_mul(sc0, sc0, tmp8)
            plan.op("vector", cwv2)
            def cwt(t, tt=tt, w=plan.val(vv), pm=prev_cwt_mm):
                t.wait_ge(plan.sems[vv], w)
                return t.transpose(bufs.PSW[0:8, 0:128], sc0[:, 0:8], cst["ident_f"])
            plan.op("tensor", cwt, inc=(mmx, 1))
            def cwev(v, tt=tt, mw=plan.val(mmx)):
                v.wait_ge(plan.sems[mmx], mw)
                return v.tensor_copy(cwT[:, 128 * tt:128 * (tt + 1)], bufs.PSW[0:8, 0:128])
            plan.op("vector", cwev)
            prev_cwt_mm = plan.val(vv)
        plan.op("sync", lambda sy, t0=t0: sy.dma_start(
            out=dram["s_cwT"][:, t0:t0 + TH], in_=cwT), dma=ph + "cw",
            wait=[(vv, plan.val(vv))])
        cw_st = plan.val(ph + "cw")

        first_acc = True
        wld_free_mm = 0
        wd_free_mm = [0, 0]
        for e in range(9):
            shared = (e == 8)
            if not shared:
                plan.sem(ph + "dbg")
                plan.op("sync", lambda sy, e=e, t0=t0: sy.dma_start(
                    out=crow, in_=dram["s_cwT"][e:e + 1, t0:t0 + TH]),
                    dma=ph + "cr", wait=[(ph + "cw", cw_st), (vv, plan.val(vv)),
                                         (ph + "dbg", plan.val(ph + "dbg"))])
                crow_ld = plan.val(ph + "cr")
                for chk in range(4):
                    def cwbmm(t, chk=chk, w=crow_ld, pe=plan.val(vv)):
                        t.wait_ge(plan.sems[ph + "cr"], w)
                        t.wait_ge(plan.sems[vv], pe)
                        return t.matmul(bufs.PSW[:, 0:512], cst["ones_row_f"],
                                        crow[:, 512 * chk:512 * (chk + 1)],
                                        start=True, stop=True)
                    plan.op("tensor", cwbmm, inc=(mmx, 1))
                    def cwbev(v, chk=chk, mw=plan.val(mmx)):
                        v.wait_ge(plan.sems[mmx], mw)
                        return v.tensor_copy(cwb[:, 512 * chk:512 * (chk + 1)],
                                             bufs.PSW[:, 0:512])
                    plan.op("vector", cwbev)
            cwb_done = plan.val(vv)
            if th == 0 and e == 0 and "s_dbgc" in dram:
                plan.op("sync", lambda sy: sy.dma_start(out=dram["s_dbgc"], in_=cwb),
                        dma=ph + "dbg", wait=[(vv, cwb_done)])
            for ic in range(4 if shared else 8):
                if shared:
                    wg_src = dram["ws_g"][:, IC * ic:IC * (ic + 1)]
                    wu_src = dram["ws_u"][:, IC * ic:IC * (ic + 1)]
                    wd_src = dram["ws_d"][IC * ic:IC * (ic + 1), :]
                else:
                    wg_src = dram["moe_wg"][e][:, IC * ic:IC * (ic + 1)]
                    wu_src = dram["moe_wu"][e][:, IC * ic:IC * (ic + 1)]
                    wd_src = dram["moe_wd"][e][IC * ic:IC * (ic + 1), :]
                def wgld(g, wg_src=wg_src, wu_src=wu_src, wf=plan.val(mmx), dp=list(deps),
                         pw=plan.val(ph + "w")):
                    for s_, v_ in dp:
                        g.wait_ge(plan.sems[s_], v_)
                    g.wait_ge(plan.sems[mmx], wf)
                    g.wait_ge(plan.sems[ph + "w"], pw)
                    g.dma_start(out=wg_buf.rearrange("p (ct i) -> p ct i", ct=8),
                                in_=wg_src.rearrange("(ct p) i -> p ct i", p=128))
                    return g.dma_start(out=wu_buf.rearrange("p (ct i) -> p ct i", ct=8),
                                       in_=wu_src.rearrange("(ct p) i -> p ct i", p=128))
                plan.op("gpsimd", wgld, dma=ph + "w")
                wgu_ld = plan.val(ph + "w")
                wdp = ic % 2
                def wdld(g, wd_src=wd_src, wdp=wdp, wf=plan.val(mmx), dp=list(deps),
                         pw=plan.val(ph + "wd" + str(wdp))):
                    for s_, v_ in dp:
                        g.wait_ge(plan.sems[s_], v_)
                    g.wait_ge(plan.sems[mmx], wf)
                    g.wait_ge(plan.sems[ph + "wd" + str(wdp)], pw)
                    return g.dma_start(out=wd_buf[wdp].rearrange("p (it d) -> p it d", it=4),
                                       in_=wd_src.rearrange("(it p) d -> p it d", p=128))
                plan.op("gpsimd", wdld, dma=ph + "wd" + str(wdp))
                wd_ld = plan.val(ph + "wd" + str(wdp))
                for uu in range(2):
                    wbuf = wg_buf if uu == 0 else wu_buf
                    dst_t = g_t if uu == 0 else u_t
                    for it in range(4):
                        for tc in range(4):
                            psb, bfree, bid = take_bank()
                            def gu_mm(t, it=it, tc=tc, uu=uu, wbuf=wbuf, psb=psb,
                                      bfree=bfree, wl=wgu_ld, xl=xn_ld,
                                      pst=plan.val(ph + "st")):
                                t.wait_ge(plan.sems[ph + "w"], wl)
                                t.wait_ge(plan.sems[ld], xl)
                                t.wait_ge(plan.sems[ph + "st"], pst)
                                t.wait_ge(plan.sems[bfree[0]], bfree[1])
                                for ci in range(8):
                                    ins = t.matmul(
                                        psb[:, :],
                                        wbuf[:, 512 * ci + 128 * it:512 * ci + 128 * (it + 1)],
                                        xn_t[ci][:, 512 * tc:512 * (tc + 1)],
                                        start=(ci == 0), stop=(ci == 7))
                                return ins
                            plan.op("tensor", gu_mm, inc=(mmx, 1))
                            mmv = plan.val(mmx)
                            if uu == 0:
                                def sev(s, it=it, tc=tc, psb=psb, mw=mmv):
                                    s.wait_ge(plan.sems[mmx], mw)
                                    return s.activation(
                                        g_t[it][:, 512 * tc:512 * (tc + 1)], psb[:, :], AF.Silu)
                                plan.op("scalar", sev)
                                bank_free[bid] = ("sc_p", plan.val(ss))
                            else:
                                def uev(v, it=it, tc=tc, psb=psb, mw=mmv, shared=shared,
                                        cd=cwb_done):
                                    v.wait_ge(plan.sems[mmx], mw)
                                    if shared:
                                        return v.tensor_copy(
                                            u_t[it][:, 512 * tc:512 * (tc + 1)], psb[:, :])
                                    v.wait_ge(plan.sems[vv], cd)
                                    return v.tensor_mul(
                                        u_t[it][:, 512 * tc:512 * (tc + 1)], psb[:, :],
                                        cwb[:, 512 * tc:512 * (tc + 1)])
                                plan.op("vector", uev)
                                bank_free[bid] = ("ve_p", plan.val(vv))
                wld_free_mm = plan.val(mmx)
                def hmul(v, w=plan.val(ss)):
                    v.wait_ge(plan.sems[ss], w)
                    for it in range(4):
                        ins = v.tensor_mul(g_t[it], g_t[it], u_t[it])
                    return ins
                plan.op("vector", hmul)
                h_done = plan.val(vv)
                for dt in range(8):
                    for tc in range(4):
                        psb, bfree, bid = take_bank()
                        def yo_mm(t, dt=dt, tc=tc, psb=psb, bfree=bfree, wl=wd_ld,
                                  hd=h_done, wdp=wdp):
                            t.wait_ge(plan.sems[ph + "wd" + str(wdp)], wl)
                            t.wait_ge(plan.sems[vv], hd)
                            t.wait_ge(plan.sems[bfree[0]], bfree[1])
                            for it in range(4):
                                ins = t.matmul(
                                    psb[:, :],
                                    wd_buf[wdp][:, 1024 * it + 128 * dt:1024 * it + 128 * (dt + 1)],
                                    g_t[it][:, 512 * tc:512 * (tc + 1)],
                                    start=(it == 0), stop=(it == 3))
                            return ins
                        plan.op("tensor", yo_mm, inc=(mmx, 1))
                        def yoev(v, dt=dt, tc=tc, psb=psb, mw=plan.val(mmx), fa=first_acc):
                            v.wait_ge(plan.sems[mmx], mw)
                            dsl = acc[dt][:, 512 * tc:512 * (tc + 1)]
                            if fa:
                                return v.tensor_copy(dsl, psb[:, :])
                            return v.tensor_add(dsl, dsl, psb[:, :])
                        plan.op("vector", yoev)
                        bank_free[bid] = ("ve_p", plan.val(vv))
                first_acc = False
                wd_free_mm[wdp] = plan.val(mmx)
        x0stg = [A[:, 8192 + 2048 * p:8192 + 2048 * (p + 1)] for p in range(2)]
        atstg = [a_bf[:, 24576 + 2048 * p:24576 + 2048 * (p + 1)] for p in range(2)]
        for dt in range(8):
            p = dt % 2
            def x1ld(sy, dt=dt, p=p, t0=t0, pe=plan.val(vv), pst=plan.val(ph + "st")):
                sy.wait_ge(plan.sems[vv], pe)
                sy.wait_ge(plan.sems[ph + "st"], pst)
                sy.dma_start(out=x0stg[p],
                             in_=dram["x0T"].rearrange("(dt q) t -> dt q t", q=128)[dt, :, t0:t0 + TH])
                return sy.dma_start(out=atstg[p],
                                    in_=dram["s_attn0"].rearrange("(dt q) t -> dt q t", q=128)[dt, :, t0:t0 + TH])
            plan.op("sync", x1ld, dma=ph + "x1")
            def x1v(v, dt=dt, p=p, w=plan.val(ph + "x1")):
                v.wait_ge(plan.sems[ph + "x1"], w)
                v.tensor_add(x0stg[p], x0stg[p], acc[dt])
                return v.tensor_add(x0stg[p], x0stg[p], atstg[p])
            plan.op("vector", x1v)
            plan.op("sync", lambda sy, dt=dt, p=p, t0=t0, w=plan.val(vv): (
                sy.wait_ge(plan.sems[vv], w),
                sy.dma_start(out=dram["s_x1"].rearrange("(dt q) t -> dt q t", q=128)[dt, :, t0:t0 + TH],
                             in_=x0stg[p]))[1], dma=ph + "st")
        prev_half_v = plan.val(vv)
    return ph + "st", plan.val(ph + "st")



def emit_attn(plan, bufs, ph, dram, deps=()):
    cst = plan.consts
    a_bf, b_bf, c_bf, w_bf = bufs.a_bf(), bufs.b_bf(), bufs.c_bf(), bufs.w_bf()
    A, Bb, C, SM = bufs.AR_A, bufs.AR_B, bufs.AR_C, bufs.SM
    mmx = ph + "mm"
    plan.sem(mmx)
    vv, ss = "ve_p", "sc_p"
    deps = list(deps) + list(plan.cst_wait)
    qT = [a_bf[:, 2048 * i:2048 * (i + 1)] for i in range(8)]
    kT = [a_bf[:, 16384 + 2048 * i:16384 + 2048 * (i + 1)] for i in range(8)]
    xt = [b_bf[:, 2048 * i:2048 * (i + 1)] for i in range(8)]
    vt = [c_bf[:, 1024 * i:1024 * (i + 1)] for i in range(16)]
    et = [b_bf[:, 16384 + 512 * i:16384 + 512 * (i + 1)] for i in range(16)]
    wbuf = w_bf[:, 0:8192]
    ystg = b_bf[:, 24576 + 0:24576 + 512]
    rbst = Bb[:, 12544:13056]
    q2s = Bb[:, 13056:13568]
    nbank = 4
    bank_free = [("ve_p", 0)] * nbank
    bkc = [0]

    def takeb():
        b = bkc[0] % nbank
        bkc[0] += 1
        return bufs.PS[b], bank_free[b], b

    prev_y_store = 0
    for b in range(2):
        t0 = b * 2048
        plan.op("sync", lambda sy, t0=t0: sy.dma_start(
            out=b_bf[:, 0:16384].rearrange("p (dt t) -> p dt t", dt=8),
            in_=fm_dram(dram["s_xn1"], t0, 2048)), dma=ph + "x",
            wait=deps + [(mmx, plan.val(mmx)), (vv, plan.val(vv)), (ss, plan.val(ss))])
        x_ld = plan.val(ph + "x")
        for wi, wname in enumerate(["wqT", "wkT", "wvT"]):
            plan.op("gpsimd", lambda g, wname=wname, mv=plan.val(mmx): (
                g.wait_ge(plan.sems[mmx], mv),
                g.dma_start(out=wbuf.rearrange("p (ct n) -> p ct n", ct=8),
                            in_=dram[wname].rearrange("(ct p) n -> p ct n", p=128)))[1],
                dma=ph + "w", wait=deps)
            w_ld = plan.val(ph + "w")
            if wi < 2:
                dst = qT if wi == 0 else kT
                for m in range(8):
                    for tc in range(4):
                        psb, bfree, bid = takeb()
                        def pmm(t, m=m, tc=tc, psb=psb, bfree=bfree, w_ld=w_ld, x_ld=x_ld):
                            t.wait_ge(plan.sems[ph + "w"], w_ld)
                            t.wait_ge(plan.sems[ph + "x"], x_ld)
                            t.wait_ge(plan.sems[bfree[0]], bfree[1])
                            for ci in range(8):
                                ins = t.matmul(psb[:, :],
                                               wbuf[:, 1024 * ci + 128 * m:1024 * ci + 128 * (m + 1)],
                                               xt[ci][:, 512 * tc:512 * (tc + 1)],
                                               start=(ci == 0), stop=(ci == 7))
                            return ins
                        plan.op("tensor", pmm, inc=(mmx, 1))
                        def pev(v, m=m, tc=tc, psb=psb, mw=plan.val(mmx), dst=dst):
                            v.wait_ge(plan.sems[mmx], mw)
                            return v.tensor_copy(dst[m][:, 512 * tc:512 * (tc + 1)], psb[:, :])
                        plan.op("vector", pev)
                        bank_free[bid] = ("ve_p", plan.val(vv))
            else:
                for tt in range(16):
                    for nc2 in range(2):
                        psb, bfree, bid = takeb()
                        def vmm(t, tt=tt, nc2=nc2, psb=psb, bfree=bfree, w_ld=w_ld, x_ld=x_ld):
                            t.wait_ge(plan.sems[ph + "w"], w_ld)
                            t.wait_ge(plan.sems[ph + "x"], x_ld)
                            t.wait_ge(plan.sems[bfree[0]], bfree[1])
                            for ci in range(8):
                                ins = t.matmul(psb[:, :],
                                               xt[ci][:, 128 * tt:128 * (tt + 1)],
                                               wbuf[:, 1024 * ci + 512 * nc2:1024 * ci + 512 * (nc2 + 1)],
                                               start=(ci == 0), stop=(ci == 7))
                            return ins
                        plan.op("tensor", vmm, inc=(mmx, 1))
                        def vev(v, tt=tt, nc2=nc2, psb=psb, mw=plan.val(mmx)):
                            v.wait_ge(plan.sems[mmx], mw)
                            return v.tensor_copy(vt[tt][:, 512 * nc2:512 * (nc2 + 1)], psb[:, :])
                        plan.op("vector", vev)
                        bank_free[bid] = ("ve_p", plan.val(vv))
        qkv_done = plan.val(vv)
        sq = Bb[:, 13568:15616]
        q2m = SM[0:1, 104:105]
        k2m = SM[0:1, 105:106]
        pch = [bufs.PSW[0:1, 0:512], bufs.PSW[0:1, 512:1024],
               bufs.PS[4][0:1, 0:512], bufs.PS[5][0:1, 0:512]]
        for qi, (src_, dstm) in enumerate([(qT, q2m), (kT, k2m)]):
            prev_nmm = plan.val(mmx)
            for m in range(8):
                def sq1(s, m=m, src_=src_, w=qkv_done, pn=prev_nmm):
                    s.wait_ge(plan.sems[vv], w)
                    s.wait_ge(plan.sems[mmx], pn)
                    return s.activation(sq, src_[m], AF.Square)
                plan.op("scalar", sq1)
                def nmm(t, m=m, w=plan.val(ss), pv=plan.val(vv)):
                    t.wait_ge(plan.sems[ss], w)
                    t.wait_ge(plan.sems[vv], pv)
                    for cc in range(4):
                        ins = t.matmul(pch[cc], cst["ones_col_f"],
                                       sq[:, 512 * cc:512 * (cc + 1)],
                                       start=(m == 0), stop=(m == 7))
                    return ins
                plan.op("tensor", nmm, inc=(mmx, 1))
                prev_nmm = plan.val(mmx)
            def bfin(v, dstm=dstm, mw=prev_nmm):
                v.wait_ge(plan.sems[mmx], mw)
                v.reduce_max(SM[0:1, 107:108], pch[0], axis=AX)
                v.reduce_max(SM[0:1, 108:109], pch[1], axis=AX)
                v.tensor_max(SM[0:1, 107:108], SM[0:1, 107:108], SM[0:1, 108:109])
                v.reduce_max(SM[0:1, 108:109], pch[2], axis=AX)
                v.tensor_max(SM[0:1, 107:108], SM[0:1, 107:108], SM[0:1, 108:109])
                v.reduce_max(SM[0:1, 108:109], pch[3], axis=AX)
                return v.tensor_max(dstm, SM[0:1, 107:108], SM[0:1, 108:109])
            plan.op("vector", bfin)
        def bnd_v(v):
            v.tensor_mul(q2m, q2m, k2m)
            return v.tensor_scalar(q2m, q2m, 1.0 / 64.0, None, ALU.mult)
        plan.op("vector", bnd_v)
        def bnd_s(s, w=plan.val(vv)):
            s.wait_ge(plan.sems[vv], w)
            return s.activation(q2m, q2m, AF.Sqrt)
        plan.op("scalar", bnd_s)
        def bnd_v2(v, w=plan.val(ss)):
            v.wait_ge(plan.sems[ss], w)
            return v.tensor_scalar(q2m, q2m, -1.02 * 0.125, None, ALU.mult)
        plan.op("vector", bnd_v2)
        def bnd_bc(t, w=plan.val(vv)):
            t.wait_ge(plan.sems[vv], w)
            return t.matmul(bufs.PS[5][:, 0:1], cst["ones_row_f"], q2m, start=True, stop=True)
        plan.op("tensor", bnd_bc, inc=(mmx, 1))
        negb = SM[:, 109:110]
        def bnd_ev(v, w=plan.val(mmx)):
            v.wait_ge(plan.sems[mmx], w)
            return v.tensor_copy(negb, bufs.PS[5][:, 0:1])
        plan.op("vector", bnd_ev)
        bnd_done = plan.val(vv)
        prev_dpv = 0
        for h in range(16):
            hb = (h % 2) * 64
            hm = h // 2
            for qc in range(4):
                stl = 4 * (qc + 1)
                for st in range(stl):
                    psb, bfree, bid = takeb()
                    def smm(t, hm=hm, hb=hb, qc=qc, st=st, psb=psb, bfree=bfree,
                            w=max(qkv_done, bnd_done), pd=prev_dpv):
                        t.wait_ge(plan.sems[vv], w)
                        t.wait_ge(plan.sems[mmx], pd)
                        t.wait_ge(plan.sems[bfree[0]], bfree[1])
                        return t.matmul(psb[:, :],
                                        kT[hm][hb:hb + 64, 128 * st:128 * (st + 1)],
                                        qT[hm][hb:hb + 64, 512 * qc:512 * (qc + 1)],
                                        start=True, stop=True)
                    plan.op("tensor", smm, inc=(mmx, 1))
                    def sev(s, st=st, psb=psb, mw=plan.val(mmx)):
                        s.wait_ge(plan.sems[mmx], mw)
                        return s.activation(et[st], psb[:, :], AF.Exp, bias=negb, scale=0.125)
                    plan.op("scalar", sev)
                    bank_free[bid] = ("sc_p", plan.val(ss))
                    j = st - 4 * qc
                    if 0 <= j:
                        def msk(v, st=st, j=j, w=plan.val(ss)):
                            v.wait_ge(plan.sems[ss], w)
                            return v.tensor_mul(et[st], et[st], cst["mask" + str(j)])
                        plan.op("vector", msk)
                e_done = (plan.val(ss), plan.val(vv))
                def dpv(t, h=h, qc=qc, stl=stl, ed=e_done):
                    t.wait_ge(plan.sems[ss], ed[0])
                    t.wait_ge(plan.sems[vv], ed[1])
                    for st in range(stl):
                        t.matmul(bufs.PS[4][0:1, 0:512], cst["ones_col_b"], et[st],
                                 start=(st == 0), stop=(st == stl - 1))
                    for st in range(stl):
                        ins = t.matmul(bufs.PS[5][0:64, 0:512], vt[st][:, 64 * h:64 * (h + 1)],
                                       et[st], start=(st == 0), stop=(st == stl - 1))
                    return ins
                plan.op("tensor", dpv, inc=(mmx, 1))
                prev_dpv = plan.val(mmx)
                def rec(v, mw=plan.val(mmx)):
                    v.wait_ge(plan.sems[mmx], mw)
                    return v.reciprocal(SM[0:1, 110:622], bufs.PS[4][0:1, 0:512])
                plan.op("vector", rec)
                def rbc(t, w=plan.val(vv)):
                    t.wait_ge(plan.sems[vv], w)
                    return t.matmul(bufs.PSW[0:64, 0:512], cst["ones_row_f"][:, 0:64],
                                    SM[0:1, 110:622], start=True, stop=True)
                plan.op("tensor", rbc, inc=(mmx, 1))
                def yev(v, h=h, qc=qc, mw=plan.val(mmx), pst=prev_y_store):
                    v.wait_ge(plan.sems[mmx], mw)
                    if pst:
                        v.wait_ge(plan.sems[ph + "y"], pst)
                    v.tensor_copy(rbst[0:64, :], bufs.PSW[0:64, 0:512])
                    return v.tensor_mul(ystg[0:64, :], bufs.PS[5][0:64, 0:512], rbst[0:64, :])
                plan.op("vector", yev)
                plan.op("sync", lambda sy, b=b, h=h, qc=qc, w=plan.val(vv): (
                    sy.wait_ge(plan.sems[vv], w),
                    sy.dma_start(out=dram["s_yatt"][64 * h:64 * (h + 1),
                                                    2048 * b + 512 * qc:2048 * b + 512 * (qc + 1)],
                                 in_=ystg[0:64, :]))[1], dma=ph + "y")
                prev_y_store = plan.val(ph + "y")
    plan.sem(ph + "o")
    for b in range(2):
        t0 = b * 2048
        plan.op("sync", lambda sy, t0=t0, mv=plan.val(mmx), yv=prev_y_store: (
            sy.wait_ge(plan.sems[ph + "y"], yv),
            sy.wait_ge(plan.sems[mmx], mv),
            sy.dma_start(out=a_bf[:, 0:16384].rearrange("p (dt t) -> p dt t", dt=8),
                         in_=fm_dram(dram["s_yatt"], t0, 2048)))[2], dma=ph + "ya")
        y_ld = plan.val(ph + "ya")
        plan.op("gpsimd", lambda g, mv=plan.val(mmx): (
            g.wait_ge(plan.sems[mmx], mv),
            g.dma_start(out=wbuf.rearrange("p (ct n) -> p ct n", ct=8),
                        in_=dram["w_o"].rearrange("(ct p) n -> p ct n", p=128)))[1],
            dma=ph + "w")
        wo_ld = plan.val(ph + "w")
        for m in range(8):
            for tc in range(4):
                psb, bfree, bid = takeb()
                def omm(t, m=m, tc=tc, psb=psb, bfree=bfree, y_ld=y_ld, wo_ld=wo_ld):
                    t.wait_ge(plan.sems[ph + "ya"], y_ld)
                    t.wait_ge(plan.sems[ph + "w"], wo_ld)
                    t.wait_ge(plan.sems[bfree[0]], bfree[1])
                    for ci in range(8):
                        ins = t.matmul(psb[:, :],
                                       wbuf[:, 1024 * ci + 128 * m:1024 * ci + 128 * (m + 1)],
                                       a_bf[:, 2048 * ci + 512 * tc:2048 * ci + 512 * (tc + 1)],
                                       start=(ci == 0), stop=(ci == 7))
                    return ins
                plan.op("tensor", omm, inc=(mmx, 1))
                ostg = b_bf[:, 24576 + 1024 * (bkc[0] % 2):24576 + 1024 * (bkc[0] % 2) + 512]
                def oev(v, psb=psb, ostg=ostg, mw=plan.val(mmx), pst=plan.val(ph + "o")):
                    v.wait_ge(plan.sems[mmx], mw)
                    if pst:
                        v.wait_ge(plan.sems[ph + "o"], pst)
                    return v.tensor_copy(ostg, psb[:, :])
                plan.op("vector", oev)
                bank_free[bid] = ("ve_p", plan.val(vv))
                plan.op("sync", lambda sy, m=m, tc=tc, t0=t0, ostg=ostg, w=plan.val(vv): (
                    sy.wait_ge(plan.sems[vv], w),
                    sy.dma_start(out=dram["s_attn1"].rearrange("(dt p) t -> dt p t", p=128)[m, :, t0 + 512 * tc:t0 + 512 * (tc + 1)],
                                 in_=ostg))[1], dma=ph + "o")
    return ph + "o", plan.val(ph + "o")



def emit_fp4(plan, bufs, ph, dram, deps=()):
    cst = plan.consts
    a_bf, b_bf, c_bf, w_bf = bufs.a_bf(), bufs.b_bf(), bufs.c_bf(), bufs.w_bf()
    A, Bb, C, SM = bufs.AR_A, bufs.AR_B, bufs.AR_C, bufs.SM
    mmx = ph + "mm"
    plan.sem(mmx)
    vv, ss = "ve_p", "sc_p"
    deps = list(deps) + list(plan.cst_wait)

    def qdq(xf, out_bf, sc_base, F):
        A1 = A[:, sc_base:sc_base + F]
        A2 = A[:, sc_base + F:sc_base + 2 * F]
        A3 = A[:, sc_base + 2 * F:sc_base + 3 * F]
        A4 = A[:, sc_base + 3 * F:sc_base + 4 * F]
        A5 = A[:, sc_base + 4 * F:sc_base + 5 * F]
        nb = F // 16
        sc1 = A[:, sc_base + 5 * F:sc_base + 5 * F + nb]
        sc2 = A[:, sc_base + 5 * F + nb:sc_base + 5 * F + 2 * nb]
        def v1(v, w=plan.val(ss)):
            v.wait_ge(plan.sems[ss], w)
            v.reduce_max(sc1, xf.rearrange("p (b k) -> p b k", k=16), axis=AX,
                         apply_absolute_value=True)
            v.tensor_scalar(sc1, sc1, 1.0 / 6.0, 1e-12, ALU.mult, ALU.max)
            v.reciprocal(sc2, sc1)
            return v.tensor_tensor(
                A1.rearrange("p (b k) -> p b k", k=16),
                xf.rearrange("p (b k) -> p b k", k=16),
                sc2.rearrange("p (b k) -> p b k", k=1).broadcast_to([128, nb, 16]),
                ALU.mult)
        plan.op("vector", v1)
        def s1(s, w=plan.val(vv)):
            s.wait_ge(plan.sems[vv], w)
            s.activation(A2, A1, AF.Sign)
            return s.activation(A1, A1, AF.Abs)
        plan.op("scalar", s1)
        def v2(v, w=plan.val(ss)):
            v.wait_ge(plan.sems[ss], w)
            v.tensor_scalar(A3, A1, 0.25, None, ALU.is_gt)
            v.scalar_tensor_tensor(A3, A1, 0.75, A3, ALU.is_gt, ALU.add)
            v.scalar_tensor_tensor(A3, A1, 1.25, A3, ALU.is_gt, ALU.add)
            v.scalar_tensor_tensor(A3, A1, 1.75, A3, ALU.is_gt, ALU.add)
            v.tensor_scalar(A4, A1, 5.0, None, ALU.is_gt)
            v.scalar_tensor_tensor(A5, A1, 3.5, A4, ALU.is_gt, ALU.add)
            v.scalar_tensor_tensor(A5, A1, 2.5, A5, ALU.is_gt, ALU.add)
            v.tensor_add(A5, A5, A4)
            v.scalar_tensor_tensor(A3, A5, 2.0, A3, ALU.mult, ALU.add)
            v.tensor_mul(A3, A3, A2)
            v.tensor_tensor(
                A3.rearrange("p (b k) -> p b k", k=16),
                A3.rearrange("p (b k) -> p b k", k=16),
                sc1.rearrange("p (b k) -> p b k", k=1).broadcast_to([128, nb, 16]),
                ALU.mult)
            return v.tensor_scalar(out_bf, A3, 0.5, None, ALU.mult)
        plan.op("vector", v2)

    psb_t = bufs.PSW[:, :].bitcast(BF16)

    wspec = [("w_gate", "s_wgq", 22, 1024), ("w_up", "s_wuq", 22, 1024),
             ("w_down", "s_wdq", 8, 2816)]
    ldst = c_bf[:, 0:5632]
    prev_wst = 0
    for wname, dname, ntile, F in wspec:
        for ti in range(ntile):
            xf = A[:, 0:F]
            def wld(sy, wname=wname, ti=ti, xf=xf, mv=plan.val(mmx), vvv=plan.val(vv)):
                sy.wait_ge(plan.sems[mmx], mv)
                sy.wait_ge(plan.sems[vv], vvv)
                return sy.dma_start(out=xf, in_=dram[wname][128 * ti:128 * (ti + 1), :])
            plan.op("sync", wld, dma=ph + "wl", wait=deps)
            wl = plan.val(ph + "wl")
            plan.op("vector", lambda v, wl=wl: (
                v.wait_ge(plan.sems[ph + "wl"], wl), v.memset(SM[0:1, 120:121], 0.0))[1])
            out_bf = a_bf[:, 16384:16384 + F]
            CB = 704 if F > 1024 else F
            for cb in range(0, F, CB):
                qdq(xf[:, cb:cb + CB], out_bf[:, cb:cb + CB], 9728, CB)
            q_done = plan.val(vv)
            nblk = F // 128
            for g0 in range(0, nblk, 11):
                gn = min(11, nblk - g0)
                def wtr(t, g0=g0, gn=gn, out_bf=out_bf, w=q_done, pv=plan.val(vv)):
                    t.wait_ge(plan.sems[vv], max(w, pv))
                    for bj in range(gn):
                        ins = t.transpose(psb_t[:, 128 * bj:128 * (bj + 1)],
                                          out_bf[:, 128 * (g0 + bj):128 * (g0 + bj + 1)],
                                          cst["ident"])
                    return ins
                plan.op("tensor", wtr, inc=(mmx, 1))
                def wev(v, g0=g0, gn=gn, mw=plan.val(mmx), pst=prev_wst):
                    v.wait_ge(plan.sems[mmx], mw)
                    if pst:
                        v.wait_ge(plan.sems[ph + "ws"], pst)
                    return v.tensor_copy(ldst[:, 128 * g0:128 * (g0 + gn)],
                                         psb_t[:, 0:128 * gn])
                plan.op("vector", wev)
            def wst(sy, dname=dname, ti=ti, nblk=nblk, w=plan.val(vv)):
                sy.wait_ge(plan.sems[vv], w)
                return sy.dma_start(
                    out=dram[dname].rearrange("(bj p) i -> p bj i", p=128)[:, :, 128 * ti:128 * (ti + 1)],
                    in_=ldst[:, 0:128 * nblk].rearrange("p (bj k) -> p bj k", bj=nblk))
            plan.op("sync", wst, dma=ph + "ws")
            prev_wst = plan.val(ph + "ws")

    for tt in range(32):
        xcol = b_bf[:, 0:1024]
        def xld(sy, tt=tt, mv=plan.val(mmx), vvv=plan.val(vv)):
            sy.wait_ge(plan.sems[mmx], mv)
            sy.wait_ge(plan.sems[vv], vvv)
            return sy.dma_start(
                out=xcol.rearrange("p (ct k) -> p ct k", ct=8),
                in_=dram["s_xn1"].rearrange("(ct p) t -> p ct t", p=128)[:, :, 128 * tt:128 * (tt + 1)])
        plan.op("sync", xld, dma=ph + "xl", wait=deps)
        xl = plan.val(ph + "xl")
        def xtr(t, w=xl, pv=plan.val(vv)):
            t.wait_ge(plan.sems[ph + "xl"], w)
            t.wait_ge(plan.sems[vv], pv)
            for ct in range(8):
                ins = t.transpose(psb_t[:, 128 * ct:128 * (ct + 1)],
                                  xcol[:, 128 * ct:128 * (ct + 1)], cst["ident"])
            return ins
        plan.op("tensor", xtr, inc=(mmx, 1))
        xtok = A[:, 0:1024]
        def xev(v, mw=plan.val(mmx)):
            v.wait_ge(plan.sems[mmx], mw)
            return v.tensor_copy(xtok, psb_t[:, 0:1024])
        plan.op("vector", xev)
        xq_bf = a_bf[:, 16384:17408]
        qdq(xtok, xq_bf, 9728, 1024)
        def xqt(t, w=plan.val(vv)):
            t.wait_ge(plan.sems[vv], w)
            for ct in range(8):
                ins = t.transpose(psb_t[:, 128 * ct:128 * (ct + 1)],
                                  xq_bf[:, 128 * ct:128 * (ct + 1)], cst["ident"])
            return ins
        plan.op("tensor", xqt, inc=(mmx, 1))
        def xqev(v, mw=plan.val(mmx), pst=plan.val(ph + "xs") if (ph + "xs") in plan.cnt else 0,
                 pws=plan.val(ph + "ws")):
            v.wait_ge(plan.sems[mmx], mw)
            v.wait_ge(plan.sems[ph + "ws"], pws)
            if pst:
                v.wait_ge(plan.sems[ph + "xs"], pst)
            return v.tensor_copy(ldst[:, 0:1024], psb_t[:, 0:1024])
        plan.sem(ph + "xs")
        plan.op("vector", xqev)
        def xqst(sy, tt=tt, w=plan.val(vv)):
            sy.wait_ge(plan.sems[vv], w)
            return sy.dma_start(
                out=dram["s_xq"].rearrange("(ct p) t -> p ct t", p=128)[:, :, 128 * tt:128 * (tt + 1)],
                in_=ldst[:, 0:1024].rearrange("p (ct k) -> p ct k", ct=8))
        plan.op("sync", xqst, dma=ph + "xs")
    xq_done = plan.val(ph + "xs")

    ICS = [512, 512, 512, 512, 512, 256]
    nbank = 4
    bank_free = [("ve_p", 0)] * nbank
    bkc = [0]

    def takeb():
        b = bkc[0] % nbank
        bkc[0] += 1
        return bufs.PS[b], bank_free[b], b

    xq_t = [a_bf[:, 512 * i:512 * (i + 1)] for i in range(8)]
    h_t = [b_bf[:, 2816 * i:2816 * (i + 1)] for i in range(4)]
    gst = b_bf[:, 11264:11776]
    ust = b_bf[:, 11776:12288]
    hqT_t = [c_bf[:, 512 * i:512 * (i + 1)] for i in range(22)]
    wgq_s = w_bf[:, 0:4096]
    wuq_s = w_bf[:, 4096:8192]
    wdq_s = [w_bf[:, 8192 + 2816 * p:8192 + 2816 * (p + 1)] for p in range(2)]
    x1stg = C[:, 5632:6144]
    atstg = c_bf[:, 12288:12800]
    outstg = C[:, 6400:6912]
    for tg in range(8):
        t0 = 512 * tg
        def xqld(sy, t0=t0, w=xq_done, mv=plan.val(mmx), vvv=plan.val(vv)):
            sy.wait_ge(plan.sems[ph + "xs"], w)
            sy.wait_ge(plan.sems[mmx], mv)
            sy.wait_ge(plan.sems[vv], vvv)
            return sy.dma_start(out=a_bf[:, 0:4096].rearrange("p (ct k) -> p ct k", ct=8),
                                in_=fm_dram(dram["s_xq"], t0, 512))
        plan.op("sync", xqld, dma=ph + "xg")
        xg = plan.val(ph + "xg")
        ic0 = 0
        for ici, icsz in enumerate(ICS):
            def wsld(g, ici=ici, icsz=icsz, ic0=ic0, mv=plan.val(mmx),
                     pw=plan.val(ph + "wg")if (ph + "wg") in plan.cnt else 0):
                g.wait_ge(plan.sems[mmx], mv)
                if pw:
                    g.wait_ge(plan.sems[ph + "wg"], pw)
                g.dma_start(out=wgq_s.rearrange("p (ct k) -> p ct k", ct=8)[:, :, 0:icsz],
                            in_=dram["s_wgq"].rearrange("(ct p) i -> p ct i", p=128)[:, :, ic0:ic0 + icsz])
                return g.dma_start(out=wuq_s.rearrange("p (ct k) -> p ct k", ct=8)[:, :, 0:icsz],
                                   in_=dram["s_wuq"].rearrange("(ct p) i -> p ct i", p=128)[:, :, ic0:ic0 + icsz])
            plan.sem(ph + "wg")
            plan.op("gpsimd", wsld, dma=ph + "wg")
            wgl = plan.val(ph + "wg")
            for tt in range(4):
                for uu in range(2):
                    wsl = wgq_s if uu == 0 else wuq_s
                    psb, bfree, bid = takeb()
                    def gumm(t, tt=tt, uu=uu, wsl=wsl, icsz=icsz, psb=psb, bfree=bfree,
                             wgl=wgl, xg=xg):
                        t.wait_ge(plan.sems[ph + "wg"], wgl)
                        t.wait_ge(plan.sems[ph + "xg"], xg)
                        t.wait_ge(plan.sems[bfree[0]], bfree[1])
                        for ci in range(8):
                            ins = t.matmul(psb[:, 0:icsz],
                                           xq_t[ci][:, 128 * tt:128 * (tt + 1)],
                                           wsl[:, 512 * ci:512 * ci + icsz],
                                           start=(ci == 0), stop=(ci == 7))
                        return ins
                    plan.op("tensor", gumm, inc=(mmx, 1))
                    if uu == 0:
                        def gev(s, psb=psb, icsz=icsz, mw=plan.val(mmx), pv=plan.val(vv)):
                            s.wait_ge(plan.sems[mmx], mw)
                            s.wait_ge(plan.sems[vv], pv)
                            return s.activation(gst[:, 0:icsz], psb[:, 0:icsz], AF.Silu)
                        plan.op("scalar", gev)
                        bank_free[bid] = ("sc_p", plan.val(ss))
                    else:
                        def uev(v, psb=psb, icsz=icsz, mw=plan.val(mmx), sw=plan.val(ss)):
                            v.wait_ge(plan.sems[mmx], mw)
                            v.wait_ge(plan.sems[ss], sw)
                            return v.tensor_copy(ust[:, 0:icsz], psb[:, 0:icsz])
                        plan.op("vector", uev)
                        def hset(v, tt=tt, ic0=ic0, icsz=icsz):
                            return v.tensor_mul(h_t[tt][:, ic0:ic0 + icsz],
                                                gst[:, 0:icsz], ust[:, 0:icsz])
                        plan.op("vector", hset)
                        bank_free[bid] = ("ve_p", plan.val(vv))
            ic0 += icsz
        for tt in range(4):
            htok = A[:, 0:2816]
            def hcp(v, tt=tt, w=plan.val(ss)):
                v.wait_ge(plan.sems[ss], w)
                return v.tensor_copy(htok[:, 0:2816], h_t[tt])
            plan.op("vector", hcp)
            hq_bf = a_bf[:, 16384:16384 + 2816]
            for cb in range(0, 2816, 704):
                qdq(htok[:, cb:cb + 704], hq_bf[:, cb:cb + 704], 9728, 704)
            hq_done = plan.val(vv)
            for half in range(2):
                def htr(t, tt=tt, half=half, w=hq_done, pv=plan.val(vv)):
                    t.wait_ge(plan.sems[vv], max(w, pv))
                    for bj in range(11):
                        ins = t.transpose(psb_t[:, 128 * bj:128 * (bj + 1)],
                                          hq_bf[:, 1408 * half + 128 * bj:1408 * half + 128 * (bj + 1)],
                                          cst["ident"])
                    return ins
                plan.op("tensor", htr, inc=(mmx, 1))
                def hev(v, tt=tt, half=half, mw=plan.val(mmx)):
                    v.wait_ge(plan.sems[mmx], mw)
                    for bj in range(11):
                        ins = v.tensor_copy(hqT_t[11 * half + bj][:, 128 * tt:128 * (tt + 1)],
                                            psb_t[:, 128 * bj:128 * (bj + 1)])
                    return ins
                plan.op("vector", hev)
        hq_all = plan.val(vv)
        for dt in range(8):
            wdp = dt % 2
            def wdld(g, dt=dt, wdp=wdp, mv=plan.val(mmx),
                     pw=plan.val(ph + "wd" + str(wdp)) if (ph + "wd" + str(wdp)) in plan.cnt else 0):
                g.wait_ge(plan.sems[mmx], mv)
                if pw:
                    g.wait_ge(plan.sems[ph + "wd" + str(wdp)], pw)
                return g.dma_start(
                    out=wdq_s[wdp].rearrange("p (it k) -> p it k", it=22),
                    in_=dram["s_wdq"].rearrange("(it p) d -> p it d", p=128)[:, :, 128 * dt:128 * (dt + 1)])
            plan.sem(ph + "wd" + str(wdp))
            plan.op("gpsimd", wdld, dma=ph + "wd" + str(wdp))
            wdl = plan.val(ph + "wd" + str(wdp))
            def x1ld(sy, dt=dt, t0=t0, vvv=plan.val(vv)):
                sy.wait_ge(plan.sems[vv], vvv)
                sy.dma_start(out=x1stg, in_=dram["s_x1"].rearrange("(q p) t -> q p t", p=128)[dt, :, t0:t0 + 512])
                return sy.dma_start(out=atstg, in_=dram["s_attn1"].rearrange("(q p) t -> q p t", p=128)[dt, :, t0:t0 + 512])
            plan.op("sync", x1ld, dma=ph + "r")
            rl_ = plan.val(ph + "r")
            psb, bfree, bid = takeb()
            def dmm(t, dt=dt, wdp=wdp, psb=psb, bfree=bfree, wdl=wdl, hq=hq_all):
                t.wait_ge(plan.sems[ph + "wd" + str(wdp)], wdl)
                t.wait_ge(plan.sems[vv], hq)
                t.wait_ge(plan.sems[bfree[0]], bfree[1])
                for it in range(22):
                    ins = t.matmul(psb[:, :], wdq_s[wdp][:, 128 * it:128 * (it + 1)],
                                   hqT_t[it], start=(it == 0), stop=(it == 21))
                return ins
            plan.op("tensor", dmm, inc=(mmx, 1))
            def fev(v, dt=dt, psb=psb, mw=plan.val(mmx), rw=rl_,
                    pst=plan.val(ph + "o") if (ph + "o") in plan.cnt else 0):
                v.wait_ge(plan.sems[mmx], mw)
                v.wait_ge(plan.sems[ph + "r"], rw)
                if pst:
                    v.wait_ge(plan.sems[ph + "o"], pst)
                v.tensor_add(outstg, psb[:, :], x1stg)
                return v.tensor_add(outstg, outstg, atstg)
            plan.sem(ph + "o")
            plan.op("vector", fev)
            bank_free[bid] = ("ve_p", plan.val(vv))
            plan.op("sync", lambda sy, dt=dt, t0=t0, w=plan.val(vv): (
                sy.wait_ge(plan.sems[vv], w),
                sy.dma_start(out=dram["outT"].rearrange("(q p) t -> q p t", p=128)[dt, :, t0:t0 + 512],
                             in_=outstg))[1], dma=ph + "o")
    return ph + "o", plan.val(ph + "o")



def emit_kgather(plan, dram, wait):
    rs = plan.sem("kbc")
    sems = plan.sems

    def go(g):
        for s, v in wait:
            g.wait_ge(sems[s], v)
        g.collective_compute(
            "AllGather", mybir.AluOpType.bypass,
            replica_groups=[[0, 1, 2, 3, 4, 5, 6, 7]],
            ins=[dram["s_kslice"]], outs=[dram["s_kag"]],
        ).then_inc(rs, 1)
    plan.emit("gpsimd", go)
    plan.cnt["kbc"] += 1
    plan.dma_sems.add("kbc")
    return "kbc", 1


def barrier(plan):
    return [(s, c) for s, c in plan.cnt.items() if c > 0]



SCRATCH_SHAPES = {
    "s_xn": ([D, T], BF16), "s_xnT": ([T, D], BF16),
    "s_prm": ([4, 8192], F32), "s_kslice": ([L, HSL], BF16),
    "s_kag": ([8 * L, HSL], BF16),
    "s_uf": ([2, 2, NFP, D], BF16), "s_kf": ([2, NFP, D], BF16),
    "s_g": ([2, L, D], BF16), "s_glu": ([D, T], BF16),
    "s_attn0": ([D, T], BF16),
    "s_cwT": ([E, T], F32), "s_x1": ([D, T], F32),
    "s_xn1": ([D, T], BF16),
    "s_yatt": ([D, T], BF16), "s_attn1": ([D, T], BF16),
    "s_wgq": ([D, FFI], BF16), "s_wuq": ([D, FFI], BF16), "s_wdq": ([FFI, D], BF16),
    "s_xq": ([D, T], BF16),
    "outT": ([D, T], F32),
}


def _scr(names, outs):
    return {n: (SCRATCH_SHAPES[n][0], SCRATCH_SHAPES[n][1],
                "ExternalOutput" if n in outs else
                ("ExternalInput" if n.startswith("s_") and n in names and n in names else "Internal"))
            for n in names}


def _mk_scratch(ins, internal, outs):
    d = {}
    for n in ins:
        d[n] = (SCRATCH_SHAPES[n][0], SCRATCH_SHAPES[n][1], "ExternalInput")
    for n in internal:
        d[n] = (SCRATCH_SHAPES[n][0], SCRATCH_SHAPES[n][1], "Internal")
    for n in outs:
        d[n] = (SCRATCH_SHAPES[n][0], SCRATCH_SHAPES[n][1], "ExternalOutput")
    return d


def kernel(**inputs):
    in_maps = host_prepare(inputs)
    EI = "ExternalInput"

    def fused(plan, bufs, dram):
        emit_ln(plan, bufs, "ln1", dram["x0T"], dram["s_xn"], dstT=dram["s_xnT"],
                wait_st=plan.cst_wait)
        ln1_st = ("ln1st", plan.val("ln1st"))
        kst, kstv = emit_ssm_k(plan, bufs, "k", dram)
        emit_kgather(plan, dram, wait=[(kst, kstv)])
        b1 = [ln1_st, ("kbc", 1)]
        s_lns, v_lns = emit_l0attn(plan, bufs, dram, deps=b1)
        b2 = [(s_lns, v_lns)]
        emit_moe(plan, bufs, "mo", dram, deps=b2)
        b3 = [("most", plan.val("most"))]
        emit_ln(plan, bufs, "ln2", dram["s_x1"], dram["s_xn1"], wait_st=b3)
        b4 = [("ln2st", plan.val("ln2st"))]
        emit_attn(plan, bufs, "at", dram, deps=b4)
        b5 = [("ato", plan.val("ato")), ("ln2st", plan.val("ln2st"))]
        emit_fp4(plan, bufs, "f4", dram, deps=b5)

    io_kinds = {n: EI for n in input_specs() if not n.startswith("s_")}
    internal = [n for n in SCRATCH_SHAPES if n != "outT"]
    nc = build([fused], io_kinds,
               extra_scratch=_mk_scratch([], internal, ["outT"]))
    r = run_bass_kernel_spmd(nc, in_maps, core_ids=list(range(NCORES)))
    out = np.empty((16, 2048, 1024), dtype=np.float32)
    for c in range(NCORES):
        out[2 * c:2 * c + 2] = np.asarray(r.results[c]["outT"]).T.reshape(2, L, D)
    return out


# version 09
# speedup vs baseline: 1.1367x; 1.1367x over previous
import math
import contextlib
import numpy as np
import ml_dtypes
import concourse.bass as bass
import concourse.mybir as mybir
from concourse.bass_utils import run_bass_kernel_spmd

F32 = mybir.dt.float32
BF16 = mybir.dt.bfloat16
I32 = mybir.dt.int32
AF = mybir.ActivationFunctionType
ALU = mybir.AluOpType
AX = mybir.AxisListType.X

NCORES = 8
D = 1024
DT = 8
B = 2
L = 2048
T = B * L
NF = 2049
NFP = 2176
N2 = 4096
SN = 64
HSL = 128
E = 8
I_MOE = 4096
SH_I = 2048
NH = 16
DH = 64
FFI = 2816
EPS = 1e-5

GRID_C = [0.5, 0.5, 0.5, 0.5, 1.0, 1.0, 2.0]
MID = [0.25, 0.75, 1.25, 1.75, 2.5, 3.5, 5.0]



ENGINES = ("tensor", "vector", "scalar", "sync", "gpsimd")


class MockIns:
    def then_inc(self, sem, k):
        return self


class MockEng:

    def __init__(self):
        self.n = 0
        self.ndma = 0

    def wait_ge(self, *a, **k):
        pass

    def dma_start(self, *a, **k):
        self.ndma += 1
        return MockIns()

    def indirect_dma_start(self, *a, **k):
        self.ndma += 1
        return MockIns()

    def __getattr__(self, name):
        def f(*a, **k):
            self.n += 1
            return MockIns()
        return f


CHAIN_SEM = {"vector": "ve_p", "scalar": "sc_p", "gpsimd": "gp_p"}
CHAIN_OPS_EXCLUDE = {"wait_ge", "dma_start", "indirect_dma_start", "then_inc"}


class ChainProxy:

    def __init__(self, eng, sem, start, dma_sem=None):
        self._eng = eng
        self._sem = sem
        self._cnt = start
        self._dma = dma_sem

    def wait_ge(self, sem, v):
        return self._eng.wait_ge(sem, v)

    def dma_start(self, *a, **k):
        ins = self._eng.dma_start(*a, **k)
        if self._dma is not None:
            ins.then_inc(self._dma, 16)
        return ins

    def indirect_dma_start(self, *a, **k):
        ins = self._eng.indirect_dma_start(*a, **k)
        if self._dma is not None:
            ins.then_inc(self._dma, 16)
        return ins

    def __getattr__(self, name):
        f = getattr(self._eng, name)
        if name.startswith("_") or name in CHAIN_OPS_EXCLUDE or not callable(f):
            return f

        def wrapped(*a, **k):
            if self._sem is None:
                return f(*a, **k)
            if self._cnt > 0:
                self._eng.wait_ge(self._sem, self._cnt)
            ins = f(*a, **k)
            ins.then_inc(self._sem, 1)
            self._cnt += 1
            return ins
        return wrapped


class Plan:

    def __init__(self, nc, es):
        self.nc = nc
        self.es = es
        self.ops = {e: [] for e in ENGINES}
        self.sems = {}
        self.cnt = {}
        self.dma_sems = set()
        for s in CHAIN_SEM.values():
            self.sem(s)

    def sem(self, name):
        if name not in self.sems:
            self.sems[name] = self.es.enter_context(self.nc.semaphore(name))
            self.cnt[name] = 0
        return self.sems[name]

    def emit(self, eng, fn):
        self.ops[eng].append(fn)

    def inc(self, name, k=1):
        self.sem(name)
        self.cnt[name] += k
        return self.cnt[name]

    def val(self, name):
        self.sem(name)
        return self.cnt[name]

    def op(self, eng, fn, inc=None, wait=None, dma=None):
        sems = self.sems
        mock = MockEng()
        fn(mock)
        chain = CHAIN_SEM.get(eng)
        base = None
        if chain is not None:
            base = self.cnt[chain]
            self.cnt[chain] += mock.n
            if inc is not None and mock.n > 0:
                raise RuntimeError("no explicit inc on chained compute closures")
        if mock.ndma > 0:
            assert dma is not None and inc is None, "DMA closures need dma= sem"
            self.inc(dma, 16 * mock.ndma)
            self.dma_sems.add(dma)
        if inc is not None:
            self.inc(inc[0], inc[1])
        wl = list(wait or [])
        dsem = self.sem(dma) if dma is not None else None

        nexp = mock.n

        def run(ex, fn=fn, wl=wl, inc=inc, chain=chain, base=base, dsem=dsem, nexp=nexp):
            px = ChainProxy(ex, sems.get(chain), base or 0, dma_sem=dsem)
            for s, v in wl:
                px.wait_ge(sems[s], v)
            ins = fn(px)
            if chain is not None:
                assert px._cnt == (base or 0) + nexp, \
                    f"op count mismatch: expected {nexp}, got {px._cnt - (base or 0)} at {fn}"
            if inc is not None:
                ins.then_inc(sems[inc[0]], inc[1])
        self.emit(eng, run)

    def finalize(self, block):
        for eng in ENGINES:
            ops = self.ops[eng]
            if not ops:
                continue
            dec = getattr(block, eng)

            def section(ex, ops=ops):
                for fn in ops:
                    fn(ex)
            dec(section)


def cdiv(a, b):
    return (a + b - 1) // b



def _bf(x):
    return np.ascontiguousarray(x.astype(ml_dtypes.bfloat16))


def _f32(x):
    return np.ascontiguousarray(x, dtype=np.float32)


def host_consts():
    c = {}
    ll = np.arange(L, dtype=np.float64)
    ff = np.arange(NF, dtype=np.float64)
    th = 2.0 * np.pi * np.outer(ll, ff) / N2
    twc = np.zeros((L, NFP)); twc[:, :NF] = np.cos(th)
    tws = np.zeros((L, NFP)); tws[:, :NF] = -np.sin(th)
    c["twc_t"] = _bf(twc.reshape(L, 17, 128).transpose(1, 0, 2))
    c["tws_t"] = _bf(tws.reshape(L, 17, 128).transpose(1, 0, 2))
    a = np.full(NF, 2.0 / N2); a[0] = 1.0 / N2; a[-1] = 1.0 / N2
    thi = th.T
    twic = np.zeros((NFP, L)); twic[:NF] = np.cos(thi) * a[:, None]
    twis = np.zeros((NFP, L)); twis[:NF] = -np.sin(thi) * a[:, None]
    c["twic_t"] = _bf(twic.reshape(NFP, 16, 128).transpose(1, 0, 2))
    c["twis_t"] = _bf(twis.reshape(NFP, 16, 128).transpose(1, 0, 2))
    c["ident"] = _bf(np.eye(128))
    c["ident_f"] = _f32(np.eye(128))
    c["ones_col_f"] = _f32(np.ones((128, 1)))
    c["ones_col_b"] = _bf(np.ones((128, 1)))
    c["ones_row_f"] = _f32(np.ones((1, 128)))
    c["ones_row_b"] = _bf(np.ones((1, 128)))
    c["iota_col"] = _f32(np.arange(128, dtype=np.float32).reshape(128, 1))
    s_i = np.arange(128)[:, None]
    q_i = np.arange(512)[None, :]
    for j in range(4):
        c[f"mask{j}"] = _bf((s_i <= q_i - 128 * j).astype(np.float32))
    return c


def host_prepare(inputs):
    consts = host_consts()
    x = np.asarray(inputs["L0_attn_x"])
    shared = {}
    shared.update(consts)
    shared["ssm_D_row"] = _f32(np.asarray(inputs["L0_attn_D"]).reshape(1, D))
    shared["ssm_gamma"] = _f32(np.asarray(inputs["L0_attn_gamma"]).reshape(D, 1))
    shared["ssm_beta"] = _f32(np.asarray(inputs["L0_attn_beta"]).reshape(D, 1))
    shared["w_out"] = _f32(np.asarray(inputs["L0_attn_W_out"]))
    shared["b_out"] = _f32(np.asarray(inputs["L0_attn_b_out"]).reshape(2 * D, 1))
    shared["gate_wT"] = _f32(np.asarray(inputs["L0_ffn_gate_w"]).T)
    shared["moe_wg"] = _f32(np.asarray(inputs["L0_ffn_wg"]))
    shared["moe_wu"] = _f32(np.asarray(inputs["L0_ffn_wu"]))
    shared["moe_wd"] = _f32(np.asarray(inputs["L0_ffn_wd"]))
    shared["ws_g"] = _f32(np.asarray(inputs["L0_ffn_ws_g"]))
    shared["ws_u"] = _f32(np.asarray(inputs["L0_ffn_ws_u"]))
    shared["ws_d"] = _f32(np.asarray(inputs["L0_ffn_ws_d"]))
    shared["wqT"] = _f32(np.asarray(inputs["L1_attn_w_q"]).reshape(D, D).T)
    shared["wkT"] = _f32(np.asarray(inputs["L1_attn_w_k"]).reshape(D, D).T)
    shared["wvT"] = _f32(np.asarray(inputs["L1_attn_w_v"]).reshape(D, D).T)
    shared["w_o"] = _f32(np.asarray(inputs["L1_attn_w_o"]))
    shared["w_gate"] = _f32(np.asarray(inputs["L1_ffn_w_gate"]))
    shared["w_up"] = _f32(np.asarray(inputs["L1_ffn_w_up"]))
    shared["w_down"] = _f32(np.asarray(inputs["L1_ffn_w_down"]))

    A_re = np.asarray(inputs["L0_attn_A_re"]); A_im = np.asarray(inputs["L0_attn_A_im"])
    C_re = np.asarray(inputs["L0_attn_C_re"]); C_im = np.asarray(inputs["L0_attn_C_im"])
    log_dt = np.asarray(inputs["L0_attn_log_dt"])

    in_maps = []
    for c in range(NCORES):
        m = dict(shared)
        xc = x[2 * c:2 * c + 2].reshape(T, D).T
        m["x0T"] = _f32(xc)
        hs = slice(HSL * c, HSL * (c + 1))
        m["ssm_A_re"] = _f32(A_re[hs]); m["ssm_A_im"] = _f32(A_im[hs])
        m["ssm_C_re"] = _f32(C_re[hs]); m["ssm_C_im"] = _f32(C_im[hs])
        m["ssm_log_dt"] = _f32(log_dt[hs].reshape(HSL, 1))
        in_maps.append(m)
    return in_maps


def input_specs():
    sp = {
        "x0T": ([D, T], F32),
        "twc_t": ([17, L, 128], BF16), "tws_t": ([17, L, 128], BF16),
        "twic_t": ([16, NFP, 128], BF16), "twis_t": ([16, NFP, 128], BF16),
        "ident": ([128, 128], BF16), "ident_f": ([128, 128], F32),
        "ones_col_f": ([128, 1], F32), "ones_col_b": ([128, 1], BF16),
        "ones_row_f": ([1, 128], F32), "ones_row_b": ([1, 128], BF16),
        "iota_col": ([128, 1], F32),
        "mask0": ([128, 512], BF16), "mask1": ([128, 512], BF16),
        "mask2": ([128, 512], BF16), "mask3": ([128, 512], BF16),
        "ssm_A_re": ([HSL, SN], F32), "ssm_A_im": ([HSL, SN], F32),
        "ssm_C_re": ([HSL, SN], F32), "ssm_C_im": ([HSL, SN], F32),
        "ssm_log_dt": ([HSL, 1], F32),
        "ssm_D_row": ([1, D], F32),
        "ssm_gamma": ([D, 1], F32), "ssm_beta": ([D, 1], F32),
        "w_out": ([D, 2 * D], F32), "b_out": ([2 * D, 1], F32),
        "gate_wT": ([D, E], F32),
        "moe_wg": ([E, D, I_MOE], F32), "moe_wu": ([E, D, I_MOE], F32),
        "moe_wd": ([E, I_MOE, D], F32),
        "ws_g": ([D, SH_I], F32), "ws_u": ([D, SH_I], F32), "ws_d": ([SH_I, D], F32),
        "wqT": ([D, D], F32), "wkT": ([D, D], F32), "wvT": ([D, D], F32),
        "w_o": ([D, D], F32),
        "w_gate": ([FFI, D], F32), "w_up": ([FFI, D], F32), "w_down": ([D, FFI], F32),
    }
    return sp



class Bufs:
    def __init__(self, nc, es):
        self.nc = nc
        self.AR_A = es.enter_context(nc.sbuf_tensor("AR_A", [128, 16384], F32))
        self.AR_B = es.enter_context(nc.sbuf_tensor("AR_B", [128, 16384], F32))
        self.AR_C = es.enter_context(nc.sbuf_tensor("AR_C", [128, 8192], F32))
        self.AR_W = es.enter_context(nc.sbuf_tensor("AR_W", [128, 7168], F32))
        self.SM = es.enter_context(nc.sbuf_tensor("SM", [128, 1536], F32))
        self.PS = [es.enter_context(nc.psum_tensor(f"PS{i}", [128, 512], F32))
                   for i in range(6)]
        self.PSW = es.enter_context(nc.psum_tensor("PSW", [128, 1024], F32))

    def a_bf(self):
        return self.AR_A[:, :].bitcast(BF16)

    def b_bf(self):
        return self.AR_B[:, :].bitcast(BF16)

    def c_bf(self):
        return self.AR_C[:, :].bitcast(BF16)

    def w_bf(self):
        return self.AR_W[:, :].bitcast(BF16)


def fm_dram(ap, tc0, tn, p=128):
    d = ap.shape[0]
    return ap.rearrange("(dt p) t -> p dt t", p=p)[:, :, tc0:tc0 + tn]



def emit_ln(plan, bufs, ph, src, dst, dstT=None, gamma=None, beta=None,
            src_f32=True, wait_st=None):
    nc = plan.nc
    cst = plan.consts
    CH = 512
    nch = T // CH
    A, Bb = bufs.AR_A, bufs.AR_B
    a_bf = bufs.a_bf(); b_bf = bufs.b_bf()
    PS_SUM, PS_SQ, PS_R, PS_MR = bufs.PS[0], bufs.PS[1], bufs.PS[2], bufs.PS[3]
    SM = bufs.SM
    ld, vv, ss, mmx, stx = ph + "ld", "ve_p", "sc_p", ph + "mm", ph + "st"
    for s in (ld, mmx, stx):
        plan.sem(s)

    deps = list(wait_st or [])
    if gamma is not None:
        plan.op("sync", lambda sy, g=gamma: sy.dma_start(
            out=SM[:, 8:16], in_=g.rearrange("(dt p) o -> p (dt o)", p=128),
            allow_slow_non_contiguous=True), dma=ld, wait=deps)
        plan.op("sync", lambda sy, b=beta: sy.dma_start(
            out=SM[:, 16:24], in_=b.rearrange("(dt p) o -> p (dt o)", p=128),
            allow_slow_non_contiguous=True), dma=ld)

    prev_norm = 0
    prev_statfin = 0
    prev_store = 0
    prev_evac = 0
    C = bufs.AR_C
    mean = C[0:1, 1024:1536]
    ex2 = C[0:1, 1536:2048]
    rstd = C[0:1, 2048:2560]
    xa = A[:, 0:4096]
    xav = xa.rearrange("p (dt t) -> p dt t", dt=8)
    xnreg = b_bf[:, 8192:8192 + 4096]

    for c in range(nch):
        t0 = c * CH
        w = deps + ([] if c == 0 else [(vv, prev_norm)])
        if src_f32:
            plan.op("sync", lambda sy, t0=t0: sy.dma_start(
                out=xav, in_=fm_dram(src, t0, CH)), dma=ld, wait=w)
            ld_x = plan.val(ld)
            v_pre = plan.val(vv)
        else:
            xab = a_bf[:, 8192:8192 + 4096].rearrange("p (dt t) -> p dt t", dt=8)
            plan.op("sync", lambda sy, t0=t0, xab=xab: sy.dma_start(
                out=xab, in_=fm_dram(src, t0, CH)), dma=ld, wait=w)
            ld_x = plan.val(ld)
            plan.op("vector", lambda v, xab=xab: v.tensor_copy(xav, xab),
                    wait=[(ld, ld_x)])
            v_pre = plan.val(vv)
        plan.op("scalar", lambda s: s.activation(Bb[:, 0:4096], xa, AF.Square),
                wait=deps + [(ld, ld_x), (vv, v_pre)])
        sq_done = plan.val(ss)
        def stats(t, ld_x=ld_x, v_pre=v_pre, sq_done=sq_done, pf=prev_statfin, dp=list(deps)):
            for s, v in dp:
                t.wait_ge(plan.sems[s], v)
            t.wait_ge(plan.sems[ld], ld_x)
            t.wait_ge(plan.sems[vv], max(v_pre, pf))
            for i in range(8):
                t.matmul(PS_SUM[0:1, :], cst["ones_col_f"], A[:, 512 * i:512 * (i + 1)],
                         start=(i == 0), stop=(i == 7))
            t.wait_ge(plan.sems[ss], sq_done)
            for i in range(8):
                ins = t.matmul(PS_SQ[0:1, :], cst["ones_col_f"], Bb[:, 512 * i:512 * (i + 1)],
                               start=(i == 0), stop=(i == 7))
            return ins
        plan.op("tensor", stats, inc=(mmx, 1))
        mm_stats = plan.val(mmx)
        def statfin_v(v, mm_stats=mm_stats):
            v.wait_ge(plan.sems[mmx], mm_stats)
            v.tensor_scalar(mean, PS_SUM[0:1, :], 1.0 / D, None, ALU.mult)
            v.tensor_scalar(ex2, PS_SQ[0:1, :], 1.0 / D, None, ALU.mult)
            v.tensor_mul(rstd, mean, mean)
            return v.tensor_sub(ex2, ex2, rstd)
        plan.op("vector", statfin_v)
        def statfin_s(s, vw=plan.val(vv)):
            s.wait_ge(plan.sems[vv], vw)
            s.activation(rstd, ex2, AF.Ln, bias=EPS)
            return s.activation(rstd, rstd, AF.Exp, scale=-0.5)
        plan.op("scalar", statfin_s)
        def statfin2_v(v, sw=plan.val(ss)):
            v.wait_ge(plan.sems[ss], sw)
            v.tensor_mul(mean, mean, rstd)
            return v.tensor_scalar(mean, mean, -1.0, None, ALU.mult)
        plan.op("vector", statfin2_v)
        prev_statfin = plan.val(vv)
        def bcast(t, vw=prev_statfin, pn=prev_norm):
            t.wait_ge(plan.sems[vv], max(vw, pn))
            t.matmul(PS_R[:, :], cst["ones_row_f"], rstd, start=True, stop=True)
            return t.matmul(PS_MR[:, :], cst["ones_row_f"], mean, start=True, stop=True)
        plan.op("tensor", bcast, inc=(mmx, 1))
        mm_b = plan.val(mmx)
        def norm(v, mm_b=mm_b, ps=prev_store):
            v.wait_ge(plan.sems[mmx], mm_b)
            if ps:
                v.wait_ge(plan.sems[stx], ps)
            for i in range(8):
                xi = A[:, 512 * i:512 * (i + 1)]
                ti = Bb[:, 512 * i:512 * (i + 1)]
                xo = xnreg[:, 512 * i:512 * (i + 1)]
                v.tensor_mul(ti, xi, PS_R[:, :])
                if gamma is None:
                    ins = v.tensor_add(xo, ti, PS_MR[:, :])
                else:
                    ti2 = bufs.AR_C[:, 0:512]
                    v.tensor_add(ti2, ti, PS_MR[:, :])
                    v.tensor_scalar(ti, ti2, SM[:, 8 + i:9 + i], None, ALU.mult)
                    ins = v.tensor_scalar(xo, ti, SM[:, 16 + i:17 + i], None, ALU.add)
            return ins
        plan.op("vector", norm)
        prev_norm = plan.val(vv)
        plan.op("sync", lambda sy, t0=t0: sy.dma_start(
            out=fm_dram(dst, t0, CH),
            in_=xnreg.rearrange("p (dt t) -> p dt t", dt=8)),
            wait=[(vv, prev_norm)], dma=stx)
        if dstT is not None:
            psb = bufs.PSW[:, :].bitcast(BF16)
            for j in range(4):
                def trans(t, j=j, nw=prev_norm, pe=prev_evac):
                    t.wait_ge(plan.sems[vv], max(nw, pe))
                    for i in range(8):
                        ins = t.transpose(psb[:, 128 * i:128 * (i + 1)],
                                          xnreg[:, 512 * i + 128 * j:512 * i + 128 * (j + 1)],
                                          cst["ident"])
                    return ins
                plan.op("tensor", trans, inc=(mmx, 1))
                treg = b_bf[:, 12288 + 1024 * j:12288 + 1024 * (j + 1)]
                def evac(v, mw=plan.val(mmx), j=j, pst=prev_store, treg=treg):
                    v.wait_ge(plan.sems[mmx], mw)
                    if pst:
                        v.wait_ge(plan.sems[stx], pst)
                    return v.tensor_copy(treg, psb[:, 0:1024])
                plan.op("vector", evac)
                prev_evac = plan.val(vv)
                plan.op("sync", lambda sy, t0=t0, j=j, treg=treg: sy.dma_start(
                    out=dstT[t0 + 128 * j:t0 + 128 * (j + 1), :], in_=treg),
                    wait=[(vv, prev_evac)], dma=stx)
        prev_store = plan.val(stx)
    return stx, plan.val(stx)



CONST_NAMES = ["ident", "ident_f", "ones_col_f", "ones_col_b", "ones_row_f", "ones_row_b",
               "iota_col", "mask0", "mask1", "mask2", "mask3"]

FLOAT_CONSTS = [EPS]


def load_consts(plan, es, dram):
    nc = plan.nc
    shapes = input_specs()
    plan.consts = {}
    for n in CONST_NAMES:
        sh, dt = shapes[n]
        t = es.enter_context(nc.sbuf_tensor("cst_" + n, sh, dt))
        plan.consts[n] = t[:, :]
        plan.op("gpsimd", lambda g, n=n, t=t, sh=sh: g.dma_start(
            out=t[0:sh[0], 0:sh[1]], in_=dram[n]), dma="cstld")
    for i, v in enumerate(FLOAT_CONSTS):
        t = es.enter_context(nc.sbuf_tensor(f"fc_{i}", [128, 1], F32))
        plan.op("gpsimd", lambda g, t=t, v=v: g.memset(t[:, :], v))
        nc.const_aps.register(F32, v, t[:, :])
    return [("cstld", plan.val("cstld")), ("gp_p", plan.val("gp_p"))]



def build(phase_fns, io_kinds, extra_scratch=None):
    nc = bass.Bass("TRN2", target_bir_lowering=False)
    specs = input_specs()
    io_kinds = dict(io_kinds)
    for n in CONST_NAMES:
        io_kinds.setdefault(n, "ExternalInput")
    es = contextlib.ExitStack()
    with es:
        dram = {}
        for n, kind in io_kinds.items():
            sh, dt = specs[n]
            dram[n] = nc.dram_tensor(n, sh, dt, kind=kind).ap()
        for n, (sh, dt, kind) in (extra_scratch or {}).items():
            dram[n] = nc.dram_tensor(n, sh, dt, kind=kind).ap()
        bufs = Bufs(nc, es)
        plan = Plan(nc, es)
        cw = load_consts(plan, es, dram)
        plan.cst_wait = cw
        for fn in phase_fns:
            fn(plan, bufs, dram)
        final = [(s, plan.cnt[s]) for s in sorted(plan.dma_sems) if plan.cnt[s] > 0]
        def fin(sy):
            for s, v in final:
                sy.wait_ge(plan.sems[s], v)
        plan.emit("sync", fin)
        block = es.enter_context(nc.Block())
        plan.finalize(block)
    return nc



TWO_PI = float(2.0 * np.pi)
INV_2PI = float(1.0 / (2.0 * np.pi))
HALF_PI = float(np.pi / 2.0)



def emit_rr(v, theta, out, iv, s1, s2):
    v.tensor_scalar(out, theta, INV_2PI, None, ALU.mult)
    v.tensor_copy(iv, out)
    v.tensor_copy(s1, iv)
    v.tensor_sub(s2, out, s1)
    v.tensor_scalar(out, s2, 0.5, None, ALU.is_gt)
    v.tensor_scalar(s2, s2, -0.5, None, ALU.is_lt)
    v.tensor_sub(out, out, s2)
    v.tensor_add(s1, s1, out)
    v.scalar_tensor_tensor(out, s1, -TWO_PI, theta, ALU.mult, ALU.add)
    return v.tensor_scalar(out, out, float(np.pi), -float(np.pi), ALU.min, ALU.max)


def emit_ssm_k(plan, bufs, ph, dram):
    cst = plan.consts
    SM, A = bufs.SM, bufs.AR_A
    b_bf = bufs.b_bf()
    a_bf = bufs.a_bf()
    ld, vv, ss, mmx, stx = ph + "ld", "ve_p", "sc_p", ph + "mm", ph + "st"
    for s in (ld, mmx, stx):
        plan.sem(s)
    cols = {"A_re": 0, "A_im": 64, "C_re": 128, "C_im": 192}
    for nm, c0 in cols.items():
        plan.op("sync", lambda sy, nm=nm, c0=c0: sy.dma_start(
            out=SM[:, c0:c0 + 64], in_=dram["ssm_" + nm]), dma=ld,
            wait=plan.cst_wait)
    plan.op("sync", lambda sy: sy.dma_start(
        out=SM[:, 256:257], in_=dram["ssm_log_dt"]), dma=ld)
    ldv = plan.val(ld)
    Are, Aim = SM[:, 0:64], SM[:, 64:128]
    Cre, Cim = SM[:, 128:192], SM[:, 192:256]
    logdt, dt = SM[:, 256:257], SM[:, 257:258]
    rho, phi = SM[:, 320:384], SM[:, 384:448]
    ctre, ctim = SM[:, 448:512], SM[:, 512:576]
    t1, t2, t3, t4 = SM[:, 576:640], SM[:, 640:704], SM[:, 704:768], SM[:, 768:832]
    er, s_, c_ = SM[:, 832:896], SM[:, 896:960], SM[:, 960:1024]
    iv64 = SM[:, 1024:1088].bitcast(I32)

    plan.op("scalar", lambda s: s.activation(dt, logdt, AF.Exp),
            wait=[(ld, ldv)])
    def vstep1(v, sw=plan.val(ss)):
        v.wait_ge(plan.sems[ss], sw)
        v.tensor_scalar(rho, Are, dt, None, ALU.mult)
        return v.tensor_scalar(phi, Aim, dt, None, ALU.mult)
    plan.op("vector", vstep1)
    def vstep2(v, w=plan.val(vv)):
        v.wait_ge(plan.sems[vv], w)
        return emit_rr(v, phi, t2, iv64, t3, t4)
    plan.op("vector", vstep2)
    plan.op("scalar", lambda s: s.activation(s_, t2, AF.Sin),
            wait=[(vv, plan.val(vv))])
    def vstep3(v, w=plan.val(ss)):
        v.wait_ge(plan.sems[ss], w)
        v.tensor_scalar(t1, phi, HALF_PI, None, ALU.add)
        return emit_rr(v, t1, t3, iv64, t2, t4)
    plan.op("vector", vstep3)
    def sstep2(s, w=plan.val(vv)):
        s.wait_ge(plan.sems[vv], w)
        s.activation(c_, t3, AF.Sin)
        return s.activation(er, rho, AF.Exp)
    plan.op("scalar", sstep2)
    zre, zim = SM[:, 1088:1152], SM[:, 1152:1216]
    def vstep4(v, w=plan.val(ss)):
        v.wait_ge(plan.sems[ss], w)
        v.tensor_mul(zre, er, c_)
        v.tensor_mul(zim, er, s_)
        v.tensor_scalar(t1, zre, -1.0, None, ALU.add)
        v.tensor_copy(t2, zim)
        v.tensor_mul(t3, Are, Are)
        v.tensor_mul(t4, Aim, Aim)
        v.tensor_add(t3, t3, t4)
        v.reciprocal(t3, t3)
        v.tensor_mul(t4, t1, Are)
        v.tensor_mul(er, t2, Aim)
        v.tensor_add(t4, t4, er)
        v.tensor_mul(t4, t4, t3)
        v.tensor_mul(er, t2, Are)
        v.tensor_mul(s_, t1, Aim)
        v.tensor_sub(er, er, s_)
        v.tensor_mul(er, er, t3)
        v.tensor_mul(ctre, Cre, t4)
        v.tensor_mul(s_, Cim, er)
        v.tensor_sub(ctre, ctre, s_)
        v.tensor_scalar(ctre, ctre, 2.0, None, ALU.mult)
        v.tensor_mul(ctim, Cre, er)
        v.tensor_mul(s_, Cim, t4)
        v.tensor_add(ctim, ctim, s_)
        return v.tensor_scalar(ctim, ctim, 2.0, None, ALU.mult)
    plan.op("vector", vstep4)

    NCH = 32
    LC = 64
    Pre = A[:, 0:4096]
    Pim = A[:, 4096:8192]
    Qb = A[:, 8192:12288]
    Tb = A[:, 12288:16384]
    kT = b_bf[:, 18432:20480]
    zc = [(SM[:, 1216:1280], SM[:, 1280:1344]),
          (SM[:, 1344:1408], SM[:, 1408:1472])]
    z64re, z64im = zc[0]

    def v3(apf, l0, ln):
        return apf.rearrange("p (n l) -> p n l", n=SN)[:, :, l0:l0 + ln]

    def bc3(z, ln):
        return z.rearrange("p (n o) -> p n o", o=1).broadcast_to([128, SN, ln])

    def vinit(v):
        v.memset(v3(Pre, 0, 1), 1.0)
        v.memset(v3(Pim, 0, 1), 0.0)
        v.tensor_copy(zc[0][0], zre)
        v.tensor_copy(zc[0][1], zim)
        for k in range(6):
            m = 1 << k
            zkre, zkim = zc[k % 2]
            znre, znim = zc[(k + 1) % 2]
            v.tensor_tensor(v3(Tb, 0, m), v3(Pre, 0, m), bc3(zkre, m), ALU.mult)
            v.tensor_tensor(v3(Qb, 0, m), v3(Pim, 0, m), bc3(zkim, m), ALU.mult)
            v.tensor_sub(v3(Pre, m, m), v3(Tb, 0, m), v3(Qb, 0, m))
            v.tensor_tensor(v3(Tb, 0, m), v3(Pim, 0, m), bc3(zkre, m), ALU.mult)
            v.tensor_tensor(v3(Qb, 0, m), v3(Pre, 0, m), bc3(zkim, m), ALU.mult)
            v.tensor_add(v3(Pim, m, m), v3(Tb, 0, m), v3(Qb, 0, m))
            v.tensor_mul(t1, zkre, zkre)
            v.tensor_mul(t2, zkim, zkim)
            v.tensor_sub(znre, t1, t2)
            v.tensor_mul(t1, zkre, zkim)
            ins = v.tensor_scalar(znim, t1, 2.0, None, ALU.mult)
        return ins
    plan.op("vector", vinit)

    for c in range(NCH):
        def kchunk(v, c=c):
            v.tensor_tensor(Qb.rearrange("p (n l) -> p n l", n=SN),
                            Pre.rearrange("p (n l) -> p n l", n=SN),
                            bc3(ctre, LC), ALU.mult)
            v.tensor_tensor(Tb.rearrange("p (n l) -> p n l", n=SN),
                            Pim.rearrange("p (n l) -> p n l", n=SN),
                            bc3(ctim, LC), ALU.mult)
            v.tensor_sub(Qb, Qb, Tb)
            Qv = Qb.rearrange("p (n l) -> p n l", n=SN)
            h = SN
            while h > 1:
                h //= 2
                v.tensor_add(Qv[:, 0:h, :], Qv[:, 0:h, :], Qv[:, h:2 * h, :])
            ins = v.tensor_copy(kT[:, LC * c:LC * (c + 1)], Qb[:, 0:LC])
            if c == NCH - 1:
                return ins
            v.tensor_tensor(Tb.rearrange("p (n l) -> p n l", n=SN),
                            Pre.rearrange("p (n l) -> p n l", n=SN),
                            bc3(z64re, LC), ALU.mult)
            v.tensor_tensor(Qb.rearrange("p (n l) -> p n l", n=SN),
                            Pim.rearrange("p (n l) -> p n l", n=SN),
                            bc3(z64im, LC), ALU.mult)
            v.tensor_sub(Tb, Tb, Qb)
            v.tensor_tensor(Qb.rearrange("p (n l) -> p n l", n=SN),
                            Pre.rearrange("p (n l) -> p n l", n=SN),
                            bc3(z64im, LC), ALU.mult)
            v.tensor_tensor(Pim.rearrange("p (n l) -> p n l", n=SN),
                            Pim.rearrange("p (n l) -> p n l", n=SN),
                            bc3(z64re, LC), ALU.mult)
            v.tensor_add(Pim, Pim, Qb)
            return v.tensor_copy(Pre, Tb)
        plan.op("vector", kchunk)
    kT_done = plan.val(vv)

    ktile = b_bf[:, 16384:18432]
    psb = bufs.PSW[:, :].bitcast(BF16)
    for g in range(2):
        def ktr(t, g=g, w=kT_done, pv=plan.val(vv)):
            t.wait_ge(plan.sems[vv], max(w, pv))
            for j in range(8):
                ins = t.transpose(psb[:, 128 * j:128 * (j + 1)],
                                  kT[:, 1024 * g + 128 * j:1024 * g + 128 * (j + 1)],
                                  cst["ident"])
            return ins
        plan.op("tensor", ktr, inc=(mmx, 1))
        def kev(v, g=g, mw=plan.val(mmx)):
            v.wait_ge(plan.sems[mmx], mw)
            return v.tensor_copy(ktile[:, 1024 * g:1024 * (g + 1)], psb[:, 0:1024])
        plan.op("vector", kev)
    prev_red = plan.val(vv)
    for lt in range(16):
        plan.op("sync", lambda sy, lt=lt: sy.dma_start(
            out=dram["s_kslice"][128 * lt:128 * (lt + 1), :],
            in_=ktile[:, 128 * lt:128 * (lt + 1)]),
            wait=[(vv, prev_red)], dma=stx)
    return stx, plan.val(stx)


def emit_ssm_k_old(plan, bufs, ph, dram):
    nc = plan.nc
    cst = plan.consts
    SM, A, Bb, C = bufs.SM, bufs.AR_A, bufs.AR_B, bufs.AR_C
    b_bf = bufs.b_bf()
    ld, vv, ss, mmx, stx = ph + "ld", "ve_p", "sc_p", ph + "mm", ph + "st"
    for s in (ld, mmx, stx):
        plan.sem(s)
    cols = {"A_re": 0, "A_im": 64, "C_re": 128, "C_im": 192}
    for nm, c0 in cols.items():
        plan.op("sync", lambda sy, nm=nm, c0=c0: sy.dma_start(
            out=SM[:, c0:c0 + 64], in_=dram["ssm_" + nm]), dma=ld,
            wait=plan.cst_wait)
    plan.op("sync", lambda sy: sy.dma_start(
        out=SM[:, 256:257], in_=dram["ssm_log_dt"]), dma=ld)
    ldv = plan.val(ld)
    Are, Aim = SM[:, 0:64], SM[:, 64:128]
    Cre, Cim = SM[:, 128:192], SM[:, 192:256]
    logdt, dt = SM[:, 256:257], SM[:, 257:258]
    rho, phi = SM[:, 320:384], SM[:, 384:448]
    ctre, ctim = SM[:, 448:512], SM[:, 512:576]
    t1, t2, t3, t4 = SM[:, 576:640], SM[:, 640:704], SM[:, 704:768], SM[:, 768:832]
    er, s_, c_ = SM[:, 832:896], SM[:, 896:960], SM[:, 960:1024]
    iv64 = SM[:, 1024:1088].bitcast(I32)

    plan.op("scalar", lambda s: s.activation(dt, logdt, AF.Exp),
            wait=[(ld, ldv)])
    def vstep1(v, sw=plan.val(ss)):
        v.wait_ge(plan.sems[ss], sw)
        v.tensor_scalar(rho, Are, dt, None, ALU.mult)
        return v.tensor_scalar(phi, Aim, dt, None, ALU.mult)
    plan.op("vector", vstep1)
    def vstep2(v, w=plan.val(vv)):
        v.wait_ge(plan.sems[vv], w)
        return emit_rr(v, phi, t2, iv64, t3, t4)
    plan.op("vector", vstep2)
    plan.op("scalar", lambda s: s.activation(s_, t2, AF.Sin),
            wait=[(vv, plan.val(vv))])
    def vstep3(v, w=plan.val(ss)):
        v.wait_ge(plan.sems[ss], w)
        v.tensor_scalar(t1, phi, HALF_PI, None, ALU.add)
        return emit_rr(v, t1, t3, iv64, t2, t4)
    plan.op("vector", vstep3)
    def sstep2(s, w=plan.val(vv)):
        s.wait_ge(plan.sems[vv], w)
        s.activation(c_, t3, AF.Sin)
        return s.activation(er, rho, AF.Exp)
    plan.op("scalar", sstep2)
    def vstep4(v, w=plan.val(ss)):
        v.wait_ge(plan.sems[ss], w)
        v.tensor_mul(t1, er, c_)
        v.tensor_mul(t2, er, s_)
        v.tensor_scalar(t1, t1, -1.0, None, ALU.add)
        v.tensor_mul(t3, Are, Are)
        v.tensor_mul(t4, Aim, Aim)
        v.tensor_add(t3, t3, t4)
        v.reciprocal(t3, t3)
        v.tensor_mul(t4, t1, Are)
        v.tensor_mul(er, t2, Aim)
        v.tensor_add(t4, t4, er)
        v.tensor_mul(t4, t4, t3)
        v.tensor_mul(er, t2, Are)
        v.tensor_mul(s_, t1, Aim)
        v.tensor_sub(er, er, s_)
        v.tensor_mul(er, er, t3)
        v.tensor_mul(ctre, Cre, t4)
        v.tensor_mul(s_, Cim, er)
        v.tensor_sub(ctre, ctre, s_)
        v.tensor_scalar(ctre, ctre, 2.0, None, ALU.mult)
        v.tensor_mul(ctim, Cre, er)
        v.tensor_mul(s_, Cim, t4)
        v.tensor_add(ctim, ctim, s_)
        return v.tensor_scalar(ctim, ctim, 2.0, None, ALU.mult)
    plan.op("vector", vstep4)
    prm_done = plan.val(vv)
    for k, src in enumerate([phi, rho, ctre, ctim]):
        plan.op("sync", lambda sy, k=k, src=src: sy.dma_start(
            out=dram["s_prm"][k:k + 1, :].rearrange("o (p n) -> p (o n)", p=128),
            in_=src), wait=[(vv, prm_done)], dma=stx)
    prm_st = plan.val(stx)

    ktile = [b_bf[:, 16384 + 128 * lt:16384 + 128 * (lt + 1)] for lt in range(16)]
    W1, W2, W3 = A[:, 0:2048], A[:, 2048:4096], A[:, 4096:6144]
    W4, W5, W6 = A[:, 6144:8192], A[:, 8192:10240], A[:, 10240:12288]
    IV = A[:, 12288:14336].bitcast(I32)
    liota = SM[:, 1088:1089]
    prev_red = 0
    prev_bc_mm = 0
    for hc in range(4):
        plan.op("sync", lambda sy, hc=hc: sy.dma_start(
            out=C[0:1, 0:8192].rearrange("p (k n) -> p k n", k=4),
            in_=dram["s_prm"].rearrange("k (c n) -> c k n", n=2048)[hc:hc + 1]),
            wait=[(stx, prm_st), (mmx, prev_bc_mm)], dma=ld)
        ldh = plan.val(ld)
        bcs = [Bb[:, 0:2048], Bb[:, 2048:4096], Bb[:, 4096:6144], Bb[:, 6144:8192]]
        for k in range(4):
            def bc1(t, k=k, ldh=ldh, pr=prev_red, pe=plan.val(vv)):
                t.wait_ge(plan.sems[ld], ldh)
                if pr:
                    t.wait_ge(plan.sems[vv], pr)
                t.wait_ge(plan.sems[vv], pe)
                for ch in range(4):
                    ins = t.matmul(bufs.PS[ch][:, :], cst["ones_row_f"],
                                   C[0:1, 2048 * k + 512 * ch:2048 * k + 512 * (ch + 1)],
                                   start=True, stop=True)
                return ins
            plan.op("tensor", bc1, inc=(mmx, 1))
            def ev1(v, k=k, mw=plan.val(mmx)):
                v.wait_ge(plan.sems[mmx], mw)
                for ch in range(4):
                    ins = v.tensor_copy(bcs[k][:, 512 * ch:512 * (ch + 1)], bufs.PS[ch][:, :])
                return ins
            plan.op("vector", ev1)
        bc_done = plan.val(vv)
        prev_bc_mm = plan.val(mmx)
        phib, rhob, ctreb, ctimb = bcs

        for lt in range(16):
            l0 = lt * 128
            def kcomp_v1(v, l0=l0, bc_done=bc_done):
                v.wait_ge(plan.sems[vv], bc_done)
                v.tensor_scalar(liota, cst["iota_col"], 1.0, float(l0), ALU.mult, ALU.add)
                v.tensor_scalar(W1, phib, liota, None, ALU.mult)
                return emit_rr(v, W1, W2, IV, W3, W6)
            plan.op("vector", kcomp_v1)
            def kcomp_s1(s, w=plan.val(vv)):
                s.wait_ge(plan.sems[vv], w)
                return s.activation(W3, W2, AF.Sin)
            plan.op("scalar", kcomp_s1)
            def kcomp_v2(v, w=plan.val(ss)):
                v.wait_ge(plan.sems[ss], w)
                v.tensor_scalar(W2, W1, HALF_PI, None, ALU.add)
                emit_rr(v, W2, W4, IV, W5, W6)
                return v.tensor_scalar(W5, rhob, liota, None, ALU.mult)
            plan.op("vector", kcomp_v2)
            def kcomp_s2(s, w=plan.val(vv)):
                s.wait_ge(plan.sems[vv], w)
                s.activation(W4, W4, AF.Sin)
                return s.activation(W5, W5, AF.Exp)
            plan.op("scalar", kcomp_s2)
            def kcomp_v3(v, w=plan.val(ss), lt=lt, hc=hc):
                v.wait_ge(plan.sems[ss], w)
                v.tensor_mul(W6, ctreb, W4)
                v.tensor_mul(W1, ctimb, W3)
                v.tensor_sub(W6, W6, W1)
                v.tensor_mul(W6, W6, W5)
                v.reduce_sum(W2[:, 0:32], W6.rearrange("p (h n) -> p h n", n=64), axis=AX)
                return v.tensor_copy(ktile[lt][:, 32 * hc:32 * (hc + 1)], W2[:, 0:32])
            plan.op("vector", kcomp_v3)
        prev_red = plan.val(vv)
    for lt in range(16):
        plan.op("sync", lambda sy, lt=lt: sy.dma_start(
            out=dram["s_kslice"][128 * lt:128 * (lt + 1), :], in_=ktile[lt]),
            wait=[(vv, prev_red)], dma=stx)
    return stx, plan.val(stx)



def emit_dft_fwd(plan, bufs, ph, dram, with_k=True, deps=()):
    cst = plan.consts
    a_bf, b_bf, c_bf, w_bf = bufs.a_bf(), bufs.b_bf(), bufs.c_bf(), bufs.w_bf()
    ld, mmx, stx = ph + "ld", ph + "mm", ph + "st"
    for s in (ld, mmx, stx):
        plan.sem(s)
    vv = "ve_p"
    deps = list(deps) + list(plan.cst_wait)
    nsrc = 3 if with_k else 2
    xt = [[a_bf[:, 16384 * b + 1024 * lt:16384 * b + 1024 * (lt + 1)] for lt in range(16)]
          for b in range(2)]
    kt = [c_bf[:, 1024 * lt:1024 * (lt + 1)] for lt in range(16)]
    ldin = ph + "in"
    for b in range(2):
        plan.op("sync", lambda sy, b=b: sy.dma_start(
            out=a_bf[:, 16384 * b:16384 * (b + 1)].rearrange("p (lt n) -> p lt n", lt=16),
            in_=dram["s_xnT"].rearrange("(b lt p) n -> b lt p n", b=2, p=128)[b]),
            dma=ldin, wait=deps)
    if with_k:
        plan.op("sync", lambda sy: sy.dma_start(
            out=c_bf[:, 0:16384].rearrange("p (lt r h) -> p lt r h", lt=16, r=8),
            in_=dram["s_kag"].rearrange("(r lt p) h -> lt p r h", r=8, p=128)),
            dma=ldin, wait=deps)
    ld_in = plan.val(ldin)
    twbuf = [[w_bf[:, 4096 * bb + 2048 * cs:4096 * bb + 2048 * (cs + 1)]
              for cs in range(2)] for bb in range(2)]
    stage = [b_bf[:, 1024 * i:1024 * (i + 1)] for i in range(4)]
    prev_grp_ev = [0, 0]
    prev_tw_mm = [0, 0]
    prev_store_all = 0
    first_mm = True
    gi = 0
    for ft in range(17):
        bb = ft % 2
        def twload(sy, ft=ft, bb=bb):
            sy.dma_start(
                out=twbuf[bb][0].rearrange("p (lt n) -> p lt n", lt=16),
                in_=dram["twc_t"][ft].rearrange("(lt p) n -> lt p n", p=128))
            return sy.dma_start(
                out=twbuf[bb][1].rearrange("p (lt n) -> p lt n", lt=16),
                in_=dram["tws_t"][ft].rearrange("(lt p) n -> lt p n", p=128))
        plan.op("sync", twload, dma=ld + str(bb), wait=[(mmx, prev_tw_mm[bb])] + deps)
        tw_ld = plan.val(ld + str(bb))
        for si in range(nsrc):
            src = xt[si] if si < 2 else kt
            g = gi % 2
            gi += 1
            if g == 0:
                ps_re = [bufs.PSW[:, 0:512], bufs.PSW[:, 512:1024]]
                ps_im = [bufs.PS[0][:, :], bufs.PS[1][:, :]]
            else:
                ps_re = [bufs.PS[2][:, :], bufs.PS[3][:, :]]
                ps_im = [bufs.PS[4][:, :], bufs.PS[5][:, :]]
            wl = [(ld + str(bb), tw_ld), (ldin, ld_in), (vv, prev_grp_ev[g])]
            if first_mm:
                wl += deps
                first_mm = False
            def mmg(t, src=src, bb=bb, ps_re=ps_re, ps_im=ps_im):
                for cs in range(2):
                    ps = ps_re if cs == 0 else ps_im
                    for nch in range(2):
                        for lt in range(16):
                            ins = t.matmul(ps[nch],
                                           twbuf[bb][cs][:, 128 * lt:128 * (lt + 1)],
                                           src[lt][:, 512 * nch:512 * (nch + 1)],
                                           start=(lt == 0), stop=(lt == 15))
                return ins
            plan.op("tensor", mmg, inc=(mmx, 1), wait=wl)
            if si == nsrc - 1:
                prev_tw_mm[bb] = plan.val(mmx)
            def ev(v, g=g, ps_re=ps_re, ps_im=ps_im):
                v.tensor_copy(stage[2 * g][:, 0:512], ps_re[0])
                v.tensor_copy(stage[2 * g][:, 512:1024], ps_re[1])
                v.tensor_copy(stage[2 * g + 1][:, 0:512], ps_im[0])
                return v.tensor_copy(stage[2 * g + 1][:, 512:1024], ps_im[1])
            plan.op("vector", ev, wait=[(mmx, plan.val(mmx)), (stx, prev_store_all)])
            prev_grp_ev[g] = plan.val(vv)
            if si < 2:
                dsl = dram["s_uf"][si]
            else:
                dsl = dram["s_kf"]
            def store(sy, ft=ft, g=g, dsl=dsl):
                sy.dma_start(out=dsl[0, 128 * ft:128 * (ft + 1), :], in_=stage[2 * g])
                return sy.dma_start(out=dsl[1, 128 * ft:128 * (ft + 1), :], in_=stage[2 * g + 1])
            plan.op("sync", store, dma=stx, wait=[(vv, prev_grp_ev[g])])
            if gi % 2 == 0:
                prev_store_all = plan.val(stx)
    return stx, plan.val(stx)



def emit_conv(plan, bufs, ph, dram, deps=()):
    cst = plan.consts
    a_bf, b_bf, c_bf, w_bf = bufs.a_bf(), bufs.b_bf(), bufs.c_bf(), bufs.w_bf()
    Bb = bufs.AR_B
    ld, mmx, stx = ph + "ld", ph + "mm", ph + "st"
    for s in (ld, mmx, stx):
        plan.sem(s)
    vv, ss = "ve_p", "sc_p"
    deps = list(deps) + list(plan.cst_wait)
    ustg = b_bf[:, 4096:8192]
    s1f, s2f = Bb[:, 4096:5120], Bb[:, 5120:6144]
    dbc = Bb[:, 6144:7168]
    y2r, t1r = Bb[:, 8192:9216], Bb[:, 9216:10240]
    gout = b_bf[:, 20480:21504]

    def vf_t(ft, cs):
        if ft < 16:
            return a_bf[:, 2048 * ft + 1024 * cs:2048 * ft + 1024 * (cs + 1)]
        return b_bf[:, 1024 * cs:1024 * (cs + 1)]

    plan.op("sync", lambda sy: sy.dma_start(out=Bb[0:1, 7168:8192], in_=dram["ssm_D_row"]),
            dma=ph + "d", wait=deps)
    def dbc_mm(t):
        t.matmul(bufs.PS[0][:, :], cst["ones_row_f"], Bb[0:1, 7168:7680], start=True, stop=True)
        return t.matmul(bufs.PS[1][:, :], cst["ones_row_f"], Bb[0:1, 7680:8192], start=True, stop=True)
    plan.op("tensor", dbc_mm, inc=(mmx, 1), wait=[(ph + "d", plan.val(ph + "d"))] + deps)
    def dbc_ev(v):
        v.tensor_copy(dbc[:, 0:512], bufs.PS[0][:, :])
        return v.tensor_copy(dbc[:, 512:1024], bufs.PS[1][:, :])
    plan.op("vector", dbc_ev, wait=[(mmx, plan.val(mmx))])

    prev_itw_mm = [0, 0]
    prev_imm_all = 0
    prev_epi = 0
    prev_gstore = 0
    for b in range(2):
        for ft in range(17):
            def ldt(sy, b=b, ft=ft):
                sy.dma_start(out=ustg[:, 0:2048].rearrange("p (c n) -> p c n", c=2),
                             in_=dram["s_uf"][b].rearrange("c (ft p) n -> ft p c n", p=128)[ft])
                return sy.dma_start(out=ustg[:, 2048:4096].rearrange("p (c n) -> p c n", c=2),
                                    in_=dram["s_kf"].rearrange("c (ft p) n -> ft p c n", p=128)[ft])
            plan.op("sync", ldt, dma=ld,
                    wait=[(vv, plan.val(vv))] + deps)
            ur, ui = ustg[:, 0:1024], ustg[:, 1024:2048]
            kr, ki = ustg[:, 2048:3072], ustg[:, 3072:4096]
            def vfc(v, w=plan.val(ld), ft=ft, mprev=prev_imm_all):
                v.wait_ge(plan.sems[ld], w)
                v.wait_ge(plan.sems[mmx], mprev)
                v.tensor_mul(s1f, ur, kr)
                v.tensor_mul(s2f, ui, ki)
                v.tensor_sub(vf_t(ft, 0), s1f, s2f)
                v.tensor_mul(s1f, ur, ki)
                v.tensor_mul(s2f, ui, kr)
                return v.tensor_add(vf_t(ft, 1), s1f, s2f)
            plan.op("vector", vfc)
        vf_done = plan.val(vv)
        xn_t = [c_bf[:, 1024 * lt:1024 * (lt + 1)] for lt in range(16)]
        plan.op("sync", lambda sy, b=b: sy.dma_start(
            out=c_bf[:, 0:16384].rearrange("p (lt n) -> p lt n", lt=16),
            in_=dram["s_xnT"].rearrange("(b lt p) n -> b p lt n", b=2, p=128)[b]),
            dma=ph + "x", wait=[(vv, prev_epi)] + deps)
        xn_ld = plan.val(ph + "x")
        for lt in range(16):
            bb = 0
            def itwload(sy, lt=lt):
                sy.dma_start(
                    out=w_bf[:, 0:2176].rearrange("p (ft n) -> p ft n", ft=17),
                    in_=dram["twic_t"][lt].rearrange("(ft p) n -> p ft n", p=128))
                return sy.dma_start(
                    out=w_bf[:, 2176:4352].rearrange("p (ft n) -> p ft n", ft=17),
                    in_=dram["twis_t"][lt].rearrange("(ft p) n -> p ft n", p=128))
            plan.op("sync", itwload, dma=ph + "t", wait=[(mmx, prev_imm_all)] + deps)
            tw_ld = plan.val(ph + "t")
            def imm(t):
                for nch in range(2):
                    for cs in range(2):
                        for ft in range(17):
                            ins = t.matmul(
                                bufs.PSW[:, 512 * nch:512 * (nch + 1)],
                                w_bf[:, 2176 * cs + 128 * ft:2176 * cs + 128 * (ft + 1)],
                                vf_t(ft, cs)[:, 512 * nch:512 * (nch + 1)],
                                start=(cs == 0 and ft == 0), stop=(cs == 1 and ft == 16))
                return ins
            plan.op("tensor", imm, inc=(mmx, 1),
                    wait=[(ph + "t", tw_ld), (vv, max(vf_done, prev_epi))])
            prev_itw_mm[bb] = plan.val(mmx)
            prev_imm_all = plan.val(mmx)
            def epi_v(v, lt=lt, mw=plan.val(mmx), xn_ld=xn_ld, sw=plan.val("sc_p")):
                v.wait_ge(plan.sems[mmx], mw)
                v.wait_ge(plan.sems[ph + "x"], xn_ld)
                v.wait_ge(plan.sems["sc_p"], sw)
                v.tensor_mul(t1r, xn_t[lt], dbc)
                return v.tensor_add(y2r, bufs.PSW[:, :], t1r)
            plan.op("vector", epi_v)
            prev_epi = plan.val(vv)
            def epi_s(s, w=prev_epi, pg=prev_gstore):
                s.wait_ge(plan.sems[vv], w)
                s.wait_ge(plan.sems[stx], pg)
                return s.activation(gout, y2r, AF.Gelu_apprx_tanh)
            plan.op("scalar", epi_s)
            plan.op("sync", lambda sy, b=b, lt=lt: sy.dma_start(
                out=dram["s_g"][b, 128 * lt:128 * (lt + 1), :], in_=gout),
                dma=stx, wait=[(ss, plan.val(ss))])
            prev_gstore = plan.val(stx)
    return stx, plan.val(stx)



def emit_wout(plan, bufs, ph, dram, deps=()):
    cst = plan.consts
    a_bf, b_bf, c_bf, w_bf = bufs.a_bf(), bufs.b_bf(), bufs.c_bf(), bufs.w_bf()
    SM = bufs.SM
    ld, mmx, stx = ph + "ld", ph + "mm", ph + "st"
    for s in (ld, mmx, stx):
        plan.sem(s)
    vv, ss = "ve_p", "sc_p"
    deps = list(deps) + list(plan.cst_wait)
    plan.op("sync", lambda sy: sy.dma_start(
        out=SM[:, 24:40], in_=dram["b_out"].rearrange("(dt p) o -> p (dt o)", p=128),
        allow_slow_non_contiguous=True),
        dma=ph + "b", wait=deps)
    bout_ld = plan.val(ph + "b")
    prev_gT_mm = 0
    prev_z = 0
    prev_glu_store = 0
    for b in range(2):
        g_t = [a_bf[:, 1024 * lt:1024 * (lt + 1)] for lt in range(16)]
        plan.op("sync", lambda sy, b=b: sy.dma_start(
            out=a_bf[:, 0:16384].rearrange("p (lt n) -> p lt n", lt=16),
            in_=dram["s_g"].rearrange("b (lt p) n -> b p lt n", p=128)[b]),
            dma=ph + "g", wait=[(mmx, prev_gT_mm)] + deps)
        g_ld = plan.val(ph + "g")
        gT = [c_bf[:, 2048 * ht:2048 * (ht + 1)] for ht in range(8)]
        psb = bufs.PSW[:, :].bitcast(BF16)
        for ht in range(8):
            def trans(t, ht=ht, g_ld=g_ld, pe=plan.val(vv)):
                t.wait_ge(plan.sems[ph + "g"], g_ld)
                t.wait_ge(plan.sems[vv], pe)
                for lt in range(16):
                    ins = t.transpose(psb[:, 128 * lt:128 * (lt + 1)],
                                      g_t[lt][:, 128 * ht:128 * (ht + 1)], cst["ident"])
                return ins
            plan.op("tensor", trans, inc=(mmx, 1))
            def tev(v, ht=ht, mw=plan.val(mmx)):
                v.wait_ge(plan.sems[mmx], mw)
                return v.tensor_copy(gT[ht], psb[:, 0:2048])
            plan.op("vector", tev)
        gT_done = plan.val(vv)
        prev_gT_mm = plan.val(mmx)
        for dpt in range(16):
            bb = dpt % 2
            wslab = w_bf[:, 1024 * bb:1024 * (bb + 1)]
            def wload(g, dpt=dpt, wslab=wslab):
                return g.dma_start(
                    out=wslab.rearrange("p (ht n) -> p ht n", ht=8),
                    in_=dram["w_out"].rearrange("(ht p) dp -> p ht dp", p=128)[:, :, 128 * dpt:128 * (dpt + 1)])
            wsem = ph + "wld" + str(bb)
            plan.op("gpsimd", wload, dma=wsem,
                    wait=[(mmx, prev_gT_mm if dpt < 2 else plan.val(mmx))])
            w_ld = plan.val(wsem)
            def zmm(t, dpt=dpt, wslab=wslab, w_ld=w_ld, gd=gT_done, pz=prev_z, wsem=wsem):
                t.wait_ge(plan.sems[wsem], w_ld)
                t.wait_ge(plan.sems[vv], max(gd, pz))
                for nch in range(4):
                    for ht in range(8):
                        ins = t.matmul(bufs.PS[nch][:, :],
                                       wslab[:, 128 * ht:128 * (ht + 1)],
                                       gT[ht][:, 512 * nch:512 * (nch + 1)],
                                       start=(ht == 0), stop=(ht == 7))
                return ins
            plan.op("tensor", zmm, inc=(mmx, 1))
            zt = b_bf[:, 2048 * dpt:2048 * (dpt + 1)]
            def zev(v, dpt=dpt, zt=zt, mw=plan.val(mmx), bl=bout_ld, sw=plan.val("sc_p")):
                v.wait_ge(plan.sems[mmx], mw)
                v.wait_ge(plan.sems[ph + "b"], bl)
                v.wait_ge(plan.sems["sc_p"], sw)
                for nch in range(4):
                    ins = v.tensor_scalar(zt[:, 512 * nch:512 * (nch + 1)],
                                          bufs.PS[nch][:, :], SM[:, 24 + dpt:25 + dpt],
                                          None, ALU.add)
                return ins
            plan.op("vector", zev)
            prev_z = plan.val(vv)
        for ht in range(8):
            sg = a_bf[:, 16384 + 2048 * (ht % 2):16384 + 2048 * (ht % 2 + 1)]
            def glu_s(s, ht=ht, sg=sg, w=prev_z, pg=prev_glu_store):
                s.wait_ge(plan.sems[vv], w)
                s.wait_ge(plan.sems[stx], pg)
                return s.activation(sg, b_bf[:, 2048 * (8 + ht):2048 * (9 + ht)], AF.Sigmoid)
            plan.op("scalar", glu_s)
            def glu_v(v, ht=ht, sg=sg, w=plan.val(ss), pg=prev_glu_store):
                v.wait_ge(plan.sems[ss], w)
                v.wait_ge(plan.sems[stx], pg)
                return v.tensor_mul(sg, b_bf[:, 2048 * ht:2048 * (ht + 1)], sg)
            plan.op("vector", glu_v)
            plan.op("sync", lambda sy, b=b, ht=ht, sg=sg: sy.dma_start(
                out=dram["s_glu"].rearrange("(ht p) t -> ht p t", p=128)[ht, :, 2048 * b:2048 * (b + 1)],
                in_=sg), dma=stx, wait=[(vv, plan.val(vv))])
            prev_glu_store = plan.val(stx)
    return stx, plan.val(stx)


def emit_l0attn(plan, bufs, dram, deps=()):
    s1, v1 = emit_dft_fwd(plan, bufs, "df", dram, with_k=True, deps=deps)
    s2, v2 = emit_conv(plan, bufs, "cv", dram, deps=[(s1, v1)])
    s3, v3 = emit_wout(plan, bufs, "wo", dram, deps=[(s2, v2)])
    s4, v4 = emit_ln(plan, bufs, "lns", dram["s_glu"], dram["s_attn0"],
                     gamma=dram["ssm_gamma"], beta=dram["ssm_beta"],
                     src_f32=False, wait_st=[(s3, v3)])
    return s4, v4



def emit_moe(plan, bufs, ph, dram, deps=()):
    cst = plan.consts
    a_bf, b_bf, c_bf, w_bf = bufs.a_bf(), bufs.b_bf(), bufs.c_bf(), bufs.w_bf()
    A, Bb, C, SM = bufs.AR_A, bufs.AR_B, bufs.AR_C, bufs.SM
    ld, mmx = ph + "ld", ph + "mm"
    for s in (ld, mmx):
        plan.sem(s)
    vv, ss = "ve_p", "sc_p"
    deps = list(deps) + list(plan.cst_wait)
    IC = 512
    TH = 2048
    xn_t = [a_bf[:, 2048 * i:2048 * (i + 1)] for i in range(8)]
    acc = [Bb[:, 2048 * i:2048 * (i + 1)] for i in range(8)]
    g_t = [a_bf[:, 16384 + 2048 * i:16384 + 2048 * (i + 1)] for i in range(4)]
    u_t = [a_bf[:, 24576 + 2048 * i:24576 + 2048 * (i + 1)] for i in range(4)]
    cwb = C[:, 0:2048]
    crow = C[0:1, 0:2048]
    wd_buf = [c_bf[:, 4096 + 4096 * p:4096 + 4096 * (p + 1)] for p in range(2)]
    cwT = C[0:8, 6144:8192]
    wg_buf = [w_bf[:, 0:4096], w_bf[:, 8192:12288]]
    wu_buf = [w_bf[:, 4096:8192], c_bf[:, 12288:16384]]
    gate_b = SM[:, 704:736].bitcast(BF16)
    wfree = [0, 0]
    icg = [0]

    plan.op("gpsimd", lambda g: g.dma_start(
        out=gate_b.rearrange("p (i e) -> p i e", i=8),
        in_=dram["gate_wT"].rearrange("(i p) e -> p i e", p=128)), dma=ph + "ga", wait=deps)
    gate_ld = plan.val(ph + "ga")

    nbank = 6
    bank_free = [("ve_p", 0)] * nbank
    bk = [0]

    def take_bank():
        b = bk[0] % nbank
        bk[0] += 1
        return bufs.PS[b], bank_free[b], b

    prev_half_v = 0
    for th in range(2):
        t0 = th * TH
        plan.op("sync", lambda sy, t0=t0: sy.dma_start(
            out=a_bf[:, 0:16384].rearrange("p (dt t) -> p dt t", dt=8),
            in_=fm_dram(dram["s_xn"], t0, TH)), dma=ld,
            wait=deps + [(vv, prev_half_v), (ph + "st", plan.val(ph + "st"))])
        xn_ld = plan.val(ld)
        sc0 = SM[:, 40:48]; m1 = SM[:, 48:49]; m2 = SM[:, 49:50]; tmp8 = SM[:, 56:64]
        prev_cwt_mm = 0
        for tt in range(16):
            psg, bfree, bid = take_bank()
            def gmm(t, tt=tt, psg=psg, bfree=bfree, xn_ld=xn_ld):
                t.wait_ge(plan.sems[ld], xn_ld)
                t.wait_ge(plan.sems[ph + "ga"], gate_ld)
                t.wait_ge(plan.sems[bfree[0]], bfree[1])
                for ci in range(8):
                    ins = t.matmul(psg[:, 0:8], xn_t[ci][:, 128 * tt:128 * (tt + 1)],
                                   gate_b[:, 8 * ci:8 * (ci + 1)],
                                   start=(ci == 0), stop=(ci == 7))
                return ins
            plan.op("tensor", gmm, inc=(mmx, 1))
            def cwv(v, psg=psg, mw=plan.val(mmx)):
                v.wait_ge(plan.sems[mmx], mw)
                v.reduce_max(m1, psg[:, 0:8], axis=AX)
                v.tensor_scalar(m1, m1, -1.0, None, ALU.mult)
                return v.tensor_scalar(sc0, psg[:, 0:8], m1, None, ALU.add)
            plan.op("vector", cwv)
            bank_free[bid] = ("ve_p", plan.val(vv))
            plan.op("scalar", lambda s, w=plan.val(vv): (
                s.wait_ge(plan.sems[vv], w), s.activation(sc0, sc0, AF.Exp))[1])
            def cwv2(v, w=plan.val(ss)):
                v.wait_ge(plan.sems[ss], w)
                v.reduce_sum(m1, sc0, axis=AX)
                v.reciprocal(m1, m1)
                v.tensor_scalar(sc0, sc0, m1, None, ALU.mult)
                v.reduce_max(m1, sc0, axis=AX)
                v.tensor_scalar(tmp8, sc0, m1, None, ALU.is_ge)
                v.scalar_tensor_tensor(tmp8, tmp8, -1e30, sc0, ALU.mult, ALU.add)
                v.reduce_max(m2, tmp8, axis=AX)
                v.tensor_scalar(tmp8, sc0, m2, None, ALU.is_ge)
                return v.tensor_mul(sc0, sc0, tmp8)
            plan.op("vector", cwv2)
            def cwt(t, tt=tt, w=plan.val(vv), pm=prev_cwt_mm):
                t.wait_ge(plan.sems[vv], w)
                return t.transpose(bufs.PSW[0:8, 0:128], sc0[:, 0:8], cst["ident_f"])
            plan.op("tensor", cwt, inc=(mmx, 1))
            def cwev(v, tt=tt, mw=plan.val(mmx)):
                v.wait_ge(plan.sems[mmx], mw)
                return v.tensor_copy(cwT[:, 128 * tt:128 * (tt + 1)], bufs.PSW[0:8, 0:128])
            plan.op("vector", cwev)
            prev_cwt_mm = plan.val(vv)
        plan.op("sync", lambda sy, t0=t0: sy.dma_start(
            out=dram["s_cwT"][:, t0:t0 + TH], in_=cwT), dma=ph + "cw",
            wait=[(vv, plan.val(vv))])
        cw_st = plan.val(ph + "cw")

        first_acc = True
        wld_free_mm = 0
        wd_free_mm = [0, 0]
        for e in range(9):
            shared = (e == 8)
            if not shared:
                plan.sem(ph + "dbg")
                plan.op("sync", lambda sy, e=e, t0=t0: sy.dma_start(
                    out=crow, in_=dram["s_cwT"][e:e + 1, t0:t0 + TH]),
                    dma=ph + "cr", wait=[(ph + "cw", cw_st), (vv, plan.val(vv)),
                                         (ph + "dbg", plan.val(ph + "dbg"))])
                crow_ld = plan.val(ph + "cr")
                for chk in range(4):
                    def cwbmm(t, chk=chk, w=crow_ld, pe=plan.val(vv)):
                        t.wait_ge(plan.sems[ph + "cr"], w)
                        t.wait_ge(plan.sems[vv], pe)
                        return t.matmul(bufs.PSW[:, 0:512], cst["ones_row_f"],
                                        crow[:, 512 * chk:512 * (chk + 1)],
                                        start=True, stop=True)
                    plan.op("tensor", cwbmm, inc=(mmx, 1))
                    def cwbev(v, chk=chk, mw=plan.val(mmx)):
                        v.wait_ge(plan.sems[mmx], mw)
                        return v.tensor_copy(cwb[:, 512 * chk:512 * (chk + 1)],
                                             bufs.PSW[:, 0:512])
                    plan.op("vector", cwbev)
            cwb_done = plan.val(vv)
            if th == 0 and e == 0 and "s_dbgc" in dram:
                plan.op("sync", lambda sy: sy.dma_start(out=dram["s_dbgc"], in_=cwb),
                        dma=ph + "dbg", wait=[(vv, cwb_done)])
            for ic in range(4 if shared else 8):
                if shared:
                    wg_src = dram["ws_g"][:, IC * ic:IC * (ic + 1)]
                    wu_src = dram["ws_u"][:, IC * ic:IC * (ic + 1)]
                    wd_src = dram["ws_d"][IC * ic:IC * (ic + 1), :]
                else:
                    wg_src = dram["moe_wg"][e][:, IC * ic:IC * (ic + 1)]
                    wu_src = dram["moe_wu"][e][:, IC * ic:IC * (ic + 1)]
                    wd_src = dram["moe_wd"][e][IC * ic:IC * (ic + 1), :]
                p_gu = icg[0] % 2
                icg[0] += 1
                cww = [(ph + "cw", cw_st)] if (e == 0 and ic < 2) else []

                def wgld(g, wg_src=wg_src, wu_src=wu_src, p_gu=p_gu, wf=wfree[p_gu],
                         dp=list(deps) + cww):
                    for s_, v_ in dp:
                        g.wait_ge(plan.sems[s_], v_)
                    g.wait_ge(plan.sems[mmx], wf)
                    g.dma_start(out=wg_buf[p_gu].rearrange("p (ct i) -> p ct i", ct=8),
                                in_=wg_src.rearrange("(ct p) i -> p ct i", p=128))
                    return g.dma_start(out=wu_buf[p_gu].rearrange("p (ct i) -> p ct i", ct=8),
                                       in_=wu_src.rearrange("(ct p) i -> p ct i", p=128))
                plan.op("gpsimd", wgld, dma=ph + "w")
                wgu_ld = plan.val(ph + "w")
                wdp = ic % 2
                def wdld(g, wd_src=wd_src, wdp=wdp, wf=plan.val(mmx), dp=list(deps),
                         pw=plan.val(ph + "wd" + str(wdp))):
                    for s_, v_ in dp:
                        g.wait_ge(plan.sems[s_], v_)
                    g.wait_ge(plan.sems[mmx], wf)
                    g.wait_ge(plan.sems[ph + "wd" + str(wdp)], pw)
                    return g.dma_start(out=wd_buf[wdp].rearrange("p (it d) -> p it d", it=4),
                                       in_=wd_src.rearrange("(it p) d -> p it d", p=128))
                plan.op("gpsimd", wdld, dma=ph + "wd" + str(wdp))
                wd_ld = plan.val(ph + "wd" + str(wdp))
                for uu in range(2):
                    wbuf = wg_buf[p_gu] if uu == 0 else wu_buf[p_gu]
                    dst_t = g_t if uu == 0 else u_t
                    for it in range(4):
                        for tc in range(4):
                            psb, bfree, bid = take_bank()
                            def gu_mm(t, it=it, tc=tc, uu=uu, wbuf=wbuf, psb=psb,
                                      bfree=bfree, wl=wgu_ld, xl=xn_ld,
                                      pst=plan.val(ph + "st")):
                                t.wait_ge(plan.sems[ph + "w"], wl)
                                t.wait_ge(plan.sems[ld], xl)
                                t.wait_ge(plan.sems[ph + "st"], pst)
                                t.wait_ge(plan.sems[bfree[0]], bfree[1])
                                for ci in range(8):
                                    ins = t.matmul(
                                        psb[:, :],
                                        wbuf[:, 512 * ci + 128 * it:512 * ci + 128 * (it + 1)],
                                        xn_t[ci][:, 512 * tc:512 * (tc + 1)],
                                        start=(ci == 0), stop=(ci == 7))
                                return ins
                            plan.op("tensor", gu_mm, inc=(mmx, 1))
                            mmv = plan.val(mmx)
                            if uu == 0:
                                def sev(s, it=it, tc=tc, psb=psb, mw=mmv):
                                    s.wait_ge(plan.sems[mmx], mw)
                                    return s.activation(
                                        g_t[it][:, 512 * tc:512 * (tc + 1)], psb[:, :], AF.Silu)
                                plan.op("scalar", sev)
                                bank_free[bid] = ("sc_p", plan.val(ss))
                            else:
                                def uev(v, it=it, tc=tc, psb=psb, mw=mmv, shared=shared,
                                        cd=cwb_done):
                                    v.wait_ge(plan.sems[mmx], mw)
                                    if shared:
                                        return v.tensor_copy(
                                            u_t[it][:, 512 * tc:512 * (tc + 1)], psb[:, :])
                                    v.wait_ge(plan.sems[vv], cd)
                                    return v.tensor_mul(
                                        u_t[it][:, 512 * tc:512 * (tc + 1)], psb[:, :],
                                        cwb[:, 512 * tc:512 * (tc + 1)])
                                plan.op("vector", uev)
                                bank_free[bid] = ("ve_p", plan.val(vv))
                wfree[p_gu] = plan.val(mmx)
                def hmul(v, w=plan.val(ss)):
                    v.wait_ge(plan.sems[ss], w)
                    for it in range(4):
                        ins = v.tensor_mul(g_t[it], g_t[it], u_t[it])
                    return ins
                plan.op("vector", hmul)
                h_done = plan.val(vv)
                for dt in range(8):
                    for tc in range(4):
                        psb, bfree, bid = take_bank()
                        def yo_mm(t, dt=dt, tc=tc, psb=psb, bfree=bfree, wl=wd_ld,
                                  hd=h_done, wdp=wdp):
                            t.wait_ge(plan.sems[ph + "wd" + str(wdp)], wl)
                            t.wait_ge(plan.sems[vv], hd)
                            t.wait_ge(plan.sems[bfree[0]], bfree[1])
                            for it in range(4):
                                ins = t.matmul(
                                    psb[:, :],
                                    wd_buf[wdp][:, 1024 * it + 128 * dt:1024 * it + 128 * (dt + 1)],
                                    g_t[it][:, 512 * tc:512 * (tc + 1)],
                                    start=(it == 0), stop=(it == 3))
                            return ins
                        plan.op("tensor", yo_mm, inc=(mmx, 1))
                        def yoev(v, dt=dt, tc=tc, psb=psb, mw=plan.val(mmx), fa=first_acc):
                            v.wait_ge(plan.sems[mmx], mw)
                            dsl = acc[dt][:, 512 * tc:512 * (tc + 1)]
                            if fa:
                                return v.tensor_copy(dsl, psb[:, :])
                            return v.tensor_add(dsl, dsl, psb[:, :])
                        plan.op("vector", yoev)
                        bank_free[bid] = ("ve_p", plan.val(vv))
                first_acc = False
                wd_free_mm[wdp] = plan.val(mmx)
        x0stg = [A[:, 8192 + 2048 * p:8192 + 2048 * (p + 1)] for p in range(2)]
        atstg = [a_bf[:, 24576 + 2048 * p:24576 + 2048 * (p + 1)] for p in range(2)]
        for dt in range(8):
            p = dt % 2
            def x1ld(sy, dt=dt, p=p, t0=t0, pe=plan.val(vv), pst=plan.val(ph + "st")):
                sy.wait_ge(plan.sems[vv], pe)
                sy.wait_ge(plan.sems[ph + "st"], pst)
                sy.dma_start(out=x0stg[p],
                             in_=dram["x0T"].rearrange("(dt q) t -> dt q t", q=128)[dt, :, t0:t0 + TH])
                return sy.dma_start(out=atstg[p],
                                    in_=dram["s_attn0"].rearrange("(dt q) t -> dt q t", q=128)[dt, :, t0:t0 + TH])
            plan.op("sync", x1ld, dma=ph + "x1")
            def x1v(v, dt=dt, p=p, w=plan.val(ph + "x1")):
                v.wait_ge(plan.sems[ph + "x1"], w)
                v.tensor_add(x0stg[p], x0stg[p], acc[dt])
                return v.tensor_add(x0stg[p], x0stg[p], atstg[p])
            plan.op("vector", x1v)
            plan.op("sync", lambda sy, dt=dt, p=p, t0=t0, w=plan.val(vv): (
                sy.wait_ge(plan.sems[vv], w),
                sy.dma_start(out=dram["s_x1"].rearrange("(dt q) t -> dt q t", q=128)[dt, :, t0:t0 + TH],
                             in_=x0stg[p]))[1], dma=ph + "st")
        prev_half_v = plan.val(vv)
    return ph + "st", plan.val(ph + "st")



def emit_attn(plan, bufs, ph, dram, deps=()):
    cst = plan.consts
    a_bf, b_bf, c_bf, w_bf = bufs.a_bf(), bufs.b_bf(), bufs.c_bf(), bufs.w_bf()
    A, Bb, C, SM = bufs.AR_A, bufs.AR_B, bufs.AR_C, bufs.SM
    mmx = ph + "mm"
    plan.sem(mmx)
    vv, ss = "ve_p", "sc_p"
    deps = list(deps) + list(plan.cst_wait)
    qT = [a_bf[:, 2048 * i:2048 * (i + 1)] for i in range(8)]
    kT = [a_bf[:, 16384 + 2048 * i:16384 + 2048 * (i + 1)] for i in range(8)]
    xt = [b_bf[:, 2048 * i:2048 * (i + 1)] for i in range(8)]
    vt = [c_bf[:, 1024 * i:1024 * (i + 1)] for i in range(16)]
    et = [b_bf[:, 16384 + 512 * i:16384 + 512 * (i + 1)] for i in range(16)]
    wbuf = w_bf[:, 0:8192]
    ystg = b_bf[:, 24576 + 0:24576 + 512]
    rbst = Bb[:, 12544:13056]
    q2s = Bb[:, 13056:13568]
    nbank = 4
    bank_free = [("ve_p", 0)] * nbank
    bkc = [0]

    def takeb():
        b = bkc[0] % nbank
        bkc[0] += 1
        return bufs.PS[b], bank_free[b], b

    prev_y_store = 0
    for b in range(2):
        t0 = b * 2048
        plan.op("sync", lambda sy, t0=t0: sy.dma_start(
            out=b_bf[:, 0:16384].rearrange("p (dt t) -> p dt t", dt=8),
            in_=fm_dram(dram["s_xn1"], t0, 2048)), dma=ph + "x",
            wait=deps + [(mmx, plan.val(mmx)), (vv, plan.val(vv)), (ss, plan.val(ss))])
        x_ld = plan.val(ph + "x")
        for wi, wname in enumerate(["wqT", "wkT", "wvT"]):
            plan.op("gpsimd", lambda g, wname=wname, mv=plan.val(mmx): (
                g.wait_ge(plan.sems[mmx], mv),
                g.dma_start(out=wbuf.rearrange("p (ct n) -> p ct n", ct=8),
                            in_=dram[wname].rearrange("(ct p) n -> p ct n", p=128)))[1],
                dma=ph + "w", wait=deps)
            w_ld = plan.val(ph + "w")
            if wi < 2:
                dst = qT if wi == 0 else kT
                for m in range(8):
                    for tc in range(4):
                        psb, bfree, bid = takeb()
                        def pmm(t, m=m, tc=tc, psb=psb, bfree=bfree, w_ld=w_ld, x_ld=x_ld):
                            t.wait_ge(plan.sems[ph + "w"], w_ld)
                            t.wait_ge(plan.sems[ph + "x"], x_ld)
                            t.wait_ge(plan.sems[bfree[0]], bfree[1])
                            for ci in range(8):
                                ins = t.matmul(psb[:, :],
                                               wbuf[:, 1024 * ci + 128 * m:1024 * ci + 128 * (m + 1)],
                                               xt[ci][:, 512 * tc:512 * (tc + 1)],
                                               start=(ci == 0), stop=(ci == 7))
                            return ins
                        plan.op("tensor", pmm, inc=(mmx, 1))
                        def pev(v, m=m, tc=tc, psb=psb, mw=plan.val(mmx), dst=dst):
                            v.wait_ge(plan.sems[mmx], mw)
                            return v.tensor_copy(dst[m][:, 512 * tc:512 * (tc + 1)], psb[:, :])
                        plan.op("vector", pev)
                        bank_free[bid] = ("ve_p", plan.val(vv))
            else:
                for tt in range(16):
                    for nc2 in range(2):
                        psb, bfree, bid = takeb()
                        def vmm(t, tt=tt, nc2=nc2, psb=psb, bfree=bfree, w_ld=w_ld, x_ld=x_ld):
                            t.wait_ge(plan.sems[ph + "w"], w_ld)
                            t.wait_ge(plan.sems[ph + "x"], x_ld)
                            t.wait_ge(plan.sems[bfree[0]], bfree[1])
                            for ci in range(8):
                                ins = t.matmul(psb[:, :],
                                               xt[ci][:, 128 * tt:128 * (tt + 1)],
                                               wbuf[:, 1024 * ci + 512 * nc2:1024 * ci + 512 * (nc2 + 1)],
                                               start=(ci == 0), stop=(ci == 7))
                            return ins
                        plan.op("tensor", vmm, inc=(mmx, 1))
                        def vev(v, tt=tt, nc2=nc2, psb=psb, mw=plan.val(mmx)):
                            v.wait_ge(plan.sems[mmx], mw)
                            return v.tensor_copy(vt[tt][:, 512 * nc2:512 * (nc2 + 1)], psb[:, :])
                        plan.op("vector", vev)
                        bank_free[bid] = ("ve_p", plan.val(vv))
        qkv_done = plan.val(vv)
        sq = Bb[:, 13568:15616]
        q2m = SM[0:1, 104:105]
        k2m = SM[0:1, 105:106]
        pch = [bufs.PSW[0:1, 0:512], bufs.PSW[0:1, 512:1024],
               bufs.PS[4][0:1, 0:512], bufs.PS[5][0:1, 0:512]]
        for qi, (src_, dstm) in enumerate([(qT, q2m), (kT, k2m)]):
            prev_nmm = plan.val(mmx)
            for m in range(8):
                def sq1(s, m=m, src_=src_, w=qkv_done, pn=prev_nmm):
                    s.wait_ge(plan.sems[vv], w)
                    s.wait_ge(plan.sems[mmx], pn)
                    return s.activation(sq, src_[m], AF.Square)
                plan.op("scalar", sq1)
                def nmm(t, m=m, w=plan.val(ss), pv=plan.val(vv)):
                    t.wait_ge(plan.sems[ss], w)
                    t.wait_ge(plan.sems[vv], pv)
                    for cc in range(4):
                        ins = t.matmul(pch[cc], cst["ones_col_f"],
                                       sq[:, 512 * cc:512 * (cc + 1)],
                                       start=(m == 0), stop=(m == 7))
                    return ins
                plan.op("tensor", nmm, inc=(mmx, 1))
                prev_nmm = plan.val(mmx)
            def bfin(v, dstm=dstm, mw=prev_nmm):
                v.wait_ge(plan.sems[mmx], mw)
                v.reduce_max(SM[0:1, 107:108], pch[0], axis=AX)
                v.reduce_max(SM[0:1, 108:109], pch[1], axis=AX)
                v.tensor_max(SM[0:1, 107:108], SM[0:1, 107:108], SM[0:1, 108:109])
                v.reduce_max(SM[0:1, 108:109], pch[2], axis=AX)
                v.tensor_max(SM[0:1, 107:108], SM[0:1, 107:108], SM[0:1, 108:109])
                v.reduce_max(SM[0:1, 108:109], pch[3], axis=AX)
                return v.tensor_max(dstm, SM[0:1, 107:108], SM[0:1, 108:109])
            plan.op("vector", bfin)
        def bnd_v(v):
            v.tensor_mul(q2m, q2m, k2m)
            return v.tensor_scalar(q2m, q2m, 1.0 / 64.0, None, ALU.mult)
        plan.op("vector", bnd_v)
        def bnd_s(s, w=plan.val(vv)):
            s.wait_ge(plan.sems[vv], w)
            return s.activation(q2m, q2m, AF.Sqrt)
        plan.op("scalar", bnd_s)
        def bnd_v2(v, w=plan.val(ss)):
            v.wait_ge(plan.sems[ss], w)
            return v.tensor_scalar(q2m, q2m, -1.02 * 0.125, None, ALU.mult)
        plan.op("vector", bnd_v2)
        def bnd_bc(t, w=plan.val(vv)):
            t.wait_ge(plan.sems[vv], w)
            return t.matmul(bufs.PS[5][:, 0:1], cst["ones_row_f"], q2m, start=True, stop=True)
        plan.op("tensor", bnd_bc, inc=(mmx, 1))
        negb = SM[:, 109:110]
        def bnd_ev(v, w=plan.val(mmx)):
            v.wait_ge(plan.sems[mmx], w)
            return v.tensor_copy(negb, bufs.PS[5][:, 0:1])
        plan.op("vector", bnd_ev)
        bnd_done = plan.val(vv)
        prev_dpv = 0
        for h in range(16):
            hb = (h % 2) * 64
            hm = h // 2
            for qc in range(4):
                stl = 4 * (qc + 1)
                for st in range(stl):
                    psb, bfree, bid = takeb()
                    def smm(t, hm=hm, hb=hb, qc=qc, st=st, psb=psb, bfree=bfree,
                            w=max(qkv_done, bnd_done), pd=prev_dpv):
                        t.wait_ge(plan.sems[vv], w)
                        t.wait_ge(plan.sems[mmx], pd)
                        t.wait_ge(plan.sems[bfree[0]], bfree[1])
                        return t.matmul(psb[:, :],
                                        kT[hm][hb:hb + 64, 128 * st:128 * (st + 1)],
                                        qT[hm][hb:hb + 64, 512 * qc:512 * (qc + 1)],
                                        start=True, stop=True)
                    plan.op("tensor", smm, inc=(mmx, 1))
                    def sev(s, st=st, psb=psb, mw=plan.val(mmx)):
                        s.wait_ge(plan.sems[mmx], mw)
                        return s.activation(et[st], psb[:, :], AF.Exp, bias=negb, scale=0.125)
                    plan.op("scalar", sev)
                    bank_free[bid] = ("sc_p", plan.val(ss))
                    j = st - 4 * qc
                    if 0 <= j:
                        def msk(v, st=st, j=j, w=plan.val(ss)):
                            v.wait_ge(plan.sems[ss], w)
                            return v.tensor_mul(et[st], et[st], cst["mask" + str(j)])
                        plan.op("vector", msk)
                e_done = (plan.val(ss), plan.val(vv))
                def dpv(t, h=h, qc=qc, stl=stl, ed=e_done):
                    t.wait_ge(plan.sems[ss], ed[0])
                    t.wait_ge(plan.sems[vv], ed[1])
                    for st in range(stl):
                        t.matmul(bufs.PS[4][0:1, 0:512], cst["ones_col_b"], et[st],
                                 start=(st == 0), stop=(st == stl - 1))
                    for st in range(stl):
                        ins = t.matmul(bufs.PS[5][0:64, 0:512], vt[st][:, 64 * h:64 * (h + 1)],
                                       et[st], start=(st == 0), stop=(st == stl - 1))
                    return ins
                plan.op("tensor", dpv, inc=(mmx, 1))
                prev_dpv = plan.val(mmx)
                def rec(v, mw=plan.val(mmx)):
                    v.wait_ge(plan.sems[mmx], mw)
                    return v.reciprocal(SM[0:1, 110:622], bufs.PS[4][0:1, 0:512])
                plan.op("vector", rec)
                def rbc(t, w=plan.val(vv)):
                    t.wait_ge(plan.sems[vv], w)
                    return t.matmul(bufs.PSW[0:64, 0:512], cst["ones_row_f"][:, 0:64],
                                    SM[0:1, 110:622], start=True, stop=True)
                plan.op("tensor", rbc, inc=(mmx, 1))
                def yev(v, h=h, qc=qc, mw=plan.val(mmx), pst=prev_y_store):
                    v.wait_ge(plan.sems[mmx], mw)
                    if pst:
                        v.wait_ge(plan.sems[ph + "y"], pst)
                    v.tensor_copy(rbst[0:64, :], bufs.PSW[0:64, 0:512])
                    return v.tensor_mul(ystg[0:64, :], bufs.PS[5][0:64, 0:512], rbst[0:64, :])
                plan.op("vector", yev)
                plan.op("sync", lambda sy, b=b, h=h, qc=qc, w=plan.val(vv): (
                    sy.wait_ge(plan.sems[vv], w),
                    sy.dma_start(out=dram["s_yatt"][64 * h:64 * (h + 1),
                                                    2048 * b + 512 * qc:2048 * b + 512 * (qc + 1)],
                                 in_=ystg[0:64, :]))[1], dma=ph + "y")
                prev_y_store = plan.val(ph + "y")
    plan.sem(ph + "o")
    for b in range(2):
        t0 = b * 2048
        plan.op("sync", lambda sy, t0=t0, mv=plan.val(mmx), yv=prev_y_store: (
            sy.wait_ge(plan.sems[ph + "y"], yv),
            sy.wait_ge(plan.sems[mmx], mv),
            sy.dma_start(out=a_bf[:, 0:16384].rearrange("p (dt t) -> p dt t", dt=8),
                         in_=fm_dram(dram["s_yatt"], t0, 2048)))[2], dma=ph + "ya")
        y_ld = plan.val(ph + "ya")
        plan.op("gpsimd", lambda g, mv=plan.val(mmx): (
            g.wait_ge(plan.sems[mmx], mv),
            g.dma_start(out=wbuf.rearrange("p (ct n) -> p ct n", ct=8),
                        in_=dram["w_o"].rearrange("(ct p) n -> p ct n", p=128)))[1],
            dma=ph + "w")
        wo_ld = plan.val(ph + "w")
        for m in range(8):
            for tc in range(4):
                psb, bfree, bid = takeb()
                def omm(t, m=m, tc=tc, psb=psb, bfree=bfree, y_ld=y_ld, wo_ld=wo_ld):
                    t.wait_ge(plan.sems[ph + "ya"], y_ld)
                    t.wait_ge(plan.sems[ph + "w"], wo_ld)
                    t.wait_ge(plan.sems[bfree[0]], bfree[1])
                    for ci in range(8):
                        ins = t.matmul(psb[:, :],
                                       wbuf[:, 1024 * ci + 128 * m:1024 * ci + 128 * (m + 1)],
                                       a_bf[:, 2048 * ci + 512 * tc:2048 * ci + 512 * (tc + 1)],
                                       start=(ci == 0), stop=(ci == 7))
                    return ins
                plan.op("tensor", omm, inc=(mmx, 1))
                ostg = b_bf[:, 24576 + 1024 * (bkc[0] % 2):24576 + 1024 * (bkc[0] % 2) + 512]
                def oev(v, psb=psb, ostg=ostg, mw=plan.val(mmx), pst=plan.val(ph + "o")):
                    v.wait_ge(plan.sems[mmx], mw)
                    if pst:
                        v.wait_ge(plan.sems[ph + "o"], pst)
                    return v.tensor_copy(ostg, psb[:, :])
                plan.op("vector", oev)
                bank_free[bid] = ("ve_p", plan.val(vv))
                plan.op("sync", lambda sy, m=m, tc=tc, t0=t0, ostg=ostg, w=plan.val(vv): (
                    sy.wait_ge(plan.sems[vv], w),
                    sy.dma_start(out=dram["s_attn1"].rearrange("(dt p) t -> dt p t", p=128)[m, :, t0 + 512 * tc:t0 + 512 * (tc + 1)],
                                 in_=ostg))[1], dma=ph + "o")
    return ph + "o", plan.val(ph + "o")



def emit_fp4(plan, bufs, ph, dram, deps=()):
    cst = plan.consts
    a_bf, b_bf, c_bf, w_bf = bufs.a_bf(), bufs.b_bf(), bufs.c_bf(), bufs.w_bf()
    A, Bb, C, SM = bufs.AR_A, bufs.AR_B, bufs.AR_C, bufs.SM
    mmx = ph + "mm"
    plan.sem(mmx)
    vv, ss = "ve_p", "sc_p"
    deps = list(deps) + list(plan.cst_wait)
    psb_t = bufs.PSW[:, :].bitcast(BF16)
    BW = 1408

    class QdqStream:

        def __init__(self, scr0, sct, sc0):
            self.scr0, self.sct, self.sc0 = scr0, sct, sc0
            self.i = 0
            self.pend = None

        def push(self, xf, out, F, wait_v=None, wait_s=None):
            p = self.i % 2
            self.i += 1
            b = self.scr0 + 3 * BW * p
            A1 = A[:, b:b + F]
            A2 = A[:, b + BW:b + BW + F]
            A3 = A[:, b + 2 * BW:b + 2 * BW + F]
            s0 = self.sc0 + 192 * p
            nb = F // 16
            sc1 = self.sct[:, s0:s0 + nb]
            sc2 = self.sct[:, s0 + 96:s0 + 96 + nb]
            wl = list(wait_v or [])

            def v1(v, wl=wl, xf=xf, F=F, A1=A1, sc1=sc1, sc2=sc2):
                for sn, sv in wl:
                    v.wait_ge(plan.sems[sn], sv)
                v.reduce_max(sc1, xf.rearrange("p (b k) -> p b k", k=16), axis=AX,
                             apply_absolute_value=True)
                v.tensor_scalar(sc1, sc1, 1.0 / 6.0, 1e-12, ALU.mult, ALU.max)
                v.reciprocal(sc2, sc1)
                v.tensor_scalar(sc1, sc1, 0.5, None, ALU.mult)
                return v.tensor_tensor(
                    A1.rearrange("p (b k) -> p b k", k=16),
                    xf.rearrange("p (b k) -> p b k", k=16),
                    sc2.rearrange("p (b k) -> p b k", k=1).broadcast_to([128, F // 16, 16]),
                    ALU.mult)
            plan.op("vector", v1)
            wsl = list(wait_s or []) + [(vv, plan.val(vv))]

            def s1(s, wsl=wsl, A1=A1, out=out):
                for sn, sv in wsl:
                    s.wait_ge(plan.sems[sn], sv)
                s.activation(out, A1, AF.Sign)
                return s.activation(A1, A1, AF.Abs)
            plan.op("scalar", s1)
            sd = plan.val(ss)

            def v2(v, sd=sd, A1=A1, A2=A2, A3=A3, sc1=sc1, out=out, F=F):
                v.wait_ge(plan.sems[ss], sd)
                v.tensor_scalar(A3, A1, 0.25, None, ALU.is_gt)
                v.scalar_tensor_tensor(A3, A1, 0.75, A3, ALU.is_gt, ALU.add)
                v.scalar_tensor_tensor(A3, A1, 1.25, A3, ALU.is_gt, ALU.add)
                v.scalar_tensor_tensor(A3, A1, 1.75, A3, ALU.is_gt, ALU.add)
                v.tensor_scalar(A2, A1, 2.5, None, ALU.is_gt)
                v.scalar_tensor_tensor(A2, A1, 3.5, A2, ALU.is_gt, ALU.add)
                v.scalar_tensor_tensor(A3, A2, 2.0, A3, ALU.mult, ALU.add)
                v.tensor_scalar(A2, A1, 5.0, None, ALU.is_gt)
                v.scalar_tensor_tensor(A3, A2, 4.0, A3, ALU.mult, ALU.add)
                v.tensor_mul(A3, A3, out)
                return v.tensor_tensor(
                    out.rearrange("p (b k) -> p b k", k=16),
                    A3.rearrange("p (b k) -> p b k", k=16),
                    sc1.rearrange("p (b k) -> p b k", k=1).broadcast_to([128, F // 16, 16]),
                    ALU.mult)
            prev = self.pend
            self.pend = v2
            if prev is not None:
                plan.op("vector", prev)

        def flush(self):
            if self.pend is not None:
                plan.op("vector", self.pend)
                self.pend = None

    wsld_sem, wst_sem = ph + "wl", ph + "ws"
    xs_sem, xl_sem = ph + "xs", ph + "xl"
    for s_ in (wsld_sem, wst_sem, xs_sem, xl_sem):
        plan.sem(s_)
    qsA = QdqStream(5632, A, 14080)
    state = {"xf_free": 0, "psw": None}
    ofree = {0: None, 1: None}
    lfree = {0: None, 1: None}
    pend_post = [None]

    def do_post():
        if pend_post[0] is None:
            return
        kind, args = pend_post[0]
        pend_post[0] = None
        q_done = plan.val(vv)
        if kind == "A":
            kp, dname, ti, F, out = args
            nblk = F // 128
            ldst = c_bf[:, 5632 * kp:5632 * kp + 128 * nblk]
            for g0 in range(0, nblk, 11):
                gn = min(11, nblk - g0)
                twl = [(vv, q_done)] + ([state["psw"]] if state["psw"] else [])

                def wtr(t, g0=g0, gn=gn, out=out, twl=twl):
                    for sn, sv in twl:
                        t.wait_ge(plan.sems[sn], sv)
                    for bj in range(gn):
                        ins = t.transpose(psb_t[:, 128 * bj:128 * (bj + 1)],
                                          out[:, 128 * (g0 + bj):128 * (g0 + bj + 1)],
                                          cst["ident"])
                    return ins
                plan.op("tensor", wtr, inc=(mmx, 1))
                evwl = [(mmx, plan.val(mmx))] + ([lfree[kp]] if (g0 == 0 and lfree[kp]) else [])

                def wev(s, g0=g0, gn=gn, ldst=ldst, evwl=evwl):
                    for sn, sv in evwl:
                        s.wait_ge(plan.sems[sn], sv)
                    return s.activation(ldst[:, 128 * g0:128 * (g0 + gn)],
                                        psb_t[:, 0:128 * gn], AF.Copy)
                plan.op("scalar", wev)
                state["psw"] = (ss, plan.val(ss))
            ofree[kp] = (mmx, plan.val(mmx))

            def wst(sy, dname=dname, ti=ti, nblk=nblk, ldst=ldst, w=plan.val(ss)):
                sy.wait_ge(plan.sems[ss], w)
                return sy.dma_start(
                    out=dram[dname].rearrange("(bj p2) i -> p2 bj i", p2=128)[:, :, 128 * ti:128 * (ti + 1)],
                    in_=ldst.rearrange("p (bj q) -> p bj q", bj=nblk))
            plan.op("sync", wst, dma=wst_sem)
            lfree[kp] = (wst_sem, plan.val(wst_sem))
        else:
            kp, k, out = args
            psq = bufs.PS[kp][:, :].bitcast(BF16)
            twl2 = [(vv, q_done)] + ([ps01_free[kp]] if ps01_free[kp] else [])

            def xqt(t, out=out, psq=psq, twl2=twl2):
                for sn, sv in twl2:
                    t.wait_ge(plan.sems[sn], sv)
                for ct in range(8):
                    ins = t.transpose(psq[:, 128 * ct:128 * (ct + 1)],
                                      out[:, 128 * ct:128 * (ct + 1)], cst["ident"])
                return ins
            plan.op("tensor", xqt, inc=(mmx, 1))
            ofree[kp] = (mmx, plan.val(mmx))
            evwl2 = [(mmx, plan.val(mmx))] + ([lfree[kp]] if lfree[kp] else [])
            ldst = c_bf[:, 5632 * kp:5632 * kp + 1024]

            def ev2(s, ldst=ldst, psq=psq, evwl2=evwl2):
                for sn, sv in evwl2:
                    s.wait_ge(plan.sems[sn], sv)
                return s.activation(ldst, psq[:, 0:1024], AF.Copy)
            plan.op("scalar", ev2)
            ps01_free[kp] = (ss, plan.val(ss))

            def xqst(sy, k=k, ldst=ldst, w=plan.val(ss)):
                sy.wait_ge(plan.sems[ss], w)
                return sy.dma_start(
                    out=dram["s_xq"].rearrange("(ct p2) t -> p2 ct t", p2=128)[:, :, 128 * k:128 * (k + 1)],
                    in_=ldst.rearrange("p (ct q) -> p ct q", ct=8))
            plan.op("sync", xqst, dma=xs_sem)
            lfree[kp] = (xs_sem, plan.val(xs_sem))

    tiles = []
    for wname, dname, ntile, F in (("w_gate", "s_wgq", 22, 1024),
                                   ("w_up", "s_wuq", 22, 1024),
                                   ("w_down", "s_wdq", 8, 2816)):
        for ti in range(ntile):
            tiles.append((wname, dname, ti, F))
    kg = 0
    for wname, dname, ti, F in tiles:
        p = kg % 2
        kg += 1
        xf = A[:, 0:F]
        out = a_bf[:, 5632 + 2816 * p:5632 + 2816 * p + F]
        wl = list(deps) + ([(vv, state["xf_free"])] if state["xf_free"] else [])
        plan.op("sync", lambda sy, wname=wname, ti=ti, xf=xf: sy.dma_start(
            out=xf, in_=dram[wname][128 * ti:128 * (ti + 1), :]),
            dma=wsld_sem, wait=wl)
        ld = plan.val(wsld_sem)
        cbs = [1408, 1408] if F == 2816 else [1024]
        cb0 = 0
        for j, cb in enumerate(cbs):
            qsA.push(xf[:, cb0:cb0 + cb], out[:, cb0:cb0 + cb], cb,
                     wait_v=([(wsld_sem, ld)] if j == 0 else None),
                     wait_s=([ofree[p]] if (j == 0 and ofree[p]) else None))
            cb0 += cb
        state["xf_free"] = plan.val(vv)
        do_post()
        pend_post[0] = ("A", (p, dname, ti, F, out))

    xcfree = {0: None, 1: None}
    xtok_free = {0: state["xf_free"], 1: state["xf_free"]}
    ps01_free = {0: None, 1: None}
    for k in range(32):
        p = kg % 2
        kg += 1
        xcol = b_bf[:, 1024 * p:1024 * (p + 1)]
        wl = list(deps) + ([xcfree[p]] if xcfree[p] else [])
        plan.op("sync", lambda sy, k=k, xcol=xcol: sy.dma_start(
            out=xcol.rearrange("p (ct q) -> p ct q", ct=8),
            in_=dram["s_xn1"].rearrange("(ct p2) t -> p2 ct t", p2=128)[:, :, 128 * k:128 * (k + 1)]),
            dma=xl_sem, wait=wl)
        xl = plan.val(xl_sem)
        twl = [(xl_sem, xl)] + ([state["psw"]] if state["psw"] else [])

        def xtr(t, xcol=xcol, twl=twl):
            for sn, sv in twl:
                t.wait_ge(plan.sems[sn], sv)
            for ct in range(8):
                ins = t.transpose(psb_t[:, 128 * ct:128 * (ct + 1)],
                                  xcol[:, 128 * ct:128 * (ct + 1)], cst["ident"])
            return ins
        plan.op("tensor", xtr, inc=(mmx, 1))
        xcfree[p] = (mmx, plan.val(mmx))
        xtok = A[:, 1024 * p:1024 * (p + 1)]
        evwl = [(mmx, plan.val(mmx)), (vv, xtok_free[p])]

        def ev1(s, xtok=xtok, evwl=evwl):
            for sn, sv in evwl:
                s.wait_ge(plan.sems[sn], sv)
            return s.activation(xtok, psb_t[:, 0:1024], AF.Copy)
        plan.op("scalar", ev1)
        state["psw"] = (ss, plan.val(ss))
        ev = plan.val(ss)
        out = a_bf[:, 5632 + 2816 * p:5632 + 2816 * p + 1024]
        qsA.push(xtok, out, 1024, wait_v=[(ss, ev)],
                 wait_s=([ofree[p]] if ofree[p] else None))
        xtok_free[p] = plan.val(vv)
        do_post()
        pend_post[0] = ("B", (p, k, out))
    qsA.flush()
    do_post()
    xq_done = plan.val(xs_sem)
    wA_done = plan.val(wst_sem)

    ICS = [512, 512, 512, 512, 512, 256]
    xq_t = [a_bf[:, 512 * i:512 * (i + 1)] for i in range(8)]
    gstb = [b_bf[:, 22528 + 512 * j:22528 + 512 * (j + 1)] for j in range(2)]
    wgq_s = [w_bf[:, 4096 * j:4096 * (j + 1)] for j in range(2)]
    wuq_s = [b_bf[:, 23552 + 4096 * j:23552 + 4096 * (j + 1)] for j in range(2)]
    wdq_s = [w_bf[:, 8192 + 2816 * j:8192 + 2816 * (j + 1)] for j in range(2)]
    hqT_t = [c_bf[:, 512 * i:512 * (i + 1)] for i in range(22)]
    htok = A[:, 2048:4864]
    hqb = [a_bf[:, 9728 + 2816 * j:9728 + 2816 * (j + 1)] for j in range(2)]
    x1stg = [C[:, 5632 + 512 * j:5632 + 512 * (j + 1)] for j in range(2)]
    atstg = [c_bf[:, 13312 + 512 * j:13312 + 512 * (j + 1)] for j in range(2)]
    outstg = [C[:, 7168 + 512 * j:7168 + 512 * (j + 1)] for j in range(2)]
    qsC = QdqStream(7680, SM, 1024)

    def h_t(hp, tt):
        return b_bf[:, 11264 * hp + 2816 * tt:11264 * hp + 2816 * (tt + 1)]

    for s_ in (ph + "xg", ph + "wg", ph + "uq", ph + "wd", ph + "r", ph + "o"):
        plan.sem(s_)
    nbank = 4
    bank_free = [None] * nbank
    bkc = [0]

    def takeb():
        b = bkc[0] % nbank
        bkc[0] += 1
        return bufs.PS[b], bank_free[b], b

    S = {"gu_end": 0, "htok_free": 0}
    h_free_sc = [0, 0]
    wg_mm = [0, 0]
    wu_mm = [0, 0]
    gst_vv = [0, 0]
    hq_free = [0, 0]
    hev_done = {}
    h_done_vv = {}
    dmm_mm = [0, 0]
    ps45_free = [0, 0]
    x1_free = [0, 0]
    out_free = [0, 0]
    gcnt = [0]
    dcnt = [0]

    def piece_push(tg1, tt):
        hp = tg1 % 2
        tp = tt % 2
        wl = [(vv, max(h_done_vv[tg1], S["htok_free"]))]

        def hcp(s, wl=wl, src=h_t(hp, tt)):
            for sn, sv in wl:
                s.wait_ge(plan.sems[sn], sv)
            return s.activation(htok, src, AF.Copy)
        plan.op("scalar", hcp)
        hc = plan.val(ss)
        for bi, cb in enumerate((0, 1408)):
            qsC.push(htok[:, cb:cb + 1408], hqb[tp][:, cb:cb + 1408], 1408,
                     wait_v=([(ss, hc)] if bi == 0 else None),
                     wait_s=([(mmx, hq_free[tp])] if (bi == 0 and hq_free[tp]) else None))
        S["htok_free"] = plan.val(vv)

    def piece_post(tg1, tt):
        tp = tt % 2
        q_done = plan.val(vv)
        for half in range(2):
            twl = [(vv, q_done)] + ([state["psw"]] if state["psw"] else [])

            def htr(t, half=half, twl=twl, src=hqb[tp]):
                for sn, sv in twl:
                    t.wait_ge(plan.sems[sn], sv)
                for bj in range(11):
                    ins = t.transpose(psb_t[:, 128 * bj:128 * (bj + 1)],
                                      src[:, 1408 * half + 128 * bj:1408 * half + 128 * (bj + 1)],
                                      cst["ident"])
                return ins
            plan.op("tensor", htr, inc=(mmx, 1))
            mw = plan.val(mmx)

            def hev(s, half=half, tt=tt, mw=mw):
                s.wait_ge(plan.sems[mmx], mw)
                for bj in range(11):
                    ins = s.activation(hqT_t[11 * half + bj][:, 128 * tt:128 * (tt + 1)],
                                       psb_t[:, 128 * bj:128 * (bj + 1)], AF.Copy)
                return ins
            plan.op("scalar", hev)
            state["psw"] = (ss, plan.val(ss))
        hq_free[tp] = plan.val(mmx)
        hev_done[tg1] = plan.val(ss)

    def emit_gateup(tg):
        t0 = 512 * tg
        hp = tg % 2
        wlx = [(xs_sem, xq_done), (mmx, S["gu_end"])]

        def xqld(sy, t0=t0, wlx=wlx):
            for sn, sv in wlx:
                sy.wait_ge(plan.sems[sn], sv)
            return sy.dma_start(out=a_bf[:, 0:4096].rearrange("p (ct q) -> p ct q", ct=8),
                                in_=fm_dram(dram["s_xq"], t0, 512))
        plan.op("sync", xqld, dma=ph + "xg")
        xg = plan.val(ph + "xg")
        ic0 = 0
        for ici, icsz in enumerate(ICS):
            p = ici % 2
            wlg = [(mmx, wg_mm[p])] + ((deps + [(wst_sem, wA_done)]) if tg == 0 else [])

            def wgld(g, ic0=ic0, icsz=icsz, p=p, wlg=wlg):
                for sn, sv in wlg:
                    g.wait_ge(plan.sems[sn], sv)
                return g.dma_start(
                    out=wgq_s[p].rearrange("p2 (ct q) -> p2 ct q", ct=8)[:, :, 0:icsz],
                    in_=dram["s_wgq"].rearrange("(ct p2) i -> p2 ct i", p2=128)[:, :, ic0:ic0 + icsz])
            plan.op("gpsimd", wgld, dma=ph + "wg")
            wgl = plan.val(ph + "wg")
            wlu = [(mmx, wu_mm[p])] + ((deps + [(wst_sem, wA_done)]) if tg == 0 else [])

            def wuld(sy, ic0=ic0, icsz=icsz, p=p, wlu=wlu):
                for sn, sv in wlu:
                    sy.wait_ge(plan.sems[sn], sv)
                return sy.dma_start(
                    out=wuq_s[p].rearrange("p2 (ct q) -> p2 ct q", ct=8)[:, :, 0:icsz],
                    in_=dram["s_wuq"].rearrange("(ct p2) i -> p2 ct i", p2=128)[:, :, ic0:ic0 + icsz])
            plan.op("sync", wuld, dma=ph + "uq")
            wul = plan.val(ph + "uq")
            for tt in range(4):
                j = gcnt[0] % 2
                gcnt[0] += 1
                psg, bfg, bidg = takeb()

                def gmm(t, tt=tt, icsz=icsz, p=p, psg=psg, bfg=bfg, wgl=wgl, xg=xg):
                    t.wait_ge(plan.sems[ph + "wg"], wgl)
                    t.wait_ge(plan.sems[ph + "xg"], xg)
                    if bfg:
                        t.wait_ge(plan.sems[bfg[0]], bfg[1])
                    for ci in range(8):
                        ins = t.matmul(psg[:, 0:icsz], xq_t[ci][:, 128 * tt:128 * (tt + 1)],
                                       wgq_s[p][:, 512 * ci:512 * ci + icsz],
                                       start=(ci == 0), stop=(ci == 7))
                    return ins
                plan.op("tensor", gmm, inc=(mmx, 1))
                gm = plan.val(mmx)

                def silu(s, j=j, icsz=icsz, psg=psg, gm=gm, gw=gst_vv[j]):
                    s.wait_ge(plan.sems[mmx], gm)
                    s.wait_ge(plan.sems[vv], gw)
                    return s.activation(gstb[j][:, 0:icsz], psg[:, 0:icsz], AF.Silu)
                plan.op("scalar", silu)
                bank_free[bidg] = (ss, plan.val(ss))
                sl = plan.val(ss)
                psu, bfu, bidu = takeb()

                def umm(t, tt=tt, icsz=icsz, p=p, psu=psu, bfu=bfu, wul=wul, xg=xg):
                    t.wait_ge(plan.sems[ph + "uq"], wul)
                    t.wait_ge(plan.sems[ph + "xg"], xg)
                    if bfu:
                        t.wait_ge(plan.sems[bfu[0]], bfu[1])
                    for ci in range(8):
                        ins = t.matmul(psu[:, 0:icsz], xq_t[ci][:, 128 * tt:128 * (tt + 1)],
                                       wuq_s[p][:, 512 * ci:512 * ci + icsz],
                                       start=(ci == 0), stop=(ci == 7))
                    return ins
                plan.op("tensor", umm, inc=(mmx, 1))
                um = plan.val(mmx)

                def hset(v, tt=tt, ic0=ic0, icsz=icsz, j=j, psu=psu, um=um, sl=sl,
                         hf=h_free_sc[hp], hp=hp):
                    v.wait_ge(plan.sems[mmx], um)
                    v.wait_ge(plan.sems[ss], max(sl, hf))
                    return v.tensor_mul(h_t(hp, tt)[:, ic0:ic0 + icsz],
                                        gstb[j][:, 0:icsz], psu[:, 0:icsz])
                plan.op("vector", hset)
                gst_vv[j] = plan.val(vv)
                bank_free[bidu] = (vv, plan.val(vv))
            wg_mm[p] = plan.val(mmx)
            wu_mm[p] = plan.val(mmx)
            if tg > 0:
                if ici < 4:
                    piece_push(tg - 1, ici)
                if 1 <= ici <= 4:
                    piece_post(tg - 1, ici - 1)
            ic0 += icsz
        if tg > 0:
            qsC.flush()
            piece_post(tg - 1, 3)
        h_done_vv[tg] = plan.val(vv)
        h_free_sc[(tg - 1) % 2] = plan.val(ss) if tg > 0 else 0
        S["gu_end"] = plan.val(mmx)

    def emit_down(tg1):
        t0 = 512 * tg1
        for dt in range(8):
            p = dcnt[0] % 2
            dcnt[0] += 1
            wld = [(mmx, dmm_mm[p])] + ((deps + [(wst_sem, wA_done)]) if tg1 == 0 else [])

            def wdld(g, dt=dt, p=p, wld=wld):
                for sn, sv in wld:
                    g.wait_ge(plan.sems[sn], sv)
                return g.dma_start(
                    out=wdq_s[p].rearrange("p2 (it q) -> p2 it q", it=22),
                    in_=dram["s_wdq"].rearrange("(it p2) d -> p2 it d", p2=128)[:, :, 128 * dt:128 * (dt + 1)])
            plan.op("gpsimd", wdld, dma=ph + "wd")
            wdl = plan.val(ph + "wd")
            wlr = [(vv, x1_free[p])] + (deps if (tg1 == 0 and dt < 2) else [])

            def x1ld(sy, dt=dt, p=p, t0=t0, wlr=wlr):
                for sn, sv in wlr:
                    sy.wait_ge(plan.sems[sn], sv)
                sy.dma_start(out=x1stg[p],
                             in_=dram["s_x1"].rearrange("(q p2) t -> q p2 t", p2=128)[dt, :, t0:t0 + 512])
                return sy.dma_start(out=atstg[p],
                                    in_=dram["s_attn1"].rearrange("(q p2) t -> q p2 t", p2=128)[dt, :, t0:t0 + 512])
            plan.op("sync", x1ld, dma=ph + "r")
            rl = plan.val(ph + "r")
            psd = bufs.PS[4 + p]

            def dmm(t, p=p, wdl=wdl, hv=hev_done[tg1], bf=ps45_free[p], psd=psd):
                t.wait_ge(plan.sems[ph + "wd"], wdl)
                t.wait_ge(plan.sems[ss], hv)
                t.wait_ge(plan.sems[vv], bf)
                for it in range(22):
                    ins = t.matmul(psd[:, :], wdq_s[p][:, 128 * it:128 * (it + 1)],
                                   hqT_t[it], start=(it == 0), stop=(it == 21))
                return ins
            plan.op("tensor", dmm, inc=(mmx, 1))
            dmm_mm[p] = plan.val(mmx)

            def fev(v, p=p, mw=plan.val(mmx), rl=rl, of=out_free[p], psd=psd):
                v.wait_ge(plan.sems[mmx], mw)
                v.wait_ge(plan.sems[ph + "r"], rl)
                v.wait_ge(plan.sems[ph + "o"], of)
                v.tensor_add(outstg[p], psd[:, :], x1stg[p])
                return v.tensor_add(outstg[p], outstg[p], atstg[p])
            plan.op("vector", fev)
            ps45_free[p] = plan.val(vv)
            x1_free[p] = plan.val(vv)
            plan.op("sync", lambda sy, dt=dt, t0=t0, p=p, w=plan.val(vv): (
                sy.wait_ge(plan.sems[vv], w),
                sy.dma_start(out=dram["outT"].rearrange("(q p2) t -> q p2 t", p2=128)[dt, :, t0:t0 + 512],
                             in_=outstg[p]))[1], dma=ph + "o")
            out_free[p] = plan.val(ph + "o")

    for tg in range(9):
        if tg < 8:
            emit_gateup(tg)
        else:
            for tt in range(4):
                piece_push(7, tt)
                if tt > 0:
                    piece_post(7, tt - 1)
            qsC.flush()
            piece_post(7, 3)
        if tg > 0:
            emit_down(tg - 1)
    return ph + "o", plan.val(ph + "o")



def emit_kgather(plan, dram, wait):
    rs = plan.sem("kbc")
    sems = plan.sems

    def go(g):
        for s, v in wait:
            g.wait_ge(sems[s], v)
        g.collective_compute(
            "AllGather", mybir.AluOpType.bypass,
            replica_groups=[[0, 1, 2, 3, 4, 5, 6, 7]],
            ins=[dram["s_kslice"]], outs=[dram["s_kag"]],
        ).then_inc(rs, 1)
    plan.emit("gpsimd", go)
    plan.cnt["kbc"] += 1
    plan.dma_sems.add("kbc")
    return "kbc", 1


def barrier(plan):
    return [(s, c) for s, c in plan.cnt.items() if c > 0]



SCRATCH_SHAPES = {
    "s_xn": ([D, T], BF16), "s_xnT": ([T, D], BF16),
    "s_prm": ([4, 8192], F32), "s_kslice": ([L, HSL], BF16),
    "s_kag": ([8 * L, HSL], BF16),
    "s_uf": ([2, 2, NFP, D], BF16), "s_kf": ([2, NFP, D], BF16),
    "s_g": ([2, L, D], BF16), "s_glu": ([D, T], BF16),
    "s_attn0": ([D, T], BF16),
    "s_cwT": ([E, T], F32), "s_x1": ([D, T], F32),
    "s_xn1": ([D, T], BF16),
    "s_yatt": ([D, T], BF16), "s_attn1": ([D, T], BF16),
    "s_wgq": ([D, FFI], BF16), "s_wuq": ([D, FFI], BF16), "s_wdq": ([FFI, D], BF16),
    "s_xq": ([D, T], BF16),
    "outT": ([D, T], F32),
}


def _scr(names, outs):
    return {n: (SCRATCH_SHAPES[n][0], SCRATCH_SHAPES[n][1],
                "ExternalOutput" if n in outs else
                ("ExternalInput" if n.startswith("s_") and n in names and n in names else "Internal"))
            for n in names}


def _mk_scratch(ins, internal, outs):
    d = {}
    for n in ins:
        d[n] = (SCRATCH_SHAPES[n][0], SCRATCH_SHAPES[n][1], "ExternalInput")
    for n in internal:
        d[n] = (SCRATCH_SHAPES[n][0], SCRATCH_SHAPES[n][1], "Internal")
    for n in outs:
        d[n] = (SCRATCH_SHAPES[n][0], SCRATCH_SHAPES[n][1], "ExternalOutput")
    return d


def kernel(**inputs):
    in_maps = host_prepare(inputs)
    EI = "ExternalInput"

    def fused(plan, bufs, dram):
        emit_ln(plan, bufs, "ln1", dram["x0T"], dram["s_xn"], dstT=dram["s_xnT"],
                wait_st=plan.cst_wait)
        ln1_st = ("ln1st", plan.val("ln1st"))
        kst, kstv = emit_ssm_k(plan, bufs, "k", dram)
        emit_kgather(plan, dram, wait=[(kst, kstv)])
        b1 = [ln1_st, ("kbc", 1)]
        s_lns, v_lns = emit_l0attn(plan, bufs, dram, deps=b1)
        b2 = [(s_lns, v_lns)]
        emit_moe(plan, bufs, "mo", dram, deps=b2)
        b3 = [("most", plan.val("most"))]
        emit_ln(plan, bufs, "ln2", dram["s_x1"], dram["s_xn1"], wait_st=b3)
        b4 = [("ln2st", plan.val("ln2st"))]
        emit_attn(plan, bufs, "at", dram, deps=b4)
        b5 = [("ato", plan.val("ato")), ("ln2st", plan.val("ln2st"))]
        emit_fp4(plan, bufs, "f4", dram, deps=b5)

    io_kinds = {n: EI for n in input_specs() if not n.startswith("s_")}
    internal = [n for n in SCRATCH_SHAPES if n != "outT"]
    nc = build([fused], io_kinds,
               extra_scratch=_mk_scratch([], internal, ["outT"]))
    r = run_bass_kernel_spmd(nc, in_maps, core_ids=list(range(NCORES)))
    out = np.empty((16, 2048, 1024), dtype=np.float32)
    for c in range(NCORES):
        out[2 * c:2 * c + 2] = np.asarray(r.results[c]["outT"]).T.reshape(2, L, D)
    return out


# version 10
# speedup vs baseline: 1.1382x; 1.0013x over previous
import math
import contextlib
import numpy as np
import ml_dtypes
import concourse.bass as bass
import concourse.mybir as mybir
from concourse.bass_utils import run_bass_kernel_spmd

F32 = mybir.dt.float32
BF16 = mybir.dt.bfloat16
I32 = mybir.dt.int32
AF = mybir.ActivationFunctionType
ALU = mybir.AluOpType
AX = mybir.AxisListType.X

NCORES = 8
D = 1024
DT = 8
B = 2
L = 2048
T = B * L
NF = 2049
NFP = 2176
N2 = 4096
SN = 64
HSL = 128
E = 8
I_MOE = 4096
SH_I = 2048
NH = 16
DH = 64
FFI = 2816
EPS = 1e-5

GRID_C = [0.5, 0.5, 0.5, 0.5, 1.0, 1.0, 2.0]
MID = [0.25, 0.75, 1.25, 1.75, 2.5, 3.5, 5.0]



ENGINES = ("tensor", "vector", "scalar", "sync", "gpsimd")


class MockIns:
    def then_inc(self, sem, k):
        return self


class MockEng:

    def __init__(self):
        self.n = 0
        self.ndma = 0

    def wait_ge(self, *a, **k):
        pass

    def dma_start(self, *a, **k):
        self.ndma += 1
        return MockIns()

    def indirect_dma_start(self, *a, **k):
        self.ndma += 1
        return MockIns()

    def __getattr__(self, name):
        def f(*a, **k):
            self.n += 1
            return MockIns()
        return f


CHAIN_SEM = {"vector": "ve_p", "scalar": "sc_p", "gpsimd": "gp_p"}
CHAIN_OPS_EXCLUDE = {"wait_ge", "dma_start", "indirect_dma_start", "then_inc"}


class ChainProxy:

    def __init__(self, eng, sem, start, dma_sem=None):
        self._eng = eng
        self._sem = sem
        self._cnt = start
        self._dma = dma_sem

    def wait_ge(self, sem, v):
        return self._eng.wait_ge(sem, v)

    def dma_start(self, *a, **k):
        ins = self._eng.dma_start(*a, **k)
        if self._dma is not None:
            ins.then_inc(self._dma, 16)
        return ins

    def indirect_dma_start(self, *a, **k):
        ins = self._eng.indirect_dma_start(*a, **k)
        if self._dma is not None:
            ins.then_inc(self._dma, 16)
        return ins

    def __getattr__(self, name):
        f = getattr(self._eng, name)
        if name.startswith("_") or name in CHAIN_OPS_EXCLUDE or not callable(f):
            return f

        def wrapped(*a, **k):
            if self._sem is None:
                return f(*a, **k)
            if self._cnt > 0:
                self._eng.wait_ge(self._sem, self._cnt)
            ins = f(*a, **k)
            ins.then_inc(self._sem, 1)
            self._cnt += 1
            return ins
        return wrapped


class Plan:

    def __init__(self, nc, es):
        self.nc = nc
        self.es = es
        self.ops = {e: [] for e in ENGINES}
        self.sems = {}
        self.cnt = {}
        self.dma_sems = set()
        for s in CHAIN_SEM.values():
            self.sem(s)

    def sem(self, name):
        if name not in self.sems:
            self.sems[name] = self.es.enter_context(self.nc.semaphore(name))
            self.cnt[name] = 0
        return self.sems[name]

    def emit(self, eng, fn):
        self.ops[eng].append(fn)

    def inc(self, name, k=1):
        self.sem(name)
        self.cnt[name] += k
        return self.cnt[name]

    def val(self, name):
        self.sem(name)
        return self.cnt[name]

    def op(self, eng, fn, inc=None, wait=None, dma=None):
        sems = self.sems
        mock = MockEng()
        fn(mock)
        chain = CHAIN_SEM.get(eng)
        base = None
        if chain is not None:
            base = self.cnt[chain]
            self.cnt[chain] += mock.n
            if inc is not None and mock.n > 0:
                raise RuntimeError("no explicit inc on chained compute closures")
        if mock.ndma > 0:
            assert dma is not None and inc is None, "DMA closures need dma= sem"
            self.inc(dma, 16 * mock.ndma)
            self.dma_sems.add(dma)
        if inc is not None:
            self.inc(inc[0], inc[1])
        wl = list(wait or [])
        dsem = self.sem(dma) if dma is not None else None

        nexp = mock.n

        def run(ex, fn=fn, wl=wl, inc=inc, chain=chain, base=base, dsem=dsem, nexp=nexp):
            px = ChainProxy(ex, sems.get(chain), base or 0, dma_sem=dsem)
            for s, v in wl:
                px.wait_ge(sems[s], v)
            ins = fn(px)
            if chain is not None:
                assert px._cnt == (base or 0) + nexp, \
                    f"op count mismatch: expected {nexp}, got {px._cnt - (base or 0)} at {fn}"
            if inc is not None:
                ins.then_inc(sems[inc[0]], inc[1])
        self.emit(eng, run)

    def finalize(self, block):
        for eng in ENGINES:
            ops = self.ops[eng]
            if not ops:
                continue
            dec = getattr(block, eng)

            def section(ex, ops=ops):
                for fn in ops:
                    fn(ex)
            dec(section)


def cdiv(a, b):
    return (a + b - 1) // b



def _bf(x):
    return np.ascontiguousarray(x.astype(ml_dtypes.bfloat16))


def _f32(x):
    return np.ascontiguousarray(x, dtype=np.float32)


def host_consts():
    c = {}
    ll = np.arange(L, dtype=np.float64)
    ff = np.arange(NF, dtype=np.float64)
    th = 2.0 * np.pi * np.outer(ll, ff) / N2
    twc = np.zeros((L, NFP)); twc[:, :NF] = np.cos(th)
    tws = np.zeros((L, NFP)); tws[:, :NF] = -np.sin(th)
    c["twc_t"] = _bf(twc.reshape(L, 17, 128).transpose(1, 0, 2))
    c["tws_t"] = _bf(tws.reshape(L, 17, 128).transpose(1, 0, 2))
    a = np.full(NF, 2.0 / N2); a[0] = 1.0 / N2; a[-1] = 1.0 / N2
    thi = th.T
    twic = np.zeros((NFP, L)); twic[:NF] = np.cos(thi) * a[:, None]
    twis = np.zeros((NFP, L)); twis[:NF] = -np.sin(thi) * a[:, None]
    c["twic_t"] = _bf(twic.reshape(NFP, 16, 128).transpose(1, 0, 2))
    c["twis_t"] = _bf(twis.reshape(NFP, 16, 128).transpose(1, 0, 2))
    c["ident"] = _bf(np.eye(128))
    c["ident_f"] = _f32(np.eye(128))
    c["ones_col_f"] = _f32(np.ones((128, 1)))
    c["ones_col_b"] = _bf(np.ones((128, 1)))
    c["ones_row_f"] = _f32(np.ones((1, 128)))
    c["ones_row_b"] = _bf(np.ones((1, 128)))
    c["iota_col"] = _f32(np.arange(128, dtype=np.float32).reshape(128, 1))
    s_i = np.arange(128)[:, None]
    q_i = np.arange(512)[None, :]
    for j in range(4):
        c[f"mask{j}"] = _bf((s_i <= q_i - 128 * j).astype(np.float32))
    return c


def host_prepare(inputs):
    consts = host_consts()
    x = np.asarray(inputs["L0_attn_x"])
    shared = {}
    shared.update(consts)
    shared["ssm_D_row"] = _f32(np.asarray(inputs["L0_attn_D"]).reshape(1, D))
    shared["ssm_gamma"] = _f32(np.asarray(inputs["L0_attn_gamma"]).reshape(D, 1))
    shared["ssm_beta"] = _f32(np.asarray(inputs["L0_attn_beta"]).reshape(D, 1))
    shared["w_out"] = _f32(np.asarray(inputs["L0_attn_W_out"]))
    shared["b_out"] = _f32(np.asarray(inputs["L0_attn_b_out"]).reshape(2 * D, 1))
    shared["gate_wT"] = _f32(np.asarray(inputs["L0_ffn_gate_w"]).T)
    shared["moe_wg"] = _f32(np.asarray(inputs["L0_ffn_wg"]))
    shared["moe_wu"] = _f32(np.asarray(inputs["L0_ffn_wu"]))
    shared["moe_wd"] = _f32(np.asarray(inputs["L0_ffn_wd"]))
    shared["ws_g"] = _f32(np.asarray(inputs["L0_ffn_ws_g"]))
    shared["ws_u"] = _f32(np.asarray(inputs["L0_ffn_ws_u"]))
    shared["ws_d"] = _f32(np.asarray(inputs["L0_ffn_ws_d"]))
    shared["wqT"] = _f32(np.asarray(inputs["L1_attn_w_q"]).reshape(D, D).T)
    shared["wkT"] = _f32(np.asarray(inputs["L1_attn_w_k"]).reshape(D, D).T)
    shared["wvT"] = _f32(np.asarray(inputs["L1_attn_w_v"]).reshape(D, D).T)
    shared["w_o"] = _f32(np.asarray(inputs["L1_attn_w_o"]))
    shared["w_gate"] = _f32(np.asarray(inputs["L1_ffn_w_gate"]))
    shared["w_up"] = _f32(np.asarray(inputs["L1_ffn_w_up"]))
    shared["w_down"] = _f32(np.asarray(inputs["L1_ffn_w_down"]))

    A_re = np.asarray(inputs["L0_attn_A_re"]); A_im = np.asarray(inputs["L0_attn_A_im"])
    C_re = np.asarray(inputs["L0_attn_C_re"]); C_im = np.asarray(inputs["L0_attn_C_im"])
    log_dt = np.asarray(inputs["L0_attn_log_dt"])

    in_maps = []
    for c in range(NCORES):
        m = dict(shared)
        xc = x[2 * c:2 * c + 2].reshape(T, D).T
        m["x0T"] = _f32(xc)
        hs = slice(HSL * c, HSL * (c + 1))
        m["ssm_A_re"] = _f32(A_re[hs]); m["ssm_A_im"] = _f32(A_im[hs])
        m["ssm_C_re"] = _f32(C_re[hs]); m["ssm_C_im"] = _f32(C_im[hs])
        m["ssm_log_dt"] = _f32(log_dt[hs].reshape(HSL, 1))
        in_maps.append(m)
    return in_maps


def input_specs():
    sp = {
        "x0T": ([D, T], F32),
        "twc_t": ([17, L, 128], BF16), "tws_t": ([17, L, 128], BF16),
        "twic_t": ([16, NFP, 128], BF16), "twis_t": ([16, NFP, 128], BF16),
        "ident": ([128, 128], BF16), "ident_f": ([128, 128], F32),
        "ones_col_f": ([128, 1], F32), "ones_col_b": ([128, 1], BF16),
        "ones_row_f": ([1, 128], F32), "ones_row_b": ([1, 128], BF16),
        "iota_col": ([128, 1], F32),
        "mask0": ([128, 512], BF16), "mask1": ([128, 512], BF16),
        "mask2": ([128, 512], BF16), "mask3": ([128, 512], BF16),
        "ssm_A_re": ([HSL, SN], F32), "ssm_A_im": ([HSL, SN], F32),
        "ssm_C_re": ([HSL, SN], F32), "ssm_C_im": ([HSL, SN], F32),
        "ssm_log_dt": ([HSL, 1], F32),
        "ssm_D_row": ([1, D], F32),
        "ssm_gamma": ([D, 1], F32), "ssm_beta": ([D, 1], F32),
        "w_out": ([D, 2 * D], F32), "b_out": ([2 * D, 1], F32),
        "gate_wT": ([D, E], F32),
        "moe_wg": ([E, D, I_MOE], F32), "moe_wu": ([E, D, I_MOE], F32),
        "moe_wd": ([E, I_MOE, D], F32),
        "ws_g": ([D, SH_I], F32), "ws_u": ([D, SH_I], F32), "ws_d": ([SH_I, D], F32),
        "wqT": ([D, D], F32), "wkT": ([D, D], F32), "wvT": ([D, D], F32),
        "w_o": ([D, D], F32),
        "w_gate": ([FFI, D], F32), "w_up": ([FFI, D], F32), "w_down": ([D, FFI], F32),
    }
    return sp



class Bufs:
    def __init__(self, nc, es):
        self.nc = nc
        self.AR_A = es.enter_context(nc.sbuf_tensor("AR_A", [128, 16384], F32))
        self.AR_B = es.enter_context(nc.sbuf_tensor("AR_B", [128, 16384], F32))
        self.AR_C = es.enter_context(nc.sbuf_tensor("AR_C", [128, 8192], F32))
        self.AR_W = es.enter_context(nc.sbuf_tensor("AR_W", [128, 7168], F32))
        self.SM = es.enter_context(nc.sbuf_tensor("SM", [128, 1536], F32))
        self.PS = [es.enter_context(nc.psum_tensor(f"PS{i}", [128, 512], F32))
                   for i in range(6)]
        self.PSW = es.enter_context(nc.psum_tensor("PSW", [128, 1024], F32))

    def a_bf(self):
        return self.AR_A[:, :].bitcast(BF16)

    def b_bf(self):
        return self.AR_B[:, :].bitcast(BF16)

    def c_bf(self):
        return self.AR_C[:, :].bitcast(BF16)

    def w_bf(self):
        return self.AR_W[:, :].bitcast(BF16)


def fm_dram(ap, tc0, tn, p=128):
    d = ap.shape[0]
    return ap.rearrange("(dt p) t -> p dt t", p=p)[:, :, tc0:tc0 + tn]



def emit_ln(plan, bufs, ph, src, dst, dstT=None, gamma=None, beta=None,
            src_f32=True, wait_st=None):
    nc = plan.nc
    cst = plan.consts
    CH = 512
    nch = T // CH
    A, Bb = bufs.AR_A, bufs.AR_B
    a_bf = bufs.a_bf(); b_bf = bufs.b_bf()
    PS_SUM, PS_SQ, PS_R, PS_MR = bufs.PS[0], bufs.PS[1], bufs.PS[2], bufs.PS[3]
    SM = bufs.SM
    ld, vv, ss, mmx, stx = ph + "ld", "ve_p", "sc_p", ph + "mm", ph + "st"
    for s in (ld, mmx, stx):
        plan.sem(s)

    deps = list(wait_st or [])
    if gamma is not None:
        plan.op("sync", lambda sy, g=gamma: sy.dma_start(
            out=SM[:, 8:16], in_=g.rearrange("(dt p) o -> p (dt o)", p=128),
            allow_slow_non_contiguous=True), dma=ld, wait=deps)
        plan.op("sync", lambda sy, b=beta: sy.dma_start(
            out=SM[:, 16:24], in_=b.rearrange("(dt p) o -> p (dt o)", p=128),
            allow_slow_non_contiguous=True), dma=ld)

    prev_norm = 0
    prev_statfin = 0
    prev_store = 0
    prev_evac = 0
    C = bufs.AR_C
    mean = C[0:1, 1024:1536]
    ex2 = C[0:1, 1536:2048]
    rstd = C[0:1, 2048:2560]
    xa = A[:, 0:4096]
    xav = xa.rearrange("p (dt t) -> p dt t", dt=8)
    xnreg = b_bf[:, 8192:8192 + 4096]

    for c in range(nch):
        t0 = c * CH
        w = deps + ([] if c == 0 else [(vv, prev_norm)])
        if src_f32:
            plan.op("sync", lambda sy, t0=t0: sy.dma_start(
                out=xav, in_=fm_dram(src, t0, CH)), dma=ld, wait=w)
            ld_x = plan.val(ld)
            v_pre = plan.val(vv)
        else:
            xab = a_bf[:, 8192:8192 + 4096].rearrange("p (dt t) -> p dt t", dt=8)
            plan.op("sync", lambda sy, t0=t0, xab=xab: sy.dma_start(
                out=xab, in_=fm_dram(src, t0, CH)), dma=ld, wait=w)
            ld_x = plan.val(ld)
            plan.op("vector", lambda v, xab=xab: v.tensor_copy(xav, xab),
                    wait=[(ld, ld_x)])
            v_pre = plan.val(vv)
        plan.op("scalar", lambda s: s.activation(Bb[:, 0:4096], xa, AF.Square),
                wait=deps + [(ld, ld_x), (vv, v_pre)])
        sq_done = plan.val(ss)
        def stats(t, ld_x=ld_x, v_pre=v_pre, sq_done=sq_done, pf=prev_statfin, dp=list(deps)):
            for s, v in dp:
                t.wait_ge(plan.sems[s], v)
            t.wait_ge(plan.sems[ld], ld_x)
            t.wait_ge(plan.sems[vv], max(v_pre, pf))
            for i in range(8):
                t.matmul(PS_SUM[0:1, :], cst["ones_col_f"], A[:, 512 * i:512 * (i + 1)],
                         start=(i == 0), stop=(i == 7))
            t.wait_ge(plan.sems[ss], sq_done)
            for i in range(8):
                ins = t.matmul(PS_SQ[0:1, :], cst["ones_col_f"], Bb[:, 512 * i:512 * (i + 1)],
                               start=(i == 0), stop=(i == 7))
            return ins
        plan.op("tensor", stats, inc=(mmx, 1))
        mm_stats = plan.val(mmx)
        def statfin_v(v, mm_stats=mm_stats):
            v.wait_ge(plan.sems[mmx], mm_stats)
            v.tensor_scalar(mean, PS_SUM[0:1, :], 1.0 / D, None, ALU.mult)
            v.tensor_scalar(ex2, PS_SQ[0:1, :], 1.0 / D, None, ALU.mult)
            v.tensor_mul(rstd, mean, mean)
            return v.tensor_sub(ex2, ex2, rstd)
        plan.op("vector", statfin_v)
        def statfin_s(s, vw=plan.val(vv)):
            s.wait_ge(plan.sems[vv], vw)
            s.activation(rstd, ex2, AF.Ln, bias=EPS)
            return s.activation(rstd, rstd, AF.Exp, scale=-0.5)
        plan.op("scalar", statfin_s)
        def statfin2_v(v, sw=plan.val(ss)):
            v.wait_ge(plan.sems[ss], sw)
            v.tensor_mul(mean, mean, rstd)
            return v.tensor_scalar(mean, mean, -1.0, None, ALU.mult)
        plan.op("vector", statfin2_v)
        prev_statfin = plan.val(vv)
        def bcast(t, vw=prev_statfin, pn=prev_norm):
            t.wait_ge(plan.sems[vv], max(vw, pn))
            t.matmul(PS_R[:, :], cst["ones_row_f"], rstd, start=True, stop=True)
            return t.matmul(PS_MR[:, :], cst["ones_row_f"], mean, start=True, stop=True)
        plan.op("tensor", bcast, inc=(mmx, 1))
        mm_b = plan.val(mmx)
        def norm(v, mm_b=mm_b, ps=prev_store):
            v.wait_ge(plan.sems[mmx], mm_b)
            if ps:
                v.wait_ge(plan.sems[stx], ps)
            for i in range(8):
                xi = A[:, 512 * i:512 * (i + 1)]
                ti = Bb[:, 512 * i:512 * (i + 1)]
                xo = xnreg[:, 512 * i:512 * (i + 1)]
                v.tensor_mul(ti, xi, PS_R[:, :])
                if gamma is None:
                    ins = v.tensor_add(xo, ti, PS_MR[:, :])
                else:
                    ti2 = bufs.AR_C[:, 0:512]
                    v.tensor_add(ti2, ti, PS_MR[:, :])
                    v.tensor_scalar(ti, ti2, SM[:, 8 + i:9 + i], None, ALU.mult)
                    ins = v.tensor_scalar(xo, ti, SM[:, 16 + i:17 + i], None, ALU.add)
            return ins
        plan.op("vector", norm)
        prev_norm = plan.val(vv)
        plan.op("sync", lambda sy, t0=t0: sy.dma_start(
            out=fm_dram(dst, t0, CH),
            in_=xnreg.rearrange("p (dt t) -> p dt t", dt=8)),
            wait=[(vv, prev_norm)], dma=stx)
        if dstT is not None:
            psb = bufs.PSW[:, :].bitcast(BF16)
            for j in range(4):
                def trans(t, j=j, nw=prev_norm, pe=prev_evac):
                    t.wait_ge(plan.sems[vv], max(nw, pe))
                    for i in range(8):
                        ins = t.transpose(psb[:, 128 * i:128 * (i + 1)],
                                          xnreg[:, 512 * i + 128 * j:512 * i + 128 * (j + 1)],
                                          cst["ident"])
                    return ins
                plan.op("tensor", trans, inc=(mmx, 1))
                treg = b_bf[:, 12288 + 1024 * j:12288 + 1024 * (j + 1)]
                def evac(v, mw=plan.val(mmx), j=j, pst=prev_store, treg=treg):
                    v.wait_ge(plan.sems[mmx], mw)
                    if pst:
                        v.wait_ge(plan.sems[stx], pst)
                    return v.tensor_copy(treg, psb[:, 0:1024])
                plan.op("vector", evac)
                prev_evac = plan.val(vv)
                plan.op("sync", lambda sy, t0=t0, j=j, treg=treg: sy.dma_start(
                    out=dstT[t0 + 128 * j:t0 + 128 * (j + 1), :], in_=treg),
                    wait=[(vv, prev_evac)], dma=stx)
        prev_store = plan.val(stx)
    return stx, plan.val(stx)



CONST_NAMES = ["ident", "ident_f", "ones_col_f", "ones_col_b", "ones_row_f", "ones_row_b",
               "iota_col", "mask0", "mask1", "mask2", "mask3"]

FLOAT_CONSTS = [EPS]


def load_consts(plan, es, dram):
    nc = plan.nc
    shapes = input_specs()
    plan.consts = {}
    for n in CONST_NAMES:
        sh, dt = shapes[n]
        t = es.enter_context(nc.sbuf_tensor("cst_" + n, sh, dt))
        plan.consts[n] = t[:, :]
        plan.op("gpsimd", lambda g, n=n, t=t, sh=sh: g.dma_start(
            out=t[0:sh[0], 0:sh[1]], in_=dram[n]), dma="cstld")
    for i, v in enumerate(FLOAT_CONSTS):
        t = es.enter_context(nc.sbuf_tensor(f"fc_{i}", [128, 1], F32))
        plan.op("gpsimd", lambda g, t=t, v=v: g.memset(t[:, :], v))
        nc.const_aps.register(F32, v, t[:, :])
    return [("cstld", plan.val("cstld")), ("gp_p", plan.val("gp_p"))]



def build(phase_fns, io_kinds, extra_scratch=None):
    nc = bass.Bass("TRN2", target_bir_lowering=False)
    specs = input_specs()
    io_kinds = dict(io_kinds)
    for n in CONST_NAMES:
        io_kinds.setdefault(n, "ExternalInput")
    es = contextlib.ExitStack()
    with es:
        dram = {}
        for n, kind in io_kinds.items():
            sh, dt = specs[n]
            dram[n] = nc.dram_tensor(n, sh, dt, kind=kind).ap()
        for n, (sh, dt, kind) in (extra_scratch or {}).items():
            dram[n] = nc.dram_tensor(n, sh, dt, kind=kind).ap()
        bufs = Bufs(nc, es)
        plan = Plan(nc, es)
        cw = load_consts(plan, es, dram)
        plan.cst_wait = cw
        for fn in phase_fns:
            fn(plan, bufs, dram)
        final = [(s, plan.cnt[s]) for s in sorted(plan.dma_sems) if plan.cnt[s] > 0]
        def fin(sy):
            for s, v in final:
                sy.wait_ge(plan.sems[s], v)
        plan.emit("sync", fin)
        block = es.enter_context(nc.Block())
        plan.finalize(block)
    return nc



TWO_PI = float(2.0 * np.pi)
INV_2PI = float(1.0 / (2.0 * np.pi))
HALF_PI = float(np.pi / 2.0)



def emit_rr(v, theta, out, iv, s1, s2):
    v.tensor_scalar(out, theta, INV_2PI, None, ALU.mult)
    v.tensor_copy(iv, out)
    v.tensor_copy(s1, iv)
    v.tensor_sub(s2, out, s1)
    v.tensor_scalar(out, s2, 0.5, None, ALU.is_gt)
    v.tensor_scalar(s2, s2, -0.5, None, ALU.is_lt)
    v.tensor_sub(out, out, s2)
    v.tensor_add(s1, s1, out)
    v.scalar_tensor_tensor(out, s1, -TWO_PI, theta, ALU.mult, ALU.add)
    return v.tensor_scalar(out, out, float(np.pi), -float(np.pi), ALU.min, ALU.max)


def emit_ssm_k(plan, bufs, ph, dram):
    cst = plan.consts
    SM, A = bufs.SM, bufs.AR_A
    b_bf = bufs.b_bf()
    a_bf = bufs.a_bf()
    ld, vv, ss, mmx, stx = ph + "ld", "ve_p", "sc_p", ph + "mm", ph + "st"
    for s in (ld, mmx, stx):
        plan.sem(s)
    cols = {"A_re": 0, "A_im": 64, "C_re": 128, "C_im": 192}
    for nm, c0 in cols.items():
        plan.op("sync", lambda sy, nm=nm, c0=c0: sy.dma_start(
            out=SM[:, c0:c0 + 64], in_=dram["ssm_" + nm]), dma=ld,
            wait=plan.cst_wait)
    plan.op("sync", lambda sy: sy.dma_start(
        out=SM[:, 256:257], in_=dram["ssm_log_dt"]), dma=ld)
    ldv = plan.val(ld)
    Are, Aim = SM[:, 0:64], SM[:, 64:128]
    Cre, Cim = SM[:, 128:192], SM[:, 192:256]
    logdt, dt = SM[:, 256:257], SM[:, 257:258]
    rho, phi = SM[:, 320:384], SM[:, 384:448]
    ctre, ctim = SM[:, 448:512], SM[:, 512:576]
    t1, t2, t3, t4 = SM[:, 576:640], SM[:, 640:704], SM[:, 704:768], SM[:, 768:832]
    er, s_, c_ = SM[:, 832:896], SM[:, 896:960], SM[:, 960:1024]
    iv64 = SM[:, 1024:1088].bitcast(I32)

    plan.op("scalar", lambda s: s.activation(dt, logdt, AF.Exp),
            wait=[(ld, ldv)])
    def vstep1(v, sw=plan.val(ss)):
        v.wait_ge(plan.sems[ss], sw)
        v.tensor_scalar(rho, Are, dt, None, ALU.mult)
        return v.tensor_scalar(phi, Aim, dt, None, ALU.mult)
    plan.op("vector", vstep1)
    def vstep2(v, w=plan.val(vv)):
        v.wait_ge(plan.sems[vv], w)
        return emit_rr(v, phi, t2, iv64, t3, t4)
    plan.op("vector", vstep2)
    plan.op("scalar", lambda s: s.activation(s_, t2, AF.Sin),
            wait=[(vv, plan.val(vv))])
    def vstep3(v, w=plan.val(ss)):
        v.wait_ge(plan.sems[ss], w)
        v.tensor_scalar(t1, phi, HALF_PI, None, ALU.add)
        return emit_rr(v, t1, t3, iv64, t2, t4)
    plan.op("vector", vstep3)
    def sstep2(s, w=plan.val(vv)):
        s.wait_ge(plan.sems[vv], w)
        s.activation(c_, t3, AF.Sin)
        return s.activation(er, rho, AF.Exp)
    plan.op("scalar", sstep2)
    zre, zim = SM[:, 1088:1152], SM[:, 1152:1216]
    def vstep4(v, w=plan.val(ss)):
        v.wait_ge(plan.sems[ss], w)
        v.tensor_mul(zre, er, c_)
        v.tensor_mul(zim, er, s_)
        v.tensor_scalar(t1, zre, -1.0, None, ALU.add)
        v.tensor_copy(t2, zim)
        v.tensor_mul(t3, Are, Are)
        v.tensor_mul(t4, Aim, Aim)
        v.tensor_add(t3, t3, t4)
        v.reciprocal(t3, t3)
        v.tensor_mul(t4, t1, Are)
        v.tensor_mul(er, t2, Aim)
        v.tensor_add(t4, t4, er)
        v.tensor_mul(t4, t4, t3)
        v.tensor_mul(er, t2, Are)
        v.tensor_mul(s_, t1, Aim)
        v.tensor_sub(er, er, s_)
        v.tensor_mul(er, er, t3)
        v.tensor_mul(ctre, Cre, t4)
        v.tensor_mul(s_, Cim, er)
        v.tensor_sub(ctre, ctre, s_)
        v.tensor_scalar(ctre, ctre, 2.0, None, ALU.mult)
        v.tensor_mul(ctim, Cre, er)
        v.tensor_mul(s_, Cim, t4)
        v.tensor_add(ctim, ctim, s_)
        return v.tensor_scalar(ctim, ctim, 2.0, None, ALU.mult)
    plan.op("vector", vstep4)

    NCH = 32
    LC = 64
    Pre = A[:, 0:4096]
    Pim = A[:, 4096:8192]
    Qb = A[:, 8192:12288]
    Tb = A[:, 12288:16384]
    kT = b_bf[:, 18432:20480]
    zc = [(SM[:, 1216:1280], SM[:, 1280:1344]),
          (SM[:, 1344:1408], SM[:, 1408:1472])]
    z64re, z64im = zc[0]

    def v3(apf, l0, ln):
        return apf.rearrange("p (n l) -> p n l", n=SN)[:, :, l0:l0 + ln]

    def bc3(z, ln):
        return z.rearrange("p (n o) -> p n o", o=1).broadcast_to([128, SN, ln])

    def vinit(v):
        v.memset(v3(Pre, 0, 1), 1.0)
        v.memset(v3(Pim, 0, 1), 0.0)
        v.tensor_copy(zc[0][0], zre)
        v.tensor_copy(zc[0][1], zim)
        for k in range(6):
            m = 1 << k
            zkre, zkim = zc[k % 2]
            znre, znim = zc[(k + 1) % 2]
            v.tensor_tensor(v3(Tb, 0, m), v3(Pre, 0, m), bc3(zkre, m), ALU.mult)
            v.tensor_tensor(v3(Qb, 0, m), v3(Pim, 0, m), bc3(zkim, m), ALU.mult)
            v.tensor_sub(v3(Pre, m, m), v3(Tb, 0, m), v3(Qb, 0, m))
            v.tensor_tensor(v3(Tb, 0, m), v3(Pim, 0, m), bc3(zkre, m), ALU.mult)
            v.tensor_tensor(v3(Qb, 0, m), v3(Pre, 0, m), bc3(zkim, m), ALU.mult)
            v.tensor_add(v3(Pim, m, m), v3(Tb, 0, m), v3(Qb, 0, m))
            v.tensor_mul(t1, zkre, zkre)
            v.tensor_mul(t2, zkim, zkim)
            v.tensor_sub(znre, t1, t2)
            v.tensor_mul(t1, zkre, zkim)
            ins = v.tensor_scalar(znim, t1, 2.0, None, ALU.mult)
        return ins
    plan.op("vector", vinit)

    for c in range(NCH):
        def kchunk(v, c=c):
            v.tensor_tensor(Qb.rearrange("p (n l) -> p n l", n=SN),
                            Pre.rearrange("p (n l) -> p n l", n=SN),
                            bc3(ctre, LC), ALU.mult)
            v.tensor_tensor(Tb.rearrange("p (n l) -> p n l", n=SN),
                            Pim.rearrange("p (n l) -> p n l", n=SN),
                            bc3(ctim, LC), ALU.mult)
            v.tensor_sub(Qb, Qb, Tb)
            Qv = Qb.rearrange("p (n l) -> p n l", n=SN)
            h = SN
            while h > 1:
                h //= 2
                v.tensor_add(Qv[:, 0:h, :], Qv[:, 0:h, :], Qv[:, h:2 * h, :])
            ins = v.tensor_copy(kT[:, LC * c:LC * (c + 1)], Qb[:, 0:LC])
            if c == NCH - 1:
                return ins
            v.tensor_tensor(Tb.rearrange("p (n l) -> p n l", n=SN),
                            Pre.rearrange("p (n l) -> p n l", n=SN),
                            bc3(z64re, LC), ALU.mult)
            v.tensor_tensor(Qb.rearrange("p (n l) -> p n l", n=SN),
                            Pim.rearrange("p (n l) -> p n l", n=SN),
                            bc3(z64im, LC), ALU.mult)
            v.tensor_sub(Tb, Tb, Qb)
            v.tensor_tensor(Qb.rearrange("p (n l) -> p n l", n=SN),
                            Pre.rearrange("p (n l) -> p n l", n=SN),
                            bc3(z64im, LC), ALU.mult)
            v.tensor_tensor(Pim.rearrange("p (n l) -> p n l", n=SN),
                            Pim.rearrange("p (n l) -> p n l", n=SN),
                            bc3(z64re, LC), ALU.mult)
            v.tensor_add(Pim, Pim, Qb)
            return v.tensor_copy(Pre, Tb)
        plan.op("vector", kchunk)
    kT_done = plan.val(vv)

    ktile = b_bf[:, 16384:18432]
    psb = bufs.PSW[:, :].bitcast(BF16)
    for g in range(2):
        def ktr(t, g=g, w=kT_done, pv=plan.val(vv)):
            t.wait_ge(plan.sems[vv], max(w, pv))
            for j in range(8):
                ins = t.transpose(psb[:, 128 * j:128 * (j + 1)],
                                  kT[:, 1024 * g + 128 * j:1024 * g + 128 * (j + 1)],
                                  cst["ident"])
            return ins
        plan.op("tensor", ktr, inc=(mmx, 1))
        def kev(v, g=g, mw=plan.val(mmx)):
            v.wait_ge(plan.sems[mmx], mw)
            return v.tensor_copy(ktile[:, 1024 * g:1024 * (g + 1)], psb[:, 0:1024])
        plan.op("vector", kev)
    prev_red = plan.val(vv)
    for lt in range(16):
        plan.op("sync", lambda sy, lt=lt: sy.dma_start(
            out=dram["s_kslice"][128 * lt:128 * (lt + 1), :],
            in_=ktile[:, 128 * lt:128 * (lt + 1)]),
            wait=[(vv, prev_red)], dma=stx)
    return stx, plan.val(stx)


def emit_ssm_k_old(plan, bufs, ph, dram):
    nc = plan.nc
    cst = plan.consts
    SM, A, Bb, C = bufs.SM, bufs.AR_A, bufs.AR_B, bufs.AR_C
    b_bf = bufs.b_bf()
    ld, vv, ss, mmx, stx = ph + "ld", "ve_p", "sc_p", ph + "mm", ph + "st"
    for s in (ld, mmx, stx):
        plan.sem(s)
    cols = {"A_re": 0, "A_im": 64, "C_re": 128, "C_im": 192}
    for nm, c0 in cols.items():
        plan.op("sync", lambda sy, nm=nm, c0=c0: sy.dma_start(
            out=SM[:, c0:c0 + 64], in_=dram["ssm_" + nm]), dma=ld,
            wait=plan.cst_wait)
    plan.op("sync", lambda sy: sy.dma_start(
        out=SM[:, 256:257], in_=dram["ssm_log_dt"]), dma=ld)
    ldv = plan.val(ld)
    Are, Aim = SM[:, 0:64], SM[:, 64:128]
    Cre, Cim = SM[:, 128:192], SM[:, 192:256]
    logdt, dt = SM[:, 256:257], SM[:, 257:258]
    rho, phi = SM[:, 320:384], SM[:, 384:448]
    ctre, ctim = SM[:, 448:512], SM[:, 512:576]
    t1, t2, t3, t4 = SM[:, 576:640], SM[:, 640:704], SM[:, 704:768], SM[:, 768:832]
    er, s_, c_ = SM[:, 832:896], SM[:, 896:960], SM[:, 960:1024]
    iv64 = SM[:, 1024:1088].bitcast(I32)

    plan.op("scalar", lambda s: s.activation(dt, logdt, AF.Exp),
            wait=[(ld, ldv)])
    def vstep1(v, sw=plan.val(ss)):
        v.wait_ge(plan.sems[ss], sw)
        v.tensor_scalar(rho, Are, dt, None, ALU.mult)
        return v.tensor_scalar(phi, Aim, dt, None, ALU.mult)
    plan.op("vector", vstep1)
    def vstep2(v, w=plan.val(vv)):
        v.wait_ge(plan.sems[vv], w)
        return emit_rr(v, phi, t2, iv64, t3, t4)
    plan.op("vector", vstep2)
    plan.op("scalar", lambda s: s.activation(s_, t2, AF.Sin),
            wait=[(vv, plan.val(vv))])
    def vstep3(v, w=plan.val(ss)):
        v.wait_ge(plan.sems[ss], w)
        v.tensor_scalar(t1, phi, HALF_PI, None, ALU.add)
        return emit_rr(v, t1, t3, iv64, t2, t4)
    plan.op("vector", vstep3)
    def sstep2(s, w=plan.val(vv)):
        s.wait_ge(plan.sems[vv], w)
        s.activation(c_, t3, AF.Sin)
        return s.activation(er, rho, AF.Exp)
    plan.op("scalar", sstep2)
    def vstep4(v, w=plan.val(ss)):
        v.wait_ge(plan.sems[ss], w)
        v.tensor_mul(t1, er, c_)
        v.tensor_mul(t2, er, s_)
        v.tensor_scalar(t1, t1, -1.0, None, ALU.add)
        v.tensor_mul(t3, Are, Are)
        v.tensor_mul(t4, Aim, Aim)
        v.tensor_add(t3, t3, t4)
        v.reciprocal(t3, t3)
        v.tensor_mul(t4, t1, Are)
        v.tensor_mul(er, t2, Aim)
        v.tensor_add(t4, t4, er)
        v.tensor_mul(t4, t4, t3)
        v.tensor_mul(er, t2, Are)
        v.tensor_mul(s_, t1, Aim)
        v.tensor_sub(er, er, s_)
        v.tensor_mul(er, er, t3)
        v.tensor_mul(ctre, Cre, t4)
        v.tensor_mul(s_, Cim, er)
        v.tensor_sub(ctre, ctre, s_)
        v.tensor_scalar(ctre, ctre, 2.0, None, ALU.mult)
        v.tensor_mul(ctim, Cre, er)
        v.tensor_mul(s_, Cim, t4)
        v.tensor_add(ctim, ctim, s_)
        return v.tensor_scalar(ctim, ctim, 2.0, None, ALU.mult)
    plan.op("vector", vstep4)
    prm_done = plan.val(vv)
    for k, src in enumerate([phi, rho, ctre, ctim]):
        plan.op("sync", lambda sy, k=k, src=src: sy.dma_start(
            out=dram["s_prm"][k:k + 1, :].rearrange("o (p n) -> p (o n)", p=128),
            in_=src), wait=[(vv, prm_done)], dma=stx)
    prm_st = plan.val(stx)

    ktile = [b_bf[:, 16384 + 128 * lt:16384 + 128 * (lt + 1)] for lt in range(16)]
    W1, W2, W3 = A[:, 0:2048], A[:, 2048:4096], A[:, 4096:6144]
    W4, W5, W6 = A[:, 6144:8192], A[:, 8192:10240], A[:, 10240:12288]
    IV = A[:, 12288:14336].bitcast(I32)
    liota = SM[:, 1088:1089]
    prev_red = 0
    prev_bc_mm = 0
    for hc in range(4):
        plan.op("sync", lambda sy, hc=hc: sy.dma_start(
            out=C[0:1, 0:8192].rearrange("p (k n) -> p k n", k=4),
            in_=dram["s_prm"].rearrange("k (c n) -> c k n", n=2048)[hc:hc + 1]),
            wait=[(stx, prm_st), (mmx, prev_bc_mm)], dma=ld)
        ldh = plan.val(ld)
        bcs = [Bb[:, 0:2048], Bb[:, 2048:4096], Bb[:, 4096:6144], Bb[:, 6144:8192]]
        for k in range(4):
            def bc1(t, k=k, ldh=ldh, pr=prev_red, pe=plan.val(vv)):
                t.wait_ge(plan.sems[ld], ldh)
                if pr:
                    t.wait_ge(plan.sems[vv], pr)
                t.wait_ge(plan.sems[vv], pe)
                for ch in range(4):
                    ins = t.matmul(bufs.PS[ch][:, :], cst["ones_row_f"],
                                   C[0:1, 2048 * k + 512 * ch:2048 * k + 512 * (ch + 1)],
                                   start=True, stop=True)
                return ins
            plan.op("tensor", bc1, inc=(mmx, 1))
            def ev1(v, k=k, mw=plan.val(mmx)):
                v.wait_ge(plan.sems[mmx], mw)
                for ch in range(4):
                    ins = v.tensor_copy(bcs[k][:, 512 * ch:512 * (ch + 1)], bufs.PS[ch][:, :])
                return ins
            plan.op("vector", ev1)
        bc_done = plan.val(vv)
        prev_bc_mm = plan.val(mmx)
        phib, rhob, ctreb, ctimb = bcs

        for lt in range(16):
            l0 = lt * 128
            def kcomp_v1(v, l0=l0, bc_done=bc_done):
                v.wait_ge(plan.sems[vv], bc_done)
                v.tensor_scalar(liota, cst["iota_col"], 1.0, float(l0), ALU.mult, ALU.add)
                v.tensor_scalar(W1, phib, liota, None, ALU.mult)
                return emit_rr(v, W1, W2, IV, W3, W6)
            plan.op("vector", kcomp_v1)
            def kcomp_s1(s, w=plan.val(vv)):
                s.wait_ge(plan.sems[vv], w)
                return s.activation(W3, W2, AF.Sin)
            plan.op("scalar", kcomp_s1)
            def kcomp_v2(v, w=plan.val(ss)):
                v.wait_ge(plan.sems[ss], w)
                v.tensor_scalar(W2, W1, HALF_PI, None, ALU.add)
                emit_rr(v, W2, W4, IV, W5, W6)
                return v.tensor_scalar(W5, rhob, liota, None, ALU.mult)
            plan.op("vector", kcomp_v2)
            def kcomp_s2(s, w=plan.val(vv)):
                s.wait_ge(plan.sems[vv], w)
                s.activation(W4, W4, AF.Sin)
                return s.activation(W5, W5, AF.Exp)
            plan.op("scalar", kcomp_s2)
            def kcomp_v3(v, w=plan.val(ss), lt=lt, hc=hc):
                v.wait_ge(plan.sems[ss], w)
                v.tensor_mul(W6, ctreb, W4)
                v.tensor_mul(W1, ctimb, W3)
                v.tensor_sub(W6, W6, W1)
                v.tensor_mul(W6, W6, W5)
                v.reduce_sum(W2[:, 0:32], W6.rearrange("p (h n) -> p h n", n=64), axis=AX)
                return v.tensor_copy(ktile[lt][:, 32 * hc:32 * (hc + 1)], W2[:, 0:32])
            plan.op("vector", kcomp_v3)
        prev_red = plan.val(vv)
    for lt in range(16):
        plan.op("sync", lambda sy, lt=lt: sy.dma_start(
            out=dram["s_kslice"][128 * lt:128 * (lt + 1), :], in_=ktile[lt]),
            wait=[(vv, prev_red)], dma=stx)
    return stx, plan.val(stx)



def emit_dft_fwd(plan, bufs, ph, dram, with_k=True, deps=()):
    cst = plan.consts
    a_bf, b_bf, c_bf, w_bf = bufs.a_bf(), bufs.b_bf(), bufs.c_bf(), bufs.w_bf()
    ld, mmx, stx = ph + "ld", ph + "mm", ph + "st"
    for s in (ld, mmx, stx):
        plan.sem(s)
    vv = "ve_p"
    deps = list(deps) + list(plan.cst_wait)
    nsrc = 3 if with_k else 2
    xt = [[a_bf[:, 16384 * b + 1024 * lt:16384 * b + 1024 * (lt + 1)] for lt in range(16)]
          for b in range(2)]
    kt = [c_bf[:, 1024 * lt:1024 * (lt + 1)] for lt in range(16)]
    ldin = ph + "in"
    for b in range(2):
        plan.op("sync", lambda sy, b=b: sy.dma_start(
            out=a_bf[:, 16384 * b:16384 * (b + 1)].rearrange("p (lt n) -> p lt n", lt=16),
            in_=dram["s_xnT"].rearrange("(b lt p) n -> b lt p n", b=2, p=128)[b]),
            dma=ldin, wait=deps)
    if with_k:
        plan.op("sync", lambda sy: sy.dma_start(
            out=c_bf[:, 0:16384].rearrange("p (lt r h) -> p lt r h", lt=16, r=8),
            in_=dram["s_kag"].rearrange("(r lt p) h -> lt p r h", r=8, p=128)),
            dma=ldin, wait=deps)
    ld_in = plan.val(ldin)
    twbuf = [[w_bf[:, 4096 * bb + 2048 * cs:4096 * bb + 2048 * (cs + 1)]
              for cs in range(2)] for bb in range(2)]
    stage = [b_bf[:, 1024 * i:1024 * (i + 1)] for i in range(4)]
    prev_grp_ev = [0, 0]
    prev_tw_mm = [0, 0]
    prev_store_all = 0
    first_mm = True
    gi = 0
    for ft in range(17):
        bb = ft % 2
        def twload(sy, ft=ft, bb=bb):
            sy.dma_start(
                out=twbuf[bb][0].rearrange("p (lt n) -> p lt n", lt=16),
                in_=dram["twc_t"][ft].rearrange("(lt p) n -> lt p n", p=128))
            return sy.dma_start(
                out=twbuf[bb][1].rearrange("p (lt n) -> p lt n", lt=16),
                in_=dram["tws_t"][ft].rearrange("(lt p) n -> lt p n", p=128))
        plan.op("sync", twload, dma=ld + str(bb), wait=[(mmx, prev_tw_mm[bb])] + deps)
        tw_ld = plan.val(ld + str(bb))
        for si in range(nsrc):
            src = xt[si] if si < 2 else kt
            g = gi % 2
            gi += 1
            if g == 0:
                ps_re = [bufs.PSW[:, 0:512], bufs.PSW[:, 512:1024]]
                ps_im = [bufs.PS[0][:, :], bufs.PS[1][:, :]]
            else:
                ps_re = [bufs.PS[2][:, :], bufs.PS[3][:, :]]
                ps_im = [bufs.PS[4][:, :], bufs.PS[5][:, :]]
            wl = [(ld + str(bb), tw_ld), (ldin, ld_in), (vv, prev_grp_ev[g])]
            if first_mm:
                wl += deps
                first_mm = False
            def mmg(t, src=src, bb=bb, ps_re=ps_re, ps_im=ps_im):
                for cs in range(2):
                    ps = ps_re if cs == 0 else ps_im
                    for nch in range(2):
                        for lt in range(16):
                            ins = t.matmul(ps[nch],
                                           twbuf[bb][cs][:, 128 * lt:128 * (lt + 1)],
                                           src[lt][:, 512 * nch:512 * (nch + 1)],
                                           start=(lt == 0), stop=(lt == 15))
                return ins
            plan.op("tensor", mmg, inc=(mmx, 1), wait=wl)
            if si == nsrc - 1:
                prev_tw_mm[bb] = plan.val(mmx)
            def ev(v, g=g, ps_re=ps_re, ps_im=ps_im):
                v.tensor_copy(stage[2 * g][:, 0:512], ps_re[0])
                v.tensor_copy(stage[2 * g][:, 512:1024], ps_re[1])
                v.tensor_copy(stage[2 * g + 1][:, 0:512], ps_im[0])
                return v.tensor_copy(stage[2 * g + 1][:, 512:1024], ps_im[1])
            plan.op("vector", ev, wait=[(mmx, plan.val(mmx)), (stx, prev_store_all)])
            prev_grp_ev[g] = plan.val(vv)
            if si < 2:
                dsl = dram["s_uf"][si]
            else:
                dsl = dram["s_kf"]
            def store(sy, ft=ft, g=g, dsl=dsl):
                sy.dma_start(out=dsl[0, 128 * ft:128 * (ft + 1), :], in_=stage[2 * g])
                return sy.dma_start(out=dsl[1, 128 * ft:128 * (ft + 1), :], in_=stage[2 * g + 1])
            plan.op("sync", store, dma=stx, wait=[(vv, prev_grp_ev[g])])
            if gi % 2 == 0:
                prev_store_all = plan.val(stx)
    return stx, plan.val(stx)



def emit_conv(plan, bufs, ph, dram, deps=()):
    cst = plan.consts
    a_bf, b_bf, c_bf, w_bf = bufs.a_bf(), bufs.b_bf(), bufs.c_bf(), bufs.w_bf()
    Bb = bufs.AR_B
    ld, mmx, stx = ph + "ld", ph + "mm", ph + "st"
    for s in (ld, mmx, stx):
        plan.sem(s)
    vv, ss = "ve_p", "sc_p"
    deps = list(deps) + list(plan.cst_wait)
    ustg = b_bf[:, 4096:8192]
    s1f, s2f = Bb[:, 4096:5120], Bb[:, 5120:6144]
    dbc = Bb[:, 6144:7168]
    y2r, t1r = Bb[:, 8192:9216], Bb[:, 9216:10240]
    gout = b_bf[:, 20480:21504]

    def vf_t(ft, cs):
        if ft < 16:
            return a_bf[:, 2048 * ft + 1024 * cs:2048 * ft + 1024 * (cs + 1)]
        return b_bf[:, 1024 * cs:1024 * (cs + 1)]

    plan.op("sync", lambda sy: sy.dma_start(out=Bb[0:1, 7168:8192], in_=dram["ssm_D_row"]),
            dma=ph + "d", wait=deps)
    def dbc_mm(t):
        t.matmul(bufs.PS[0][:, :], cst["ones_row_f"], Bb[0:1, 7168:7680], start=True, stop=True)
        return t.matmul(bufs.PS[1][:, :], cst["ones_row_f"], Bb[0:1, 7680:8192], start=True, stop=True)
    plan.op("tensor", dbc_mm, inc=(mmx, 1), wait=[(ph + "d", plan.val(ph + "d"))] + deps)
    def dbc_ev(v):
        v.tensor_copy(dbc[:, 0:512], bufs.PS[0][:, :])
        return v.tensor_copy(dbc[:, 512:1024], bufs.PS[1][:, :])
    plan.op("vector", dbc_ev, wait=[(mmx, plan.val(mmx))])

    prev_itw_mm = [0, 0]
    prev_imm_all = 0
    prev_epi = 0
    prev_gstore = 0
    for b in range(2):
        for ft in range(17):
            def ldt(sy, b=b, ft=ft):
                sy.dma_start(out=ustg[:, 0:2048].rearrange("p (c n) -> p c n", c=2),
                             in_=dram["s_uf"][b].rearrange("c (ft p) n -> ft p c n", p=128)[ft])
                return sy.dma_start(out=ustg[:, 2048:4096].rearrange("p (c n) -> p c n", c=2),
                                    in_=dram["s_kf"].rearrange("c (ft p) n -> ft p c n", p=128)[ft])
            plan.op("sync", ldt, dma=ld,
                    wait=[(vv, plan.val(vv))] + deps)
            ur, ui = ustg[:, 0:1024], ustg[:, 1024:2048]
            kr, ki = ustg[:, 2048:3072], ustg[:, 3072:4096]
            def vfc(v, w=plan.val(ld), ft=ft, mprev=prev_imm_all):
                v.wait_ge(plan.sems[ld], w)
                v.wait_ge(plan.sems[mmx], mprev)
                v.tensor_mul(s1f, ur, kr)
                v.tensor_mul(s2f, ui, ki)
                v.tensor_sub(vf_t(ft, 0), s1f, s2f)
                v.tensor_mul(s1f, ur, ki)
                v.tensor_mul(s2f, ui, kr)
                return v.tensor_add(vf_t(ft, 1), s1f, s2f)
            plan.op("vector", vfc)
        vf_done = plan.val(vv)
        xn_t = [c_bf[:, 1024 * lt:1024 * (lt + 1)] for lt in range(16)]
        plan.op("sync", lambda sy, b=b: sy.dma_start(
            out=c_bf[:, 0:16384].rearrange("p (lt n) -> p lt n", lt=16),
            in_=dram["s_xnT"].rearrange("(b lt p) n -> b p lt n", b=2, p=128)[b]),
            dma=ph + "x", wait=[(vv, prev_epi)] + deps)
        xn_ld = plan.val(ph + "x")
        for lt in range(16):
            bb = 0
            def itwload(sy, lt=lt):
                sy.dma_start(
                    out=w_bf[:, 0:2176].rearrange("p (ft n) -> p ft n", ft=17),
                    in_=dram["twic_t"][lt].rearrange("(ft p) n -> p ft n", p=128))
                return sy.dma_start(
                    out=w_bf[:, 2176:4352].rearrange("p (ft n) -> p ft n", ft=17),
                    in_=dram["twis_t"][lt].rearrange("(ft p) n -> p ft n", p=128))
            plan.op("sync", itwload, dma=ph + "t", wait=[(mmx, prev_imm_all)] + deps)
            tw_ld = plan.val(ph + "t")
            def imm(t):
                for nch in range(2):
                    for cs in range(2):
                        for ft in range(17):
                            ins = t.matmul(
                                bufs.PSW[:, 512 * nch:512 * (nch + 1)],
                                w_bf[:, 2176 * cs + 128 * ft:2176 * cs + 128 * (ft + 1)],
                                vf_t(ft, cs)[:, 512 * nch:512 * (nch + 1)],
                                start=(cs == 0 and ft == 0), stop=(cs == 1 and ft == 16))
                return ins
            plan.op("tensor", imm, inc=(mmx, 1),
                    wait=[(ph + "t", tw_ld), (vv, max(vf_done, prev_epi))])
            prev_itw_mm[bb] = plan.val(mmx)
            prev_imm_all = plan.val(mmx)
            def epi_v(v, lt=lt, mw=plan.val(mmx), xn_ld=xn_ld, sw=plan.val("sc_p")):
                v.wait_ge(plan.sems[mmx], mw)
                v.wait_ge(plan.sems[ph + "x"], xn_ld)
                v.wait_ge(plan.sems["sc_p"], sw)
                v.tensor_mul(t1r, xn_t[lt], dbc)
                return v.tensor_add(y2r, bufs.PSW[:, :], t1r)
            plan.op("vector", epi_v)
            prev_epi = plan.val(vv)
            def epi_s(s, w=prev_epi, pg=prev_gstore):
                s.wait_ge(plan.sems[vv], w)
                s.wait_ge(plan.sems[stx], pg)
                return s.activation(gout, y2r, AF.Gelu_apprx_tanh)
            plan.op("scalar", epi_s)
            plan.op("sync", lambda sy, b=b, lt=lt: sy.dma_start(
                out=dram["s_g"][b, 128 * lt:128 * (lt + 1), :], in_=gout),
                dma=stx, wait=[(ss, plan.val(ss))])
            prev_gstore = plan.val(stx)
    return stx, plan.val(stx)



def emit_wout(plan, bufs, ph, dram, deps=()):
    cst = plan.consts
    a_bf, b_bf, c_bf, w_bf = bufs.a_bf(), bufs.b_bf(), bufs.c_bf(), bufs.w_bf()
    SM = bufs.SM
    ld, mmx, stx = ph + "ld", ph + "mm", ph + "st"
    for s in (ld, mmx, stx):
        plan.sem(s)
    vv, ss = "ve_p", "sc_p"
    deps = list(deps) + list(plan.cst_wait)
    plan.op("sync", lambda sy: sy.dma_start(
        out=SM[:, 24:40], in_=dram["b_out"].rearrange("(dt p) o -> p (dt o)", p=128),
        allow_slow_non_contiguous=True),
        dma=ph + "b", wait=deps)
    bout_ld = plan.val(ph + "b")
    prev_gT_mm = 0
    prev_z = 0
    prev_glu_store = 0
    for b in range(2):
        g_t = [a_bf[:, 1024 * lt:1024 * (lt + 1)] for lt in range(16)]
        plan.op("sync", lambda sy, b=b: sy.dma_start(
            out=a_bf[:, 0:16384].rearrange("p (lt n) -> p lt n", lt=16),
            in_=dram["s_g"].rearrange("b (lt p) n -> b p lt n", p=128)[b]),
            dma=ph + "g", wait=[(mmx, prev_gT_mm)] + deps)
        g_ld = plan.val(ph + "g")
        gT = [c_bf[:, 2048 * ht:2048 * (ht + 1)] for ht in range(8)]
        psb = bufs.PSW[:, :].bitcast(BF16)
        for ht in range(8):
            def trans(t, ht=ht, g_ld=g_ld, pe=plan.val(vv)):
                t.wait_ge(plan.sems[ph + "g"], g_ld)
                t.wait_ge(plan.sems[vv], pe)
                for lt in range(16):
                    ins = t.transpose(psb[:, 128 * lt:128 * (lt + 1)],
                                      g_t[lt][:, 128 * ht:128 * (ht + 1)], cst["ident"])
                return ins
            plan.op("tensor", trans, inc=(mmx, 1))
            def tev(v, ht=ht, mw=plan.val(mmx)):
                v.wait_ge(plan.sems[mmx], mw)
                return v.tensor_copy(gT[ht], psb[:, 0:2048])
            plan.op("vector", tev)
        gT_done = plan.val(vv)
        prev_gT_mm = plan.val(mmx)
        for dpt in range(16):
            bb = dpt % 2
            wslab = w_bf[:, 1024 * bb:1024 * (bb + 1)]
            def wload(g, dpt=dpt, wslab=wslab):
                return g.dma_start(
                    out=wslab.rearrange("p (ht n) -> p ht n", ht=8),
                    in_=dram["w_out"].rearrange("(ht p) dp -> p ht dp", p=128)[:, :, 128 * dpt:128 * (dpt + 1)])
            wsem = ph + "wld" + str(bb)
            plan.op("gpsimd", wload, dma=wsem,
                    wait=[(mmx, prev_gT_mm if dpt < 2 else plan.val(mmx))])
            w_ld = plan.val(wsem)
            def zmm(t, dpt=dpt, wslab=wslab, w_ld=w_ld, gd=gT_done, pz=prev_z, wsem=wsem):
                t.wait_ge(plan.sems[wsem], w_ld)
                t.wait_ge(plan.sems[vv], max(gd, pz))
                for nch in range(4):
                    for ht in range(8):
                        ins = t.matmul(bufs.PS[nch][:, :],
                                       wslab[:, 128 * ht:128 * (ht + 1)],
                                       gT[ht][:, 512 * nch:512 * (nch + 1)],
                                       start=(ht == 0), stop=(ht == 7))
                return ins
            plan.op("tensor", zmm, inc=(mmx, 1))
            zt = b_bf[:, 2048 * dpt:2048 * (dpt + 1)]
            def zev(v, dpt=dpt, zt=zt, mw=plan.val(mmx), bl=bout_ld, sw=plan.val("sc_p")):
                v.wait_ge(plan.sems[mmx], mw)
                v.wait_ge(plan.sems[ph + "b"], bl)
                v.wait_ge(plan.sems["sc_p"], sw)
                for nch in range(4):
                    ins = v.tensor_scalar(zt[:, 512 * nch:512 * (nch + 1)],
                                          bufs.PS[nch][:, :], SM[:, 24 + dpt:25 + dpt],
                                          None, ALU.add)
                return ins
            plan.op("vector", zev)
            prev_z = plan.val(vv)
        for ht in range(8):
            sg = a_bf[:, 16384 + 2048 * (ht % 2):16384 + 2048 * (ht % 2 + 1)]
            def glu_s(s, ht=ht, sg=sg, w=prev_z, pg=prev_glu_store):
                s.wait_ge(plan.sems[vv], w)
                s.wait_ge(plan.sems[stx], pg)
                return s.activation(sg, b_bf[:, 2048 * (8 + ht):2048 * (9 + ht)], AF.Sigmoid)
            plan.op("scalar", glu_s)
            def glu_v(v, ht=ht, sg=sg, w=plan.val(ss), pg=prev_glu_store):
                v.wait_ge(plan.sems[ss], w)
                v.wait_ge(plan.sems[stx], pg)
                return v.tensor_mul(sg, b_bf[:, 2048 * ht:2048 * (ht + 1)], sg)
            plan.op("vector", glu_v)
            plan.op("sync", lambda sy, b=b, ht=ht, sg=sg: sy.dma_start(
                out=dram["s_glu"].rearrange("(ht p) t -> ht p t", p=128)[ht, :, 2048 * b:2048 * (b + 1)],
                in_=sg), dma=stx, wait=[(vv, plan.val(vv))])
            prev_glu_store = plan.val(stx)
    return stx, plan.val(stx)


def emit_l0attn(plan, bufs, dram, deps=()):
    s1, v1 = emit_dft_fwd(plan, bufs, "df", dram, with_k=True, deps=deps)
    s2, v2 = emit_conv(plan, bufs, "cv", dram, deps=[(s1, v1)])
    s3, v3 = emit_wout(plan, bufs, "wo", dram, deps=[(s2, v2)])
    s4, v4 = emit_ln(plan, bufs, "lns", dram["s_glu"], dram["s_attn0"],
                     gamma=dram["ssm_gamma"], beta=dram["ssm_beta"],
                     src_f32=False, wait_st=[(s3, v3)])
    return s4, v4



def emit_moe(plan, bufs, ph, dram, deps=()):
    cst = plan.consts
    a_bf, b_bf, c_bf, w_bf = bufs.a_bf(), bufs.b_bf(), bufs.c_bf(), bufs.w_bf()
    A, Bb, C, SM = bufs.AR_A, bufs.AR_B, bufs.AR_C, bufs.SM
    ld, mmx = ph + "ld", ph + "mm"
    for s in (ld, mmx):
        plan.sem(s)
    vv, ss = "ve_p", "sc_p"
    deps = list(deps) + list(plan.cst_wait)
    IC = 512
    TH = 2048
    xn_t = [a_bf[:, 2048 * i:2048 * (i + 1)] for i in range(8)]
    acc = [Bb[:, 2048 * i:2048 * (i + 1)] for i in range(8)]
    g_t = [a_bf[:, 16384 + 2048 * i:16384 + 2048 * (i + 1)] for i in range(4)]
    u_t = [a_bf[:, 24576 + 2048 * i:24576 + 2048 * (i + 1)] for i in range(4)]
    cwb = C[:, 0:2048]
    crow = C[0:1, 0:2048]
    wd_buf = [c_bf[:, 4096 + 4096 * p:4096 + 4096 * (p + 1)] for p in range(2)]
    cwT = C[0:8, 6144:8192]
    wg_buf = [w_bf[:, 0:4096], w_bf[:, 8192:12288]]
    wu_buf = [w_bf[:, 4096:8192], c_bf[:, 12288:16384]]
    gate_b = SM[:, 704:736].bitcast(BF16)
    wfree = [0, 0]
    icg = [0]

    plan.op("gpsimd", lambda g: g.dma_start(
        out=gate_b.rearrange("p (i e) -> p i e", i=8),
        in_=dram["gate_wT"].rearrange("(i p) e -> p i e", p=128)), dma=ph + "ga", wait=deps)
    gate_ld = plan.val(ph + "ga")

    nbank = 6
    bank_free = [("ve_p", 0)] * nbank
    bk = [0]

    def take_bank():
        b = bk[0] % nbank
        bk[0] += 1
        return bufs.PS[b], bank_free[b], b

    prev_half_v = 0
    for th in range(2):
        t0 = th * TH
        plan.op("sync", lambda sy, t0=t0: sy.dma_start(
            out=a_bf[:, 0:16384].rearrange("p (dt t) -> p dt t", dt=8),
            in_=fm_dram(dram["s_xn"], t0, TH)), dma=ld,
            wait=deps + [(vv, prev_half_v), (ph + "st", plan.val(ph + "st"))])
        xn_ld = plan.val(ld)
        sc0 = SM[:, 40:48]; m1 = SM[:, 48:49]; m2 = SM[:, 49:50]; tmp8 = SM[:, 56:64]
        prev_cwt_mm = 0
        for tt in range(16):
            psg, bfree, bid = take_bank()
            def gmm(t, tt=tt, psg=psg, bfree=bfree, xn_ld=xn_ld):
                t.wait_ge(plan.sems[ld], xn_ld)
                t.wait_ge(plan.sems[ph + "ga"], gate_ld)
                t.wait_ge(plan.sems[bfree[0]], bfree[1])
                for ci in range(8):
                    ins = t.matmul(psg[:, 0:8], xn_t[ci][:, 128 * tt:128 * (tt + 1)],
                                   gate_b[:, 8 * ci:8 * (ci + 1)],
                                   start=(ci == 0), stop=(ci == 7))
                return ins
            plan.op("tensor", gmm, inc=(mmx, 1))
            def cwv(v, psg=psg, mw=plan.val(mmx)):
                v.wait_ge(plan.sems[mmx], mw)
                v.reduce_max(m1, psg[:, 0:8], axis=AX)
                v.tensor_scalar(m1, m1, -1.0, None, ALU.mult)
                return v.tensor_scalar(sc0, psg[:, 0:8], m1, None, ALU.add)
            plan.op("vector", cwv)
            bank_free[bid] = ("ve_p", plan.val(vv))
            plan.op("scalar", lambda s, w=plan.val(vv): (
                s.wait_ge(plan.sems[vv], w), s.activation(sc0, sc0, AF.Exp))[1])
            def cwv2(v, w=plan.val(ss)):
                v.wait_ge(plan.sems[ss], w)
                v.reduce_sum(m1, sc0, axis=AX)
                v.reciprocal(m1, m1)
                v.tensor_scalar(sc0, sc0, m1, None, ALU.mult)
                v.reduce_max(m1, sc0, axis=AX)
                v.tensor_scalar(tmp8, sc0, m1, None, ALU.is_ge)
                v.scalar_tensor_tensor(tmp8, tmp8, -1e30, sc0, ALU.mult, ALU.add)
                v.reduce_max(m2, tmp8, axis=AX)
                v.tensor_scalar(tmp8, sc0, m2, None, ALU.is_ge)
                return v.tensor_mul(sc0, sc0, tmp8)
            plan.op("vector", cwv2)
            def cwt(t, tt=tt, w=plan.val(vv), pm=prev_cwt_mm):
                t.wait_ge(plan.sems[vv], w)
                return t.transpose(bufs.PSW[0:8, 0:128], sc0[:, 0:8], cst["ident_f"])
            plan.op("tensor", cwt, inc=(mmx, 1))
            def cwev(v, tt=tt, mw=plan.val(mmx)):
                v.wait_ge(plan.sems[mmx], mw)
                return v.tensor_copy(cwT[:, 128 * tt:128 * (tt + 1)], bufs.PSW[0:8, 0:128])
            plan.op("vector", cwev)
            prev_cwt_mm = plan.val(vv)
        plan.op("sync", lambda sy, t0=t0: sy.dma_start(
            out=dram["s_cwT"][:, t0:t0 + TH], in_=cwT), dma=ph + "cw",
            wait=[(vv, plan.val(vv))])
        cw_st = plan.val(ph + "cw")

        first_acc = True
        wld_free_mm = 0
        wd_free_mm = [0, 0]
        for e in range(9):
            shared = (e == 8)
            if not shared:
                plan.sem(ph + "dbg")
                plan.op("sync", lambda sy, e=e, t0=t0: sy.dma_start(
                    out=crow, in_=dram["s_cwT"][e:e + 1, t0:t0 + TH]),
                    dma=ph + "cr", wait=[(ph + "cw", cw_st), (vv, plan.val(vv)),
                                         (ph + "dbg", plan.val(ph + "dbg"))])
                crow_ld = plan.val(ph + "cr")
                for chk in range(4):
                    def cwbmm(t, chk=chk, w=crow_ld, pe=plan.val(vv)):
                        t.wait_ge(plan.sems[ph + "cr"], w)
                        t.wait_ge(plan.sems[vv], pe)
                        return t.matmul(bufs.PSW[:, 0:512], cst["ones_row_f"],
                                        crow[:, 512 * chk:512 * (chk + 1)],
                                        start=True, stop=True)
                    plan.op("tensor", cwbmm, inc=(mmx, 1))
                    def cwbev(v, chk=chk, mw=plan.val(mmx)):
                        v.wait_ge(plan.sems[mmx], mw)
                        return v.tensor_copy(cwb[:, 512 * chk:512 * (chk + 1)],
                                             bufs.PSW[:, 0:512])
                    plan.op("vector", cwbev)
            cwb_done = plan.val(vv)
            if th == 0 and e == 0 and "s_dbgc" in dram:
                plan.op("sync", lambda sy: sy.dma_start(out=dram["s_dbgc"], in_=cwb),
                        dma=ph + "dbg", wait=[(vv, cwb_done)])
            for ic in range(4 if shared else 8):
                if shared:
                    wg_src = dram["ws_g"][:, IC * ic:IC * (ic + 1)]
                    wu_src = dram["ws_u"][:, IC * ic:IC * (ic + 1)]
                    wd_src = dram["ws_d"][IC * ic:IC * (ic + 1), :]
                else:
                    wg_src = dram["moe_wg"][e][:, IC * ic:IC * (ic + 1)]
                    wu_src = dram["moe_wu"][e][:, IC * ic:IC * (ic + 1)]
                    wd_src = dram["moe_wd"][e][IC * ic:IC * (ic + 1), :]
                p_gu = icg[0] % 2
                icg[0] += 1
                cww = [(ph + "cw", cw_st)] if (e == 0 and ic < 2) else []

                def wgld(g, wg_src=wg_src, wu_src=wu_src, p_gu=p_gu, wf=wfree[p_gu],
                         dp=list(deps) + cww):
                    for s_, v_ in dp:
                        g.wait_ge(plan.sems[s_], v_)
                    g.wait_ge(plan.sems[mmx], wf)
                    g.dma_start(out=wg_buf[p_gu].rearrange("p (ct i) -> p ct i", ct=8),
                                in_=wg_src.rearrange("(ct p) i -> p ct i", p=128))
                    return g.dma_start(out=wu_buf[p_gu].rearrange("p (ct i) -> p ct i", ct=8),
                                       in_=wu_src.rearrange("(ct p) i -> p ct i", p=128))
                plan.op("gpsimd", wgld, dma=ph + "w")
                wgu_ld = plan.val(ph + "w")
                wdp = ic % 2
                def wdld(g, wd_src=wd_src, wdp=wdp, wf=plan.val(mmx), dp=list(deps),
                         pw=plan.val(ph + "wd" + str(wdp))):
                    for s_, v_ in dp:
                        g.wait_ge(plan.sems[s_], v_)
                    g.wait_ge(plan.sems[mmx], wf)
                    g.wait_ge(plan.sems[ph + "wd" + str(wdp)], pw)
                    return g.dma_start(out=wd_buf[wdp].rearrange("p (it d) -> p it d", it=4),
                                       in_=wd_src.rearrange("(it p) d -> p it d", p=128))
                plan.op("gpsimd", wdld, dma=ph + "wd" + str(wdp))
                wd_ld = plan.val(ph + "wd" + str(wdp))
                for uu in range(2):
                    wbuf = wg_buf[p_gu] if uu == 0 else wu_buf[p_gu]
                    dst_t = g_t if uu == 0 else u_t
                    for it in range(4):
                        for tc in range(4):
                            psb, bfree, bid = take_bank()
                            def gu_mm(t, it=it, tc=tc, uu=uu, wbuf=wbuf, psb=psb,
                                      bfree=bfree, wl=wgu_ld, xl=xn_ld,
                                      pst=plan.val(ph + "st")):
                                t.wait_ge(plan.sems[ph + "w"], wl)
                                t.wait_ge(plan.sems[ld], xl)
                                t.wait_ge(plan.sems[ph + "st"], pst)
                                t.wait_ge(plan.sems[bfree[0]], bfree[1])
                                for ci in range(8):
                                    ins = t.matmul(
                                        psb[:, :],
                                        wbuf[:, 512 * ci + 128 * it:512 * ci + 128 * (it + 1)],
                                        xn_t[ci][:, 512 * tc:512 * (tc + 1)],
                                        start=(ci == 0), stop=(ci == 7))
                                return ins
                            plan.op("tensor", gu_mm, inc=(mmx, 1))
                            mmv = plan.val(mmx)
                            if uu == 0:
                                def sev(s, it=it, tc=tc, psb=psb, mw=mmv):
                                    s.wait_ge(plan.sems[mmx], mw)
                                    return s.activation(
                                        g_t[it][:, 512 * tc:512 * (tc + 1)], psb[:, :], AF.Silu)
                                plan.op("scalar", sev)
                                bank_free[bid] = ("sc_p", plan.val(ss))
                            else:
                                def uev(v, it=it, tc=tc, psb=psb, mw=mmv, shared=shared,
                                        cd=cwb_done):
                                    v.wait_ge(plan.sems[mmx], mw)
                                    if shared:
                                        return v.tensor_copy(
                                            u_t[it][:, 512 * tc:512 * (tc + 1)], psb[:, :])
                                    v.wait_ge(plan.sems[vv], cd)
                                    return v.tensor_mul(
                                        u_t[it][:, 512 * tc:512 * (tc + 1)], psb[:, :],
                                        cwb[:, 512 * tc:512 * (tc + 1)])
                                plan.op("vector", uev)
                                bank_free[bid] = ("ve_p", plan.val(vv))
                wfree[p_gu] = plan.val(mmx)
                def hmul(v, w=plan.val(ss)):
                    v.wait_ge(plan.sems[ss], w)
                    for it in range(4):
                        ins = v.tensor_mul(g_t[it], g_t[it], u_t[it])
                    return ins
                plan.op("vector", hmul)
                h_done = plan.val(vv)
                for dt in range(8):
                    for tc in range(4):
                        psb, bfree, bid = take_bank()
                        def yo_mm(t, dt=dt, tc=tc, psb=psb, bfree=bfree, wl=wd_ld,
                                  hd=h_done, wdp=wdp):
                            t.wait_ge(plan.sems[ph + "wd" + str(wdp)], wl)
                            t.wait_ge(plan.sems[vv], hd)
                            t.wait_ge(plan.sems[bfree[0]], bfree[1])
                            for it in range(4):
                                ins = t.matmul(
                                    psb[:, :],
                                    wd_buf[wdp][:, 1024 * it + 128 * dt:1024 * it + 128 * (dt + 1)],
                                    g_t[it][:, 512 * tc:512 * (tc + 1)],
                                    start=(it == 0), stop=(it == 3))
                            return ins
                        plan.op("tensor", yo_mm, inc=(mmx, 1))
                        def yoev(v, dt=dt, tc=tc, psb=psb, mw=plan.val(mmx), fa=first_acc):
                            v.wait_ge(plan.sems[mmx], mw)
                            dsl = acc[dt][:, 512 * tc:512 * (tc + 1)]
                            if fa:
                                return v.tensor_copy(dsl, psb[:, :])
                            return v.tensor_add(dsl, dsl, psb[:, :])
                        plan.op("vector", yoev)
                        bank_free[bid] = ("ve_p", plan.val(vv))
                first_acc = False
                wd_free_mm[wdp] = plan.val(mmx)
        x0stg = [A[:, 8192 + 2048 * p:8192 + 2048 * (p + 1)] for p in range(2)]
        atstg = [a_bf[:, 24576 + 2048 * p:24576 + 2048 * (p + 1)] for p in range(2)]
        for dt in range(8):
            p = dt % 2
            def x1ld(sy, dt=dt, p=p, t0=t0, pe=plan.val(vv), pst=plan.val(ph + "st")):
                sy.wait_ge(plan.sems[vv], pe)
                sy.wait_ge(plan.sems[ph + "st"], pst)
                sy.dma_start(out=x0stg[p],
                             in_=dram["x0T"].rearrange("(dt q) t -> dt q t", q=128)[dt, :, t0:t0 + TH])
                return sy.dma_start(out=atstg[p],
                                    in_=dram["s_attn0"].rearrange("(dt q) t -> dt q t", q=128)[dt, :, t0:t0 + TH])
            plan.op("sync", x1ld, dma=ph + "x1")
            def x1v(v, dt=dt, p=p, w=plan.val(ph + "x1")):
                v.wait_ge(plan.sems[ph + "x1"], w)
                v.tensor_add(x0stg[p], x0stg[p], acc[dt])
                return v.tensor_add(x0stg[p], x0stg[p], atstg[p])
            plan.op("vector", x1v)
            plan.op("sync", lambda sy, dt=dt, p=p, t0=t0, w=plan.val(vv): (
                sy.wait_ge(plan.sems[vv], w),
                sy.dma_start(out=dram["s_x1"].rearrange("(dt q) t -> dt q t", q=128)[dt, :, t0:t0 + TH],
                             in_=x0stg[p]))[1], dma=ph + "st")
        prev_half_v = plan.val(vv)
    return ph + "st", plan.val(ph + "st")



def emit_attn(plan, bufs, ph, dram, deps=()):
    cst = plan.consts
    a_bf, b_bf, c_bf, w_bf = bufs.a_bf(), bufs.b_bf(), bufs.c_bf(), bufs.w_bf()
    A, Bb, C, SM = bufs.AR_A, bufs.AR_B, bufs.AR_C, bufs.SM
    mmx = ph + "mm"
    plan.sem(mmx)
    vv, ss = "ve_p", "sc_p"
    deps = list(deps) + list(plan.cst_wait)
    qT = [a_bf[:, 2048 * i:2048 * (i + 1)] for i in range(8)]
    kT = [a_bf[:, 16384 + 2048 * i:16384 + 2048 * (i + 1)] for i in range(8)]
    xt = [b_bf[:, 2048 * i:2048 * (i + 1)] for i in range(8)]
    vt = [c_bf[:, 1024 * i:1024 * (i + 1)] for i in range(16)]
    et = [b_bf[:, 16384 + 512 * i:16384 + 512 * (i + 1)] for i in range(16)]
    wbuf = w_bf[:, 0:8192]
    ystg = b_bf[:, 24576 + 0:24576 + 512]
    rbst = Bb[:, 12544:13056]
    q2s = Bb[:, 13056:13568]
    nbank = 4
    bank_free = [("ve_p", 0)] * nbank
    bkc = [0]

    def takeb():
        b = bkc[0] % nbank
        bkc[0] += 1
        return bufs.PS[b], bank_free[b], b

    prev_y_store = 0
    for b in range(2):
        t0 = b * 2048
        plan.op("sync", lambda sy, t0=t0: sy.dma_start(
            out=b_bf[:, 0:16384].rearrange("p (dt t) -> p dt t", dt=8),
            in_=fm_dram(dram["s_xn1"], t0, 2048)), dma=ph + "x",
            wait=deps + [(mmx, plan.val(mmx)), (vv, plan.val(vv)), (ss, plan.val(ss))])
        x_ld = plan.val(ph + "x")
        for wi, wname in enumerate(["wqT", "wkT", "wvT"]):
            plan.op("gpsimd", lambda g, wname=wname, mv=plan.val(mmx): (
                g.wait_ge(plan.sems[mmx], mv),
                g.dma_start(out=wbuf.rearrange("p (ct n) -> p ct n", ct=8),
                            in_=dram[wname].rearrange("(ct p) n -> p ct n", p=128)))[1],
                dma=ph + "w", wait=deps)
            w_ld = plan.val(ph + "w")
            if wi < 2:
                dst = qT if wi == 0 else kT
                for m in range(8):
                    for tc in range(4):
                        psb, bfree, bid = takeb()
                        def pmm(t, m=m, tc=tc, psb=psb, bfree=bfree, w_ld=w_ld, x_ld=x_ld):
                            t.wait_ge(plan.sems[ph + "w"], w_ld)
                            t.wait_ge(plan.sems[ph + "x"], x_ld)
                            t.wait_ge(plan.sems[bfree[0]], bfree[1])
                            for ci in range(8):
                                ins = t.matmul(psb[:, :],
                                               wbuf[:, 1024 * ci + 128 * m:1024 * ci + 128 * (m + 1)],
                                               xt[ci][:, 512 * tc:512 * (tc + 1)],
                                               start=(ci == 0), stop=(ci == 7))
                            return ins
                        plan.op("tensor", pmm, inc=(mmx, 1))
                        def pev(v, m=m, tc=tc, psb=psb, mw=plan.val(mmx), dst=dst):
                            v.wait_ge(plan.sems[mmx], mw)
                            return v.tensor_copy(dst[m][:, 512 * tc:512 * (tc + 1)], psb[:, :])
                        plan.op("vector", pev)
                        bank_free[bid] = ("ve_p", plan.val(vv))
            else:
                for tt in range(16):
                    for nc2 in range(2):
                        psb, bfree, bid = takeb()
                        def vmm(t, tt=tt, nc2=nc2, psb=psb, bfree=bfree, w_ld=w_ld, x_ld=x_ld):
                            t.wait_ge(plan.sems[ph + "w"], w_ld)
                            t.wait_ge(plan.sems[ph + "x"], x_ld)
                            t.wait_ge(plan.sems[bfree[0]], bfree[1])
                            for ci in range(8):
                                ins = t.matmul(psb[:, :],
                                               xt[ci][:, 128 * tt:128 * (tt + 1)],
                                               wbuf[:, 1024 * ci + 512 * nc2:1024 * ci + 512 * (nc2 + 1)],
                                               start=(ci == 0), stop=(ci == 7))
                            return ins
                        plan.op("tensor", vmm, inc=(mmx, 1))
                        def vev(v, tt=tt, nc2=nc2, psb=psb, mw=plan.val(mmx)):
                            v.wait_ge(plan.sems[mmx], mw)
                            return v.tensor_copy(vt[tt][:, 512 * nc2:512 * (nc2 + 1)], psb[:, :])
                        plan.op("vector", vev)
                        bank_free[bid] = ("ve_p", plan.val(vv))
        qkv_done = plan.val(vv)
        sq = Bb[:, 13568:15616]
        q2m = SM[0:1, 104:105]
        k2m = SM[0:1, 105:106]
        pch = [bufs.PSW[0:1, 0:512], bufs.PSW[0:1, 512:1024],
               bufs.PS[4][0:1, 0:512], bufs.PS[5][0:1, 0:512]]
        for qi, (src_, dstm) in enumerate([(qT, q2m), (kT, k2m)]):
            prev_nmm = plan.val(mmx)
            for m in range(8):
                def sq1(s, m=m, src_=src_, w=qkv_done, pn=prev_nmm):
                    s.wait_ge(plan.sems[vv], w)
                    s.wait_ge(plan.sems[mmx], pn)
                    return s.activation(sq, src_[m], AF.Square)
                plan.op("scalar", sq1)
                def nmm(t, m=m, w=plan.val(ss), pv=plan.val(vv)):
                    t.wait_ge(plan.sems[ss], w)
                    t.wait_ge(plan.sems[vv], pv)
                    for cc in range(4):
                        ins = t.matmul(pch[cc], cst["ones_col_f"],
                                       sq[:, 512 * cc:512 * (cc + 1)],
                                       start=(m == 0), stop=(m == 7))
                    return ins
                plan.op("tensor", nmm, inc=(mmx, 1))
                prev_nmm = plan.val(mmx)
            def bfin(v, dstm=dstm, mw=prev_nmm):
                v.wait_ge(plan.sems[mmx], mw)
                v.reduce_max(SM[0:1, 107:108], pch[0], axis=AX)
                v.reduce_max(SM[0:1, 108:109], pch[1], axis=AX)
                v.tensor_max(SM[0:1, 107:108], SM[0:1, 107:108], SM[0:1, 108:109])
                v.reduce_max(SM[0:1, 108:109], pch[2], axis=AX)
                v.tensor_max(SM[0:1, 107:108], SM[0:1, 107:108], SM[0:1, 108:109])
                v.reduce_max(SM[0:1, 108:109], pch[3], axis=AX)
                return v.tensor_max(dstm, SM[0:1, 107:108], SM[0:1, 108:109])
            plan.op("vector", bfin)
        def bnd_v(v):
            v.tensor_mul(q2m, q2m, k2m)
            return v.tensor_scalar(q2m, q2m, 1.0 / 64.0, None, ALU.mult)
        plan.op("vector", bnd_v)
        def bnd_s(s, w=plan.val(vv)):
            s.wait_ge(plan.sems[vv], w)
            return s.activation(q2m, q2m, AF.Sqrt)
        plan.op("scalar", bnd_s)
        def bnd_v2(v, w=plan.val(ss)):
            v.wait_ge(plan.sems[ss], w)
            return v.tensor_scalar(q2m, q2m, -1.02 * 0.125, None, ALU.mult)
        plan.op("vector", bnd_v2)
        def bnd_bc(t, w=plan.val(vv)):
            t.wait_ge(plan.sems[vv], w)
            return t.matmul(bufs.PS[5][:, 0:1], cst["ones_row_f"], q2m, start=True, stop=True)
        plan.op("tensor", bnd_bc, inc=(mmx, 1))
        negb = SM[:, 109:110]
        def bnd_ev(v, w=plan.val(mmx)):
            v.wait_ge(plan.sems[mmx], w)
            return v.tensor_copy(negb, bufs.PS[5][:, 0:1])
        plan.op("vector", bnd_ev)
        bnd_done = plan.val(vv)
        prev_dpv = 0
        for h in range(16):
            hb = (h % 2) * 64
            hm = h // 2
            for qc in range(4):
                stl = 4 * (qc + 1)
                for st in range(stl):
                    psb, bfree, bid = takeb()
                    def smm(t, hm=hm, hb=hb, qc=qc, st=st, psb=psb, bfree=bfree,
                            w=max(qkv_done, bnd_done), pd=prev_dpv):
                        t.wait_ge(plan.sems[vv], w)
                        t.wait_ge(plan.sems[mmx], pd)
                        t.wait_ge(plan.sems[bfree[0]], bfree[1])
                        return t.matmul(psb[:, :],
                                        kT[hm][hb:hb + 64, 128 * st:128 * (st + 1)],
                                        qT[hm][hb:hb + 64, 512 * qc:512 * (qc + 1)],
                                        start=True, stop=True)
                    plan.op("tensor", smm, inc=(mmx, 1))
                    def sev(s, st=st, psb=psb, mw=plan.val(mmx)):
                        s.wait_ge(plan.sems[mmx], mw)
                        return s.activation(et[st], psb[:, :], AF.Exp, bias=negb, scale=0.125)
                    plan.op("scalar", sev)
                    bank_free[bid] = ("sc_p", plan.val(ss))
                    j = st - 4 * qc
                    if 0 <= j:
                        def msk(v, st=st, j=j, w=plan.val(ss)):
                            v.wait_ge(plan.sems[ss], w)
                            return v.tensor_mul(et[st], et[st], cst["mask" + str(j)])
                        plan.op("vector", msk)
                e_done = (plan.val(ss), plan.val(vv))
                def dpv(t, h=h, qc=qc, stl=stl, ed=e_done):
                    t.wait_ge(plan.sems[ss], ed[0])
                    t.wait_ge(plan.sems[vv], ed[1])
                    for st in range(stl):
                        t.matmul(bufs.PS[4][0:1, 0:512], cst["ones_col_b"], et[st],
                                 start=(st == 0), stop=(st == stl - 1))
                    for st in range(stl):
                        ins = t.matmul(bufs.PS[5][0:64, 0:512], vt[st][:, 64 * h:64 * (h + 1)],
                                       et[st], start=(st == 0), stop=(st == stl - 1))
                    return ins
                plan.op("tensor", dpv, inc=(mmx, 1))
                prev_dpv = plan.val(mmx)
                def rec(v, mw=plan.val(mmx)):
                    v.wait_ge(plan.sems[mmx], mw)
                    return v.reciprocal(SM[0:1, 110:622], bufs.PS[4][0:1, 0:512])
                plan.op("vector", rec)
                def rbc(t, w=plan.val(vv)):
                    t.wait_ge(plan.sems[vv], w)
                    return t.matmul(bufs.PSW[0:64, 0:512], cst["ones_row_f"][:, 0:64],
                                    SM[0:1, 110:622], start=True, stop=True)
                plan.op("tensor", rbc, inc=(mmx, 1))
                def yev(v, h=h, qc=qc, mw=plan.val(mmx), pst=prev_y_store):
                    v.wait_ge(plan.sems[mmx], mw)
                    if pst:
                        v.wait_ge(plan.sems[ph + "y"], pst)
                    v.tensor_copy(rbst[0:64, :], bufs.PSW[0:64, 0:512])
                    return v.tensor_mul(ystg[0:64, :], bufs.PS[5][0:64, 0:512], rbst[0:64, :])
                plan.op("vector", yev)
                plan.op("sync", lambda sy, b=b, h=h, qc=qc, w=plan.val(vv): (
                    sy.wait_ge(plan.sems[vv], w),
                    sy.dma_start(out=dram["s_yatt"][64 * h:64 * (h + 1),
                                                    2048 * b + 512 * qc:2048 * b + 512 * (qc + 1)],
                                 in_=ystg[0:64, :]))[1], dma=ph + "y")
                prev_y_store = plan.val(ph + "y")
    plan.sem(ph + "o")
    for b in range(2):
        t0 = b * 2048
        plan.op("sync", lambda sy, t0=t0, mv=plan.val(mmx), yv=prev_y_store: (
            sy.wait_ge(plan.sems[ph + "y"], yv),
            sy.wait_ge(plan.sems[mmx], mv),
            sy.dma_start(out=a_bf[:, 0:16384].rearrange("p (dt t) -> p dt t", dt=8),
                         in_=fm_dram(dram["s_yatt"], t0, 2048)))[2], dma=ph + "ya")
        y_ld = plan.val(ph + "ya")
        plan.op("gpsimd", lambda g, mv=plan.val(mmx): (
            g.wait_ge(plan.sems[mmx], mv),
            g.dma_start(out=wbuf.rearrange("p (ct n) -> p ct n", ct=8),
                        in_=dram["w_o"].rearrange("(ct p) n -> p ct n", p=128)))[1],
            dma=ph + "w")
        wo_ld = plan.val(ph + "w")
        for m in range(8):
            for tc in range(4):
                psb, bfree, bid = takeb()
                def omm(t, m=m, tc=tc, psb=psb, bfree=bfree, y_ld=y_ld, wo_ld=wo_ld):
                    t.wait_ge(plan.sems[ph + "ya"], y_ld)
                    t.wait_ge(plan.sems[ph + "w"], wo_ld)
                    t.wait_ge(plan.sems[bfree[0]], bfree[1])
                    for ci in range(8):
                        ins = t.matmul(psb[:, :],
                                       wbuf[:, 1024 * ci + 128 * m:1024 * ci + 128 * (m + 1)],
                                       a_bf[:, 2048 * ci + 512 * tc:2048 * ci + 512 * (tc + 1)],
                                       start=(ci == 0), stop=(ci == 7))
                    return ins
                plan.op("tensor", omm, inc=(mmx, 1))
                ostg = b_bf[:, 24576 + 1024 * (bkc[0] % 2):24576 + 1024 * (bkc[0] % 2) + 512]
                def oev(v, psb=psb, ostg=ostg, mw=plan.val(mmx), pst=plan.val(ph + "o")):
                    v.wait_ge(plan.sems[mmx], mw)
                    if pst:
                        v.wait_ge(plan.sems[ph + "o"], pst)
                    return v.tensor_copy(ostg, psb[:, :])
                plan.op("vector", oev)
                bank_free[bid] = ("ve_p", plan.val(vv))
                plan.op("sync", lambda sy, m=m, tc=tc, t0=t0, ostg=ostg, w=plan.val(vv): (
                    sy.wait_ge(plan.sems[vv], w),
                    sy.dma_start(out=dram["s_attn1"].rearrange("(dt p) t -> dt p t", p=128)[m, :, t0 + 512 * tc:t0 + 512 * (tc + 1)],
                                 in_=ostg))[1], dma=ph + "o")
    return ph + "o", plan.val(ph + "o")



def emit_fp4(plan, bufs, ph, dram, deps=()):
    cst = plan.consts
    a_bf, b_bf, c_bf, w_bf = bufs.a_bf(), bufs.b_bf(), bufs.c_bf(), bufs.w_bf()
    A, Bb, C, SM = bufs.AR_A, bufs.AR_B, bufs.AR_C, bufs.SM
    mmx = ph + "mm"
    plan.sem(mmx)
    vv, ss = "ve_p", "sc_p"
    deps = list(deps) + list(plan.cst_wait)
    psb_t = bufs.PSW[:, :].bitcast(BF16)
    BW = 1408

    class QdqStream:

        def __init__(self, scr0, sct, sc0):
            self.scr0, self.sct, self.sc0 = scr0, sct, sc0
            self.i = 0
            self.pend = None

        def push(self, xf, out, F, wait_v=None, wait_s=None):
            p = self.i % 2
            self.i += 1
            b = self.scr0 + 3 * BW * p
            A1 = A[:, b:b + F]
            A2 = A[:, b + BW:b + BW + F]
            A3 = A[:, b + 2 * BW:b + 2 * BW + F]
            s0 = self.sc0 + 192 * p
            nb = F // 16
            sc1 = self.sct[:, s0:s0 + nb]
            sc2 = self.sct[:, s0 + 96:s0 + 96 + nb]
            wl = list(wait_v or [])

            def v1(v, wl=wl, xf=xf, F=F, A1=A1, sc1=sc1, sc2=sc2):
                for sn, sv in wl:
                    v.wait_ge(plan.sems[sn], sv)
                v.reduce_max(sc1, xf.rearrange("p (b k) -> p b k", k=16), axis=AX,
                             apply_absolute_value=True)
                v.tensor_scalar(sc1, sc1, 1.0 / 6.0, 1e-12, ALU.mult, ALU.max)
                v.reciprocal(sc2, sc1)
                v.tensor_scalar(sc1, sc1, 0.5, None, ALU.mult)
                return v.tensor_tensor(
                    A1.rearrange("p (b k) -> p b k", k=16),
                    xf.rearrange("p (b k) -> p b k", k=16),
                    sc2.rearrange("p (b k) -> p b k", k=1).broadcast_to([128, F // 16, 16]),
                    ALU.mult)
            plan.op("vector", v1)
            wsl = list(wait_s or []) + [(vv, plan.val(vv))]

            def s1(s, wsl=wsl, A1=A1, out=out):
                for sn, sv in wsl:
                    s.wait_ge(plan.sems[sn], sv)
                s.activation(out, A1, AF.Sign)
                return s.activation(A1, A1, AF.Abs)
            plan.op("scalar", s1)
            sd = plan.val(ss)

            def v2(v, sd=sd, A1=A1, A2=A2, A3=A3, sc1=sc1, out=out, F=F):
                v.wait_ge(plan.sems[ss], sd)
                v.tensor_scalar(A3, A1, 0.25, None, ALU.is_gt)
                v.scalar_tensor_tensor(A3, A1, 0.75, A3, ALU.is_gt, ALU.add)
                v.scalar_tensor_tensor(A3, A1, 1.25, A3, ALU.is_gt, ALU.add)
                v.scalar_tensor_tensor(A3, A1, 1.75, A3, ALU.is_gt, ALU.add)
                v.tensor_scalar(A2, A1, 2.5, None, ALU.is_gt)
                v.scalar_tensor_tensor(A2, A1, 3.5, A2, ALU.is_gt, ALU.add)
                v.scalar_tensor_tensor(A3, A2, 2.0, A3, ALU.mult, ALU.add)
                v.tensor_scalar(A2, A1, 5.0, None, ALU.is_gt)
                v.scalar_tensor_tensor(A3, A2, 4.0, A3, ALU.mult, ALU.add)
                v.tensor_mul(A3, A3, out)
                return v.tensor_tensor(
                    out.rearrange("p (b k) -> p b k", k=16),
                    A3.rearrange("p (b k) -> p b k", k=16),
                    sc1.rearrange("p (b k) -> p b k", k=1).broadcast_to([128, F // 16, 16]),
                    ALU.mult)
            prev = self.pend
            self.pend = v2
            if prev is not None:
                plan.op("vector", prev)

        def flush(self):
            if self.pend is not None:
                plan.op("vector", self.pend)
                self.pend = None

    wsld_sem, wst_sem = ph + "wl", ph + "ws"
    xs_sem, xl_sem = ph + "xs", ph + "xl"
    for s_ in (wsld_sem, wst_sem, xs_sem, xl_sem):
        plan.sem(s_)
    qsA = QdqStream(5632, A, 14080)
    state = {"xf_free": 0, "psw": None}
    ofree = {0: None, 1: None}
    lfree = {0: None, 1: None}
    pend_post = [None]

    def do_post():
        if pend_post[0] is None:
            return
        kind, args = pend_post[0]
        pend_post[0] = None
        q_done = plan.val(vv)
        if kind == "A":
            kp, dname, ti, F, out = args
            nblk = F // 128
            ldst = c_bf[:, 5632 * kp:5632 * kp + 128 * nblk]
            for g0 in range(0, nblk, 11):
                gn = min(11, nblk - g0)
                twl = [(vv, q_done)] + ([state["psw"]] if state["psw"] else [])

                def wtr(t, g0=g0, gn=gn, out=out, twl=twl):
                    for sn, sv in twl:
                        t.wait_ge(plan.sems[sn], sv)
                    for bj in range(gn):
                        ins = t.transpose(psb_t[:, 128 * bj:128 * (bj + 1)],
                                          out[:, 128 * (g0 + bj):128 * (g0 + bj + 1)],
                                          cst["ident"])
                    return ins
                plan.op("tensor", wtr, inc=(mmx, 1))
                evwl = [(mmx, plan.val(mmx))] + ([lfree[kp]] if (g0 == 0 and lfree[kp]) else [])

                def wev(s, g0=g0, gn=gn, ldst=ldst, evwl=evwl):
                    for sn, sv in evwl:
                        s.wait_ge(plan.sems[sn], sv)
                    return s.activation(ldst[:, 128 * g0:128 * (g0 + gn)],
                                        psb_t[:, 0:128 * gn], AF.Copy)
                plan.op("scalar", wev)
                state["psw"] = (ss, plan.val(ss))
            ofree[kp] = (mmx, plan.val(mmx))

            def wst(sy, dname=dname, ti=ti, nblk=nblk, ldst=ldst, w=plan.val(ss)):
                sy.wait_ge(plan.sems[ss], w)
                return sy.dma_start(
                    out=dram[dname].rearrange("(bj p2) i -> p2 bj i", p2=128)[:, :, 128 * ti:128 * (ti + 1)],
                    in_=ldst.rearrange("p (bj q) -> p bj q", bj=nblk))
            plan.op("sync", wst, dma=wst_sem)
            lfree[kp] = (wst_sem, plan.val(wst_sem))
        else:
            kp, k, out = args
            psq = bufs.PS[kp][:, :].bitcast(BF16)
            twl2 = [(vv, q_done)] + ([ps01_free[kp]] if ps01_free[kp] else [])

            def xqt(t, out=out, psq=psq, twl2=twl2):
                for sn, sv in twl2:
                    t.wait_ge(plan.sems[sn], sv)
                for ct in range(8):
                    ins = t.transpose(psq[:, 128 * ct:128 * (ct + 1)],
                                      out[:, 128 * ct:128 * (ct + 1)], cst["ident"])
                return ins
            plan.op("tensor", xqt, inc=(mmx, 1))
            ofree[kp] = (mmx, plan.val(mmx))
            evwl2 = [(mmx, plan.val(mmx))] + ([lfree[kp]] if lfree[kp] else [])
            ldst = c_bf[:, 5632 * kp:5632 * kp + 1024]

            def ev2(s, ldst=ldst, psq=psq, evwl2=evwl2):
                for sn, sv in evwl2:
                    s.wait_ge(plan.sems[sn], sv)
                return s.activation(ldst, psq[:, 0:1024], AF.Copy)
            plan.op("scalar", ev2)
            ps01_free[kp] = (ss, plan.val(ss))

            def xqst(sy, k=k, ldst=ldst, w=plan.val(ss)):
                sy.wait_ge(plan.sems[ss], w)
                return sy.dma_start(
                    out=dram["s_xq"].rearrange("(ct p2) t -> p2 ct t", p2=128)[:, :, 128 * k:128 * (k + 1)],
                    in_=ldst.rearrange("p (ct q) -> p ct q", ct=8))
            plan.op("sync", xqst, dma=xs_sem)
            lfree[kp] = (xs_sem, plan.val(xs_sem))

    tiles = []
    for wname, dname, ntile, F in (("w_gate", "s_wgq", 22, 1024),
                                   ("w_up", "s_wuq", 22, 1024),
                                   ("w_down", "s_wdq", 8, 2816)):
        for ti in range(ntile):
            tiles.append((wname, dname, ti, F))
    kg = 0
    xff = {0: 0, 1: 0}
    for wname, dname, ti, F in tiles:
        p = kg % 2
        kg += 1
        xf = Bb[:, 2816 * p:2816 * p + F]
        out = a_bf[:, 5632 + 2816 * p:5632 + 2816 * p + F]
        wl = list(deps) + ([(vv, xff[p])] if xff[p] else [])
        plan.op("sync", lambda sy, wname=wname, ti=ti, xf=xf: sy.dma_start(
            out=xf, in_=dram[wname][128 * ti:128 * (ti + 1), :]),
            dma=wsld_sem, wait=wl)
        ld = plan.val(wsld_sem)
        cbs = [1408, 1408] if F == 2816 else [1024]
        cb0 = 0
        for j, cb in enumerate(cbs):
            qsA.push(xf[:, cb0:cb0 + cb], out[:, cb0:cb0 + cb], cb,
                     wait_v=([(wsld_sem, ld)] if j == 0 else None),
                     wait_s=([ofree[p]] if (j == 0 and ofree[p]) else None))
            cb0 += cb
        xff[p] = plan.val(vv)
        state["xf_free"] = plan.val(vv)
        do_post()
        pend_post[0] = ("A", (p, dname, ti, F, out))

    xcfree = {0: None, 1: None}
    xtok_free = {0: state["xf_free"], 1: state["xf_free"]}
    ps01_free = {0: None, 1: None}
    for k in range(32):
        p = kg % 2
        kg += 1
        xcol = b_bf[:, 1024 * p:1024 * (p + 1)]
        wl = list(deps) + ([xcfree[p]] if xcfree[p] else
                           [(vv, state["xf_free"])])
        plan.op("sync", lambda sy, k=k, xcol=xcol: sy.dma_start(
            out=xcol.rearrange("p (ct q) -> p ct q", ct=8),
            in_=dram["s_xn1"].rearrange("(ct p2) t -> p2 ct t", p2=128)[:, :, 128 * k:128 * (k + 1)]),
            dma=xl_sem, wait=wl)
        xl = plan.val(xl_sem)
        twl = [(xl_sem, xl)] + ([state["psw"]] if state["psw"] else [])

        def xtr(t, xcol=xcol, twl=twl):
            for sn, sv in twl:
                t.wait_ge(plan.sems[sn], sv)
            for ct in range(8):
                ins = t.transpose(psb_t[:, 128 * ct:128 * (ct + 1)],
                                  xcol[:, 128 * ct:128 * (ct + 1)], cst["ident"])
            return ins
        plan.op("tensor", xtr, inc=(mmx, 1))
        xcfree[p] = (mmx, plan.val(mmx))
        xtok = A[:, 1024 * p:1024 * (p + 1)]
        evwl = [(mmx, plan.val(mmx)), (vv, xtok_free[p])]

        def ev1(s, xtok=xtok, evwl=evwl):
            for sn, sv in evwl:
                s.wait_ge(plan.sems[sn], sv)
            return s.activation(xtok, psb_t[:, 0:1024], AF.Copy)
        plan.op("scalar", ev1)
        state["psw"] = (ss, plan.val(ss))
        ev = plan.val(ss)
        out = a_bf[:, 5632 + 2816 * p:5632 + 2816 * p + 1024]
        qsA.push(xtok, out, 1024, wait_v=[(ss, ev)],
                 wait_s=([ofree[p]] if ofree[p] else None))
        xtok_free[p] = plan.val(vv)
        do_post()
        pend_post[0] = ("B", (p, k, out))
    qsA.flush()
    do_post()
    xq_done = plan.val(xs_sem)
    wA_done = plan.val(wst_sem)

    ICS = [512, 512, 512, 512, 512, 256]
    xq_t = [a_bf[:, 512 * i:512 * (i + 1)] for i in range(8)]
    gstb = [b_bf[:, 22528 + 512 * j:22528 + 512 * (j + 1)] for j in range(2)]
    wgq_s = [w_bf[:, 4096 * j:4096 * (j + 1)] for j in range(2)]
    wuq_s = [b_bf[:, 23552 + 4096 * j:23552 + 4096 * (j + 1)] for j in range(2)]
    wdq_s = [w_bf[:, 8192 + 2816 * j:8192 + 2816 * (j + 1)] for j in range(2)]
    hqT_t = [c_bf[:, 512 * i:512 * (i + 1)] for i in range(22)]
    htok = A[:, 2048:4864]
    hqb = [a_bf[:, 9728 + 2816 * j:9728 + 2816 * (j + 1)] for j in range(2)]
    x1stg = [C[:, 5632 + 512 * j:5632 + 512 * (j + 1)] for j in range(2)]
    atstg = [c_bf[:, 13312 + 512 * j:13312 + 512 * (j + 1)] for j in range(2)]
    outstg = [C[:, 7168 + 512 * j:7168 + 512 * (j + 1)] for j in range(2)]
    qsC = QdqStream(7680, SM, 1024)

    def h_t(hp, tt):
        return b_bf[:, 11264 * hp + 2816 * tt:11264 * hp + 2816 * (tt + 1)]

    for s_ in (ph + "xg", ph + "wg", ph + "uq", ph + "wd", ph + "r", ph + "o"):
        plan.sem(s_)
    nbank = 4
    bank_free = [None] * nbank
    bkc = [0]

    def takeb():
        b = bkc[0] % nbank
        bkc[0] += 1
        return bufs.PS[b], bank_free[b], b

    S = {"gu_end": 0, "htok_free": 0}
    h_free_sc = [0, 0]
    wg_mm = [0, 0]
    wu_mm = [0, 0]
    gst_vv = [0, 0]
    hq_free = [0, 0]
    hev_done = {}
    h_done_vv = {}
    dmm_mm = [0, 0]
    ps45_free = [0, 0]
    x1_free = [0, 0]
    out_free = [0, 0]
    gcnt = [0]
    dcnt = [0]

    def piece_push(tg1, tt):
        hp = tg1 % 2
        tp = tt % 2
        wl = [(vv, max(h_done_vv[tg1], S["htok_free"]))]

        def hcp(s, wl=wl, src=h_t(hp, tt)):
            for sn, sv in wl:
                s.wait_ge(plan.sems[sn], sv)
            return s.activation(htok, src, AF.Copy)
        plan.op("scalar", hcp)
        hc = plan.val(ss)
        for bi, cb in enumerate((0, 1408)):
            qsC.push(htok[:, cb:cb + 1408], hqb[tp][:, cb:cb + 1408], 1408,
                     wait_v=([(ss, hc)] if bi == 0 else None),
                     wait_s=([(mmx, hq_free[tp])] if (bi == 0 and hq_free[tp]) else None))
        qsC.flush()
        S["htok_free"] = plan.val(vv)

    def piece_post(tg1, tt):
        tp = tt % 2
        q_done = plan.val(vv)
        for half in range(2):
            twl = [(vv, q_done)] + ([state["psw"]] if state["psw"] else [])

            def htr(t, half=half, twl=twl, src=hqb[tp]):
                for sn, sv in twl:
                    t.wait_ge(plan.sems[sn], sv)
                for bj in range(11):
                    ins = t.transpose(psb_t[:, 128 * bj:128 * (bj + 1)],
                                      src[:, 1408 * half + 128 * bj:1408 * half + 128 * (bj + 1)],
                                      cst["ident"])
                return ins
            plan.op("tensor", htr, inc=(mmx, 1))
            mw = plan.val(mmx)

            def hev(s, half=half, tt=tt, mw=mw):
                s.wait_ge(plan.sems[mmx], mw)
                for bj in range(11):
                    ins = s.activation(hqT_t[11 * half + bj][:, 128 * tt:128 * (tt + 1)],
                                       psb_t[:, 128 * bj:128 * (bj + 1)], AF.Copy)
                return ins
            plan.op("scalar", hev)
            state["psw"] = (ss, plan.val(ss))
        hq_free[tp] = plan.val(mmx)
        hev_done[tg1] = plan.val(ss)

    def emit_gateup(tg):
        t0 = 512 * tg
        hp = tg % 2
        wlx = [(xs_sem, xq_done), (mmx, S["gu_end"])]

        def xqld(sy, t0=t0, wlx=wlx):
            for sn, sv in wlx:
                sy.wait_ge(plan.sems[sn], sv)
            return sy.dma_start(out=a_bf[:, 0:4096].rearrange("p (ct q) -> p ct q", ct=8),
                                in_=fm_dram(dram["s_xq"], t0, 512))
        plan.op("sync", xqld, dma=ph + "xg")
        xg = plan.val(ph + "xg")
        ic0 = 0
        for ici, icsz in enumerate(ICS):
            p = ici % 2
            wlg = [(mmx, wg_mm[p])] + ((deps + [(wst_sem, wA_done)]) if tg == 0 else [])

            def wgld(g, ic0=ic0, icsz=icsz, p=p, wlg=wlg):
                for sn, sv in wlg:
                    g.wait_ge(plan.sems[sn], sv)
                return g.dma_start(
                    out=wgq_s[p].rearrange("p2 (ct q) -> p2 ct q", ct=8)[:, :, 0:icsz],
                    in_=dram["s_wgq"].rearrange("(ct p2) i -> p2 ct i", p2=128)[:, :, ic0:ic0 + icsz])
            plan.op("gpsimd", wgld, dma=ph + "wg")
            wgl = plan.val(ph + "wg")
            wlu = [(mmx, wu_mm[p])] + ((deps + [(wst_sem, wA_done)]) if tg == 0 else [])

            def wuld(sy, ic0=ic0, icsz=icsz, p=p, wlu=wlu):
                for sn, sv in wlu:
                    sy.wait_ge(plan.sems[sn], sv)
                return sy.dma_start(
                    out=wuq_s[p].rearrange("p2 (ct q) -> p2 ct q", ct=8)[:, :, 0:icsz],
                    in_=dram["s_wuq"].rearrange("(ct p2) i -> p2 ct i", p2=128)[:, :, ic0:ic0 + icsz])
            plan.op("sync", wuld, dma=ph + "uq")
            wul = plan.val(ph + "uq")
            for tt in range(4):
                j = gcnt[0] % 2
                gcnt[0] += 1
                psg, bfg, bidg = takeb()

                def gmm(t, tt=tt, icsz=icsz, p=p, psg=psg, bfg=bfg, wgl=wgl, xg=xg):
                    t.wait_ge(plan.sems[ph + "wg"], wgl)
                    t.wait_ge(plan.sems[ph + "xg"], xg)
                    if bfg:
                        t.wait_ge(plan.sems[bfg[0]], bfg[1])
                    for ci in range(8):
                        ins = t.matmul(psg[:, 0:icsz], xq_t[ci][:, 128 * tt:128 * (tt + 1)],
                                       wgq_s[p][:, 512 * ci:512 * ci + icsz],
                                       start=(ci == 0), stop=(ci == 7))
                    return ins
                plan.op("tensor", gmm, inc=(mmx, 1))
                gm = plan.val(mmx)

                def silu(s, j=j, icsz=icsz, psg=psg, gm=gm, gw=gst_vv[j]):
                    s.wait_ge(plan.sems[mmx], gm)
                    s.wait_ge(plan.sems[vv], gw)
                    return s.activation(gstb[j][:, 0:icsz], psg[:, 0:icsz], AF.Silu)
                plan.op("scalar", silu)
                bank_free[bidg] = (ss, plan.val(ss))
                sl = plan.val(ss)
                psu, bfu, bidu = takeb()

                def umm(t, tt=tt, icsz=icsz, p=p, psu=psu, bfu=bfu, wul=wul, xg=xg):
                    t.wait_ge(plan.sems[ph + "uq"], wul)
                    t.wait_ge(plan.sems[ph + "xg"], xg)
                    if bfu:
                        t.wait_ge(plan.sems[bfu[0]], bfu[1])
                    for ci in range(8):
                        ins = t.matmul(psu[:, 0:icsz], xq_t[ci][:, 128 * tt:128 * (tt + 1)],
                                       wuq_s[p][:, 512 * ci:512 * ci + icsz],
                                       start=(ci == 0), stop=(ci == 7))
                    return ins
                plan.op("tensor", umm, inc=(mmx, 1))
                um = plan.val(mmx)

                def hset(v, tt=tt, ic0=ic0, icsz=icsz, j=j, psu=psu, um=um, sl=sl,
                         hf=h_free_sc[hp], hp=hp):
                    v.wait_ge(plan.sems[mmx], um)
                    v.wait_ge(plan.sems[ss], max(sl, hf))
                    return v.tensor_mul(h_t(hp, tt)[:, ic0:ic0 + icsz],
                                        gstb[j][:, 0:icsz], psu[:, 0:icsz])
                plan.op("vector", hset)
                gst_vv[j] = plan.val(vv)
                bank_free[bidu] = (vv, plan.val(vv))
            wg_mm[p] = plan.val(mmx)
            wu_mm[p] = plan.val(mmx)
            if tg > 0 and ici < 4:
                piece_push(tg - 1, ici)
                piece_post(tg - 1, ici)
            ic0 += icsz
        h_done_vv[tg] = plan.val(vv)
        h_free_sc[(tg - 1) % 2] = plan.val(ss) if tg > 0 else 0
        S["gu_end"] = plan.val(mmx)

    def emit_down(tg1):
        t0 = 512 * tg1
        for dt in range(8):
            p = dcnt[0] % 2
            dcnt[0] += 1
            wld = [(mmx, dmm_mm[p])] + ((deps + [(wst_sem, wA_done)]) if tg1 == 0 else [])

            def wdld(g, dt=dt, p=p, wld=wld):
                for sn, sv in wld:
                    g.wait_ge(plan.sems[sn], sv)
                return g.dma_start(
                    out=wdq_s[p].rearrange("p2 (it q) -> p2 it q", it=22),
                    in_=dram["s_wdq"].rearrange("(it p2) d -> p2 it d", p2=128)[:, :, 128 * dt:128 * (dt + 1)])
            plan.op("gpsimd", wdld, dma=ph + "wd")
            wdl = plan.val(ph + "wd")
            wlr = [(vv, x1_free[p])] + (deps if (tg1 == 0 and dt < 2) else [])

            def x1ld(sy, dt=dt, p=p, t0=t0, wlr=wlr):
                for sn, sv in wlr:
                    sy.wait_ge(plan.sems[sn], sv)
                sy.dma_start(out=x1stg[p],
                             in_=dram["s_x1"].rearrange("(q p2) t -> q p2 t", p2=128)[dt, :, t0:t0 + 512])
                return sy.dma_start(out=atstg[p],
                                    in_=dram["s_attn1"].rearrange("(q p2) t -> q p2 t", p2=128)[dt, :, t0:t0 + 512])
            plan.op("sync", x1ld, dma=ph + "r")
            rl = plan.val(ph + "r")
            psd = bufs.PS[4 + p]

            def dmm(t, p=p, wdl=wdl, hv=hev_done[tg1], bf=ps45_free[p], psd=psd):
                t.wait_ge(plan.sems[ph + "wd"], wdl)
                t.wait_ge(plan.sems[ss], hv)
                t.wait_ge(plan.sems[vv], bf)
                for it in range(22):
                    ins = t.matmul(psd[:, :], wdq_s[p][:, 128 * it:128 * (it + 1)],
                                   hqT_t[it], start=(it == 0), stop=(it == 21))
                return ins
            plan.op("tensor", dmm, inc=(mmx, 1))
            dmm_mm[p] = plan.val(mmx)

            def fev(v, p=p, mw=plan.val(mmx), rl=rl, of=out_free[p], psd=psd):
                v.wait_ge(plan.sems[mmx], mw)
                v.wait_ge(plan.sems[ph + "r"], rl)
                v.wait_ge(plan.sems[ph + "o"], of)
                v.tensor_add(outstg[p], psd[:, :], x1stg[p])
                return v.tensor_add(outstg[p], outstg[p], atstg[p])
            plan.op("vector", fev)
            ps45_free[p] = plan.val(vv)
            x1_free[p] = plan.val(vv)
            plan.op("sync", lambda sy, dt=dt, t0=t0, p=p, w=plan.val(vv): (
                sy.wait_ge(plan.sems[vv], w),
                sy.dma_start(out=dram["outT"].rearrange("(q p2) t -> q p2 t", p2=128)[dt, :, t0:t0 + 512],
                             in_=outstg[p]))[1], dma=ph + "o")
            out_free[p] = plan.val(ph + "o")

    for tg in range(9):
        if tg < 8:
            emit_gateup(tg)
        else:
            for tt in range(4):
                piece_push(7, tt)
                piece_post(7, tt)
        if tg > 0:
            emit_down(tg - 1)
    return ph + "o", plan.val(ph + "o")



def emit_kgather(plan, dram, wait):
    rs = plan.sem("kbc")
    sems = plan.sems

    def go(g):
        for s, v in wait:
            g.wait_ge(sems[s], v)
        g.collective_compute(
            "AllGather", mybir.AluOpType.bypass,
            replica_groups=[[0, 1, 2, 3, 4, 5, 6, 7]],
            ins=[dram["s_kslice"]], outs=[dram["s_kag"]],
        ).then_inc(rs, 1)
    plan.emit("gpsimd", go)
    plan.cnt["kbc"] += 1
    plan.dma_sems.add("kbc")
    return "kbc", 1


def barrier(plan):
    return [(s, c) for s, c in plan.cnt.items() if c > 0]



SCRATCH_SHAPES = {
    "s_xn": ([D, T], BF16), "s_xnT": ([T, D], BF16),
    "s_prm": ([4, 8192], F32), "s_kslice": ([L, HSL], BF16),
    "s_kag": ([8 * L, HSL], BF16),
    "s_uf": ([2, 2, NFP, D], BF16), "s_kf": ([2, NFP, D], BF16),
    "s_g": ([2, L, D], BF16), "s_glu": ([D, T], BF16),
    "s_attn0": ([D, T], BF16),
    "s_cwT": ([E, T], F32), "s_x1": ([D, T], F32),
    "s_xn1": ([D, T], BF16),
    "s_yatt": ([D, T], BF16), "s_attn1": ([D, T], BF16),
    "s_wgq": ([D, FFI], BF16), "s_wuq": ([D, FFI], BF16), "s_wdq": ([FFI, D], BF16),
    "s_xq": ([D, T], BF16),
    "outT": ([D, T], F32),
}


def _scr(names, outs):
    return {n: (SCRATCH_SHAPES[n][0], SCRATCH_SHAPES[n][1],
                "ExternalOutput" if n in outs else
                ("ExternalInput" if n.startswith("s_") and n in names and n in names else "Internal"))
            for n in names}


def _mk_scratch(ins, internal, outs):
    d = {}
    for n in ins:
        d[n] = (SCRATCH_SHAPES[n][0], SCRATCH_SHAPES[n][1], "ExternalInput")
    for n in internal:
        d[n] = (SCRATCH_SHAPES[n][0], SCRATCH_SHAPES[n][1], "Internal")
    for n in outs:
        d[n] = (SCRATCH_SHAPES[n][0], SCRATCH_SHAPES[n][1], "ExternalOutput")
    return d


def kernel(**inputs):
    in_maps = host_prepare(inputs)
    EI = "ExternalInput"

    def fused(plan, bufs, dram):
        emit_ln(plan, bufs, "ln1", dram["x0T"], dram["s_xn"], dstT=dram["s_xnT"],
                wait_st=plan.cst_wait)
        ln1_st = ("ln1st", plan.val("ln1st"))
        kst, kstv = emit_ssm_k(plan, bufs, "k", dram)
        emit_kgather(plan, dram, wait=[(kst, kstv)])
        b1 = [ln1_st, ("kbc", 1)]
        s_lns, v_lns = emit_l0attn(plan, bufs, dram, deps=b1)
        b2 = [(s_lns, v_lns)]
        emit_moe(plan, bufs, "mo", dram, deps=b2)
        b3 = [("most", plan.val("most"))]
        emit_ln(plan, bufs, "ln2", dram["s_x1"], dram["s_xn1"], wait_st=b3)
        b4 = [("ln2st", plan.val("ln2st"))]
        emit_attn(plan, bufs, "at", dram, deps=b4)
        b5 = [("ato", plan.val("ato")), ("ln2st", plan.val("ln2st"))]
        emit_fp4(plan, bufs, "f4", dram, deps=b5)

    io_kinds = {n: EI for n in input_specs() if not n.startswith("s_")}
    internal = [n for n in SCRATCH_SHAPES if n != "outT"]
    nc = build([fused], io_kinds,
               extra_scratch=_mk_scratch([], internal, ["outT"]))
    r = run_bass_kernel_spmd(nc, in_maps, core_ids=list(range(NCORES)))
    out = np.empty((16, 2048, 1024), dtype=np.float32)
    for c in range(NCORES):
        out[2 * c:2 * c + 2] = np.asarray(r.results[c]["outT"]).T.reshape(2, L, D)
    return out
